# version 06
# speedup vs baseline: 1.1592x; 1.1592x over previous
import contextlib
import numpy as np
import ml_dtypes
import concourse.bass as bass
import concourse.mybir as mybir
from concourse.alu_op_type import AluOpType as OP
from concourse.bass_utils import run_bass_kernel_spmd

F32 = mybir.dt.float32
BF16 = mybir.dt.bfloat16
U8 = mybir.dt.uint8
AF = mybir.ActivationFunctionType
AX = mybir.AxisListType
BFNP = ml_dtypes.bfloat16

NCORE = 8
D = 1024
NDC = D // 128
ENGS = ("t", "v", "s", "g", "y")


class Sem:
    __slots__ = ("h", "n")

    def __init__(self, h):
        self.h = h
        self.n = 0


class K:

    def __init__(self, T):
        self.nc = bass.Bass("TRN2", target_bir_lowering=False, debug=False)
        self.T = T
        self.stk = contextlib.ExitStack()
        self.sems = {}
        self.sb = {}
        self.dram = {}
        self.host = {}
        self.blk = None
        self.bars = None
        self.dma_sems = set()

    def sem(self, name):
        if name not in self.sems:
            h = self._sem_pool.pop()
            self.sems[name] = Sem(h)
        return self.sems[name]

    def start(self):
        self._sem_pool = [self.stk.enter_context(self.nc.semaphore(f"sp{i}"))
                          for i in range(99)][::-1]
        self.blk = self.stk.enter_context(self.nc.Block())

    def sbuf(self, name, shape, dt):
        ap = self.stk.enter_context(self.nc.sbuf_tensor(name, list(shape), dt))
        self.sb[name] = ap
        return ap

    def psum(self, name, shape, dt=F32):
        return self.stk.enter_context(self.nc.psum_tensor(name, list(shape), dt))

    def dram_in(self, name, shape, dt=F32):
        ap = self.nc.dram_tensor(name, list(shape), dt, kind="ExternalInput").ap()
        self.dram[name] = ap
        return ap

    def dram_out(self, name, shape, dt=F32):
        ap = self.nc.dram_tensor(name, list(shape), dt, kind="ExternalOutput").ap()
        self.dram[name] = ap
        return ap

    def dram_tmp(self, name, shape, dt=F32):
        ap = self.nc.dram_tensor(name, list(shape), dt, kind="Internal").ap()
        self.dram[name] = ap
        return ap

    def weight(self, name, arr):
        arr = np.ascontiguousarray(arr, dtype=np.float32)
        assert arr.ndim == 2 and arr.shape[0] <= 128, (name, arr.shape)
        self.host[name] = arr
        return self.dram_in(name, arr.shape)

    def t(self, fn):
        self.blk.tensor(fn)

    def v(self, fn):
        self.blk.vector(fn)

    def s(self, fn):
        self.blk.scalar(fn)

    def g(self, fn):
        self.blk.gpsimd(fn)

    def y(self, fn):
        self.blk.sync(fn)

    def on(self, eng, fn):
        getattr(self, eng)(fn)

    def inc(self, instr, sem, n=1):
        instr.then_inc(sem.h, n)
        sem.n += n
        return instr

    def dinc(self, instr, sem):
        self.dma_sems.add(sem)
        return self.inc(instr, sem, 16)

    def barrier(self, engines=ENGS):
        if self.bars is None:
            self.bars = {e: self.sem(f"bar_{e}") for e in ENGS}
        bs = self.sb["barsc"]
        dwaits = [(sm, sm.n) for sm in self.dma_sems if sm.n]
        for e in engines:
            be = self.bars[e]

            def _i(q, be=be, e=e):
                for sm, n in dwaits:
                    q.wait_ge(sm.h, n)
                if e == "v":
                    q.memset(bs[:, 0:1], 0.0)
                    self.inc(q.memset(bs[:, 1:2], 0.0), be)
                elif e == "s":
                    q.memzero(bs[:, 2:3])
                    self.inc(q.memzero(bs[:, 3:4]), be)
                elif e == "g":
                    q.memset(bs[:, 4:5], 0.0)
                    self.inc(q.memset(bs[:, 5:6], 0.0), be)
                else:
                    self.inc(q.nop(), be)
            self.on(e, _i)
        marks = {e: self.bars[e].n for e in engines}
        for e in engines:
            def _w(q, e=e):
                for f in engines:
                    if f != e:
                        q.wait_ge(self.bars[f].h, marks[f])
            self.on(e, _w)


def act_chain(k, sc, sem, ins1):
    k.inc(ins1, sem)
    sc.wait_ge(sem.h, sem.n)


def pbar(k, site=None):
    p = getattr(k, 'paranoid', False)
    if p and (p is True or site is None or site in p):
        k.barrier()


def mm_acc(te, ps, pairs):
    last = None
    n = len(pairs)
    for i, (l, r) in enumerate(pairs):
        last = te.matmul(ps, l, r, start=(i == 0), stop=(i == n - 1))
    return last



def wimg_lhsT(w_t):
    I, O = w_t.shape
    assert I % 128 == 0
    return np.ascontiguousarray(
        w_t.reshape(I // 128, 128, O).transpose(1, 0, 2).reshape(128, (I // 128) * O))


def bimg(b):
    O = b.shape[0]
    assert O % 128 == 0
    return np.ascontiguousarray(b.reshape(O // 128, 128).T)



def stream_mm(k, tag, wd, O, nci, src, TC, epi, og=2, kcg=8, pre=()):
    NTB = TC // 512
    assert O % (og * 128) == 0
    nog = O // (og * 128)
    nkg = (nci + kcg - 1) // kcg
    slabs = [k.sb["slab0"], k.sb["slab1"]]
    ps = k.ps_arena
    s_w = [k.sem("smm_w0"), k.sem("smm_w1")]
    s_t = k.sem("smm_t")
    s_e = k.sem("smm_e")
    t_mark = {}
    e_mark = {}
    ncols = og * 128
    wv = wd.rearrange("p (c o) -> p c o", c=nci)

    for g in range(nog):
        par = g % 2
        region = par * 2048
        for kg in range(nkg):
            L = g * nkg + kg
            slab = slabs[L % 2]
            nk = min(kcg, nci - kg * kcg)
            sv = slab[:, 0:kcg * ncols].rearrange("p (c o) -> p c o", o=ncols)

            sw = s_w[L % 2]

            def _g(gq, g=g, kg=kg, L=L, nk=nk, sv=sv, sw=sw):
                if L >= 2:
                    gq.wait_ge(s_t.h, t_mark[L - 2])
                elif pre:
                    for sm, mk in pre:
                        gq.wait_ge(sm.h, mk)
                k.dinc(gq.dma_start(
                    out=sv[:, 0:nk, :],
                    in_=wv[:, kg * kcg:kg * kcg + nk, g * ncols:(g + 1) * ncols]), sw)
            k.g(_g)
            wm = (sw, sw.n)

            def _t(te, g=g, kg=kg, nk=nk, wm=wm, region=region, slab=slab):
                te.wait_ge(wm[0].h, wm[1])
                if kg == 0:
                    for sm, mk in pre:
                        te.wait_ge(sm.h, mk)
                    em = e_mark.get(g - 2)
                    if em:
                        te.wait_ge(s_e.h, em)
                last = None
                for kk in range(nk):
                    kc = kg * kcg + kk
                    for oo in range(og):
                        for nb in range(NTB):
                            last = te.matmul(
                                ps[:, region + oo * TC + nb * 512:
                                   region + oo * TC + nb * 512 + 512],
                                slab[:, kk * ncols + oo * 128:kk * ncols + oo * 128 + 128],
                                src[:, kc * TC + nb * 512:kc * TC + nb * 512 + 512],
                                start=(kc == 0), stop=(kc == nci - 1))
                k.inc(last, s_t)
            k.t(_t)
            t_mark[L] = s_t.n

        tm = s_t.n
        for oo in range(og):
            epi(g * og + oo, ps[:, region + oo * TC:region + oo * TC + TC], tm, s_t, s_e)
        e_mark[g] = s_e.n
    return s_t, s_e



def emit_l3_ffn(k, xn_hbm, out_hbm, tag="l3f"):
    T = k.T
    TC = min(1024, T)
    nch = T // TC
    NF = 4096
    NFC = NF // 128
    NTB = TC // 512

    xnb = k.sb["l3f_xn"]
    hbuf = k.sb["l3f_h"]
    gbuf = k.sb["l3f_g"]
    oout = [k.sb["l3f_o0"], k.sb["l3f_o1"]]
    slabs = [k.sb["slab0"], k.sb["slab1"]]
    ps = k.ps_arena
    sc1 = k.sb["sb_L3_s1"]
    sc2 = k.sb["sb_L3_s2"]
    sch1 = k.sb["sb_L3_sh1"]
    sch2 = k.sb["sb_L3_sh2"]
    sc3 = k.sb["sb_L3_s3"]

    s_x = k.sem(f"{tag}x")
    s_o = [k.sem(f"{tag}o0"), k.sem(f"{tag}o1")]
    s_1w = [k.sem(f"{tag}1w0"), k.sem(f"{tag}1w1")]
    s_1t = k.sem(f"{tag}1t")
    s_1e = k.sem(f"{tag}1e")

    w1 = k.dram["L3_w1T"].rearrange("p (c o) -> p c o", c=NDC)
    w2 = k.dram["L3_w2T"].rearrange("p (c o) -> p c o", c=NDC)
    h1 = k.dram["L3_h1T"]
    h2 = k.dram["L3_h2T"]
    w3 = k.dram["L3_w3T"]

    xnv = xn_hbm.rearrange("p (c t) -> p c t", c=NDC)
    outv = out_hbm.rearrange("(c p) t -> p c t", p=128)

    prev_t = None
    for i in range(nch):
        t0 = i * TC
        om = [(so, so.n) for so in s_o if so.n]

        def _gx(gq, t0=t0, om=om):
            for so, n in om:
                gq.wait_ge(so.h, n)
            k.dinc(gq.dma_start(out=xnb.rearrange("p (c t) -> p c t", c=NDC)[:, :, 0:TC],
                                in_=xnv[:, :, t0:t0 + TC]), s_x)
        k.g(_gx)
        xm = s_x.n

        d_t_mark = {}
        d_e_mark = {}
        for o in range(NFC):
            par = o % 2
            region = par * 2048
            slab = slabs[par]
            sv = slab[:, 0:2048].rearrange("p (c o) -> p c o", o=256)

            sw1 = s_1w[o % 2]

            def _g1(gq, o=o, sv=sv, sw1=sw1):
                if o >= 2:
                    gq.wait_ge(s_1t.h, d_t_mark[o - 2])
                elif prev_t is not None:
                    gq.wait_ge(prev_t[0].h, prev_t[1])
                k.dinc(gq.dma_start(out=sv[:, :, 0:128],
                                    in_=w1[:, :, o * 128:o * 128 + 128]), sw1)
                k.dinc(gq.dma_start(out=sv[:, :, 128:256],
                                    in_=w2[:, :, o * 128:o * 128 + 128]), sw1)
            k.g(_g1)
            wm = (sw1, sw1.n)

            def _t1(te, o=o, slab=slab, wm=wm, region=region, xm=xm):
                te.wait_ge(wm[0].h, wm[1])
                te.wait_ge(s_x.h, xm)
                if o >= 2:
                    te.wait_ge(s_1e.h, d_e_mark[o - 2])
                last = None
                for w_i in range(2):
                    for nb in range(NTB):
                        last = mm_acc(te, ps[:, region + w_i * TC + nb * 512:
                                             region + w_i * TC + nb * 512 + 512],
                                      [(slab[:, c * 256 + w_i * 128:c * 256 + w_i * 128 + 128],
                                        xnb[:, c * TC + nb * 512:c * TC + nb * 512 + 512])
                                       for c in range(NDC)])
                k.inc(last, s_1t)
            k.t(_t1)
            d_t_mark[o] = s_1t.n
            tm = s_1t.n

            def _s1(sc, o=o, region=region, tm=tm):
                sc.wait_ge(s_1t.h, tm)
                k.inc(sc.activation(out=gbuf[:, o * TC:o * TC + TC],
                                    in_=ps[:, region:region + TC], func=AF.Silu,
                                    bias=0.0, scale=sc1[:, o:o + 1]), s_1e)
            k.s(_s1)
            sm = s_1e.n

            def _v1(v, o=o, region=region, sm=sm):
                v.wait_ge(s_1e.h, sm)
                k.inc(v.scalar_tensor_tensor(out=hbuf[:, o * TC:o * TC + TC],
                                             in0=ps[:, region + TC:region + 2 * TC],
                                             scalar=sc2[:, o:o + 1],
                                             in1=gbuf[:, o * TC:o * TC + TC],
                                             op0=OP.mult, op1=OP.mult), s_1e)
            k.v(_v1)
            d_e_mark[o] = s_1e.n

        def epi_2a(o, ps_ap, tm, s_t2, s_e2):
            def _s(sc, o=o, ps_ap=ps_ap, tm=tm):
                sc.wait_ge(s_t2.h, tm)
                k.inc(sc.activation(out=gbuf[:, o * TC:o * TC + TC], in_=ps_ap,
                                    func=AF.Gelu, bias=0.0, scale=sch1[:, o:o + 1]), s_e2)
            k.s(_s)
        t2a, e2a = stream_mm(k, f"{tag}2a", h1, NF, 32, hbuf, TC, epi_2a,
                             og=2, kcg=8, pre=((s_1t, s_1t.n), (s_1e, s_1e.n)))

        def epi_2b(o, ps_ap, tm, s_t2, s_e2):
            def _s(sc, o=o, ps_ap=ps_ap, tm=tm):
                sc.wait_ge(s_t2.h, tm)
                k.inc(sc.activation(out=hbuf[:, o * TC:o * TC + TC], in_=ps_ap,
                                    func=AF.Gelu, bias=0.0, scale=sch2[:, o:o + 1]), s_e2)
            k.s(_s)
        t2b, e2b = stream_mm(k, f"{tag}2b", h2, NF, 32, gbuf, TC, epi_2b,
                             og=2, kcg=8, pre=((t2a, t2a.n), (e2a, e2a.n)))

        o_marks = {}

        def epi_3(o, ps_ap, tm, s_t2, s_e2, t0=t0):
            buf = oout[o % 2]
            so = s_o[o % 2]

            def _v(v, o=o, ps_ap=ps_ap, tm=tm, buf=buf, so=so):
                v.wait_ge(s_t2.h, tm)
                if o >= 2:
                    v.wait_ge(so.h, o_marks[o - 2])
                k.inc(v.tensor_scalar(out=buf[:], in0=ps_ap, scalar1=sc3[:, o:o + 1],
                                      scalar2=None, op0=OP.mult), s_e2)
            k.v(_v)
            vm = s_e2.n

            def _g(gq, o=o, buf=buf, vm=vm, t0=t0, so=so):
                gq.wait_ge(s_e2.h, vm)
                k.dinc(gq.dma_start(out=outv[:, o, t0:t0 + TC], in_=buf[:]), so)
            k.g(_g)
            o_marks[o] = so.n
        t3, e3 = stream_mm(k, f"{tag}3", w3, D, 32, hbuf, TC, epi_3,
                           og=2, kcg=16, pre=((t2b, t2b.n), (e2b, e2b.n)))
        prev_t = (t3, t3.n)
    return s_o


def l3_ffn_host(k, inputs):
    k.weight("L3_w1T", wimg_lhsT(np.asarray(inputs["L3_ffn_w1_w"]).T))
    k.weight("L3_w2T", wimg_lhsT(np.asarray(inputs["L3_ffn_w2_w"]).T))
    k.weight("L3_h1T", wimg_lhsT(np.asarray(inputs["L3_ffn_h1_w"]).T))
    k.weight("L3_h2T", wimg_lhsT(np.asarray(inputs["L3_ffn_h2_w"]).T))
    k.weight("L3_w3T", wimg_lhsT(np.asarray(inputs["L3_ffn_w3_w"]).T))
    k.weight("L3_s1", bimg(np.asarray(inputs["L3_ffn_w1_s"])))
    k.weight("L3_s2", bimg(np.asarray(inputs["L3_ffn_w2_s"])))
    k.weight("L3_sh1", bimg(np.asarray(inputs["L3_ffn_h1_s"])))
    k.weight("L3_sh2", bimg(np.asarray(inputs["L3_ffn_h2_s"])))
    k.weight("L3_s3", bimg(np.asarray(inputs["L3_ffn_w3_s"])))


def alloc_l3f_bufs(k):
    TC = min(1024, k.T)
    k.sb["l3f_xn"] = bfv(k, "ARC", 0, NDC * TC)
    k.sb["l3f_h"] = bfv(k, "ARA", 0, 32 * TC)
    k.sb["l3f_g"] = bfv(k, "ARB", 0, 32 * TC)
    k.sb["l3f_o0"] = f32v(k, "ARD", 0, TC)
    k.sb["l3f_o1"] = f32v(k, "ARD", TC, 2 * TC)


FCONSTS = [1e-5, 1e-6, 1e-12, 1e-24, 1e-30, 1e-38, 0.5, 1.0, -20.0,
           1.5707963267948966, -13.815510557964274]


def alloc_common(k):
    alloc_arenas(k)
    k.sbuf("barsc", [128, 8], F32)
    cp = k.sbuf("cpool", [128, len(FCONSTS)], F32)
    for i, v in enumerate(FCONSTS):
        k.nc.const_aps.register(F32, v, cp[:, i:i + 1])
    k.sbuf("ones_col_f32", [128, 1], F32)
    k.sbuf("ones_row_f32", [1, 128], F32)
    k.sbuf("slab0", [128, 4096], BF16)
    k.sbuf("slab1", [128, 4096], BF16)
    k.ps_arena = k.psum("ps_arena", [128, 4096], F32)


def emit_init_consts(k):
    s_c = k.sem("cinit")

    def _v(v):
        v.memset(k.sb["ones_col_f32"][:], 1.0)
        for i, val in enumerate(FCONSTS):
            v.memset(k.sb["cpool"][:, i:i + 1], float(val))
        k.inc(v.memset(k.sb["ones_row_f32"][:], 1.0), s_c)
    k.v(_v)
    return s_c


def alloc_w_sb(k, names_dt):
    for name, dt in names_dt:
        arr = k.host[name]
        k.sbuf("sb_" + name, list(arr.shape), dt)


def emit_load_w(k, names):
    s_w = k.sem("wload")

    def _g(gq):
        for name in names:
            k.dinc(gq.dma_start(out=k.sb["sb_" + name][:], in_=k.dram[name][:, :]), s_w)
    k.g(_g)
    return s_w


def finish_outputs(k, sems):
    flat = []
    for s in sems:
        flat.extend(s if isinstance(s, (list, tuple)) else [s])

    def _g(gq):
        for s in flat:
            gq.wait_ge(s.h, s.n)
    k.g(_g)



RT_ISQ = 0.08838834764831845


def emit_l3_attn(k, xn_hbm, out_hbm, tag="l3a"):
    T = k.T
    N = 2048
    nrows = T // N
    NW = N // 64

    xnr = k.sb["l3a_xn"]
    qf = k.sb["l3a_q"]
    kf = k.sb["l3a_k"]
    vtm = k.sb["l3a_v"]
    ofm = k.sb["l3a_xn"]
    attw = k.sb["l3a_aw"]
    attT = k.sb["l3a_at"]
    mx = k.sb["l3a_mx"]
    sums = k.sb["l3a_sm"]
    vstage = k.sb["l3a_vs"]
    ostage = [k.sb["l3a_os0"], k.sb["l3a_os0"]]
    scal = k.sb["l3a_scal"]
    idb = k.sb["sb_ident_bf"]
    onesb = k.sb["sb_ones_row_bf"]
    slabs = [k.sb["slab0"], k.sb["slab1"]]
    ps = k.ps_arena
    psb = ps.bitcast(BF16)

    s_x = k.sem(f"{tag}x")
    s_w = [k.sem(f"{tag}w0"), k.sem(f"{tag}w1")]
    s_wv = k.sem(f"{tag}wv")
    s_vb = k.sem(f"{tag}vb")
    s_t = k.sem("C_t")
    s_e = k.sem("C_e")
    s_o = k.sem(f"{tag}o")

    wq = k.dram["L3a_wqT"].rearrange("p (c o) -> p c o", c=NDC)
    wk = k.dram["L3a_wkT"].rearrange("p (c o) -> p c o", c=NDC)
    wv = k.dram["L3a_wvT"].rearrange("p (c o) -> p c o", c=NDC)
    wo = k.dram["L3a_woT"].rearrange("p (c o) -> p c o", c=NDC)
    bq = k.sb["sb_L3a_bq"]
    bk = k.sb["sb_L3a_bk"]
    bo = k.sb["sb_L3a_bo"]
    bvb = k.sb["l3a_bvb"]
    xnv = xn_hbm.rearrange("p (c t) -> p c t", c=NDC)
    outv = out_hbm.rearrange("(c p) t -> p c t", p=128)

    def _g0(gq):
        k.dinc(gq.dma_start(out=bvb[:], in_=k.dram["L3a_bv_row"][:, :]), s_wv)
    k.g(_g0)

    def _v0(v):
        v.wait_ge(s_wv.h, s_wv.n)
        v.tensor_scalar(out=scal[:, 0:8], in0=bq[:], scalar1=RT_ISQ, scalar2=None, op0=OP.mult)
        k.inc(v.tensor_scalar(out=scal[:, 8:16], in0=bo[:], scalar1=2.0, scalar2=None, op0=OP.mult), s_e)
    k.v(_v0)

    def _gwv(gq):
        k.dinc(gq.dma_start(out=k.sb["slab_wv"][:], in_=k.dram["L3a_wvT"][:, :]), s_wv)
    k.g(_gwv)
    wvm = s_wv.n

    for r in range(nrows):
        R = r * N
        tprev = s_t.n

        def _gx(gq, R=R, tprev=tprev):
            if tprev:
                gq.wait_ge(s_t.h, tprev)
            k.dinc(gq.dma_start(out=xnr.rearrange("p (c t) -> p c t", c=NDC)[:, :, 0:N],
                                in_=xnv[:, :, R:R + N]), s_x)
        k.g(_gx)
        xm = s_x.n

        for mi, (wd, dst, bias_ap, scl) in enumerate(
                [(wq, qf, scal[:, 0:8], RT_ISQ), (wk, kf, bk, 1.0)]):
            for o in range(NDC):
                par = (mi * NDC + o) % 2
                region = par * 2048
                slab = slabs[par]
                em = s_e.n

                sw = s_w[(mi * NDC + o) % 2]

                def _gw(gq, wd=wd, o=o, slab=slab, sw=sw):
                    gq.wait_ge(s_t.h, s_t.n)
                    k.dinc(gq.dma_start(
                        out=slab[:, 0:1024].rearrange("p (c o) -> p c o", o=128),
                        in_=wd[:, :, o * 128:o * 128 + 128]), sw)
                k.g(_gw)
                wm = (sw, sw.n)

                def _t1(te, o=o, slab=slab, wm=wm, region=region, xm=xm, em=em):
                    te.wait_ge(wm[0].h, wm[1])
                    te.wait_ge(s_x.h, xm)
                    if em:
                        te.wait_ge(s_e.h, em)
                    for nb in range(4):
                        last = mm_acc(te, ps[:, region + nb * 512:region + nb * 512 + 512],
                                      [(slab[:, c * 128:c * 128 + 128],
                                        xnr[:, c * N + nb * 512:c * N + nb * 512 + 512])
                                       for c in range(NDC)])
                    k.inc(last, s_t)
                k.t(_t1)
                tm = s_t.n

                def _s1(sc, o=o, region=region, tm=tm, dst=dst, bias_ap=bias_ap, scl=scl):
                    sc.wait_ge(s_t.h, tm)
                    k.inc(sc.activation(out=dst[:, o * N:o * N + N],
                                        in_=ps[:, region:region + 2048], func=AF.Identity,
                                        bias=bias_ap[:, o:o + 1], scale=scl), s_e)
                k.s(_s1)

        for m in range(16):
            par = m % 2
            region = par * 2048
            em = s_e.n
            o0 = 32 + 128 * m

            def _t2(te, m=m, region=region, em=em, o0=o0, xm=xm):
                te.wait_ge(s_x.h, xm)
                te.wait_ge(s_wv.h, wvm)
                te.wait_ge(s_e.h, em)
                if m < 15:
                    for nb in range(2):
                        pairs = [(xnr[:, c * N + o0:c * N + o0 + 128],
                                  k.sb["slab_wv"][:, c * 1024 + nb * 512:c * 1024 + nb * 512 + 512])
                                 for c in range(NDC)]
                        pairs.append((k.sb["sb_ones_row_bf"][0:1, 0:128],
                                      bvb[0:1, nb * 512:nb * 512 + 512]))
                        last = mm_acc(te, ps[:, region + nb * 512:region + nb * 512 + 512], pairs)
                else:
                    for nb in range(2):
                        pairs = [(xnr[:, c * N + (N - 96):c * N + N],
                                  k.sb["slab_wv"][:, c * 1024 + nb * 512:c * 1024 + nb * 512 + 512])
                                 for c in range(NDC)]
                        pairs.append((k.sb["sb_ones_row_bf"][0:1, 0:96],
                                      bvb[0:1, nb * 512:nb * 512 + 512]))
                        last = mm_acc(te, ps[0:96, region + nb * 512:region + nb * 512 + 512], pairs)
                    region_b = (1 - (m % 2)) * 2048
                    for nb in range(2):
                        pairs = [(xnr[:, c * N:c * N + 32],
                                  k.sb["slab_wv"][:, c * 1024 + nb * 512:c * 1024 + nb * 512 + 512])
                                 for c in range(NDC)]
                        pairs.append((k.sb["sb_ones_row_bf"][0:1, 0:32],
                                      bvb[0:1, nb * 512:nb * 512 + 512]))
                        last = mm_acc(te, ps[0:32, region_b + nb * 512:region_b + nb * 512 + 512], pairs)
                k.inc(last, s_t)
            k.t(_t2)
            tm = s_t.n

            if m < 15:
                def _s2(sc, m=m, region=region, tm=tm):
                    sc.wait_ge(s_t.h, tm)
                    k.inc(sc.activation(out=vtm[:, m * 1024:(m + 1) * 1024],
                                        in_=ps[:, region:region + 1024], func=AF.Copy), s_e)
                k.s(_s2)
            else:
                def _s2(sc, region=region, tm=tm):
                    sc.wait_ge(s_t.h, tm)
                    sc.activation(out=vtm[0:96, 15 * 1024:16 * 1024],
                                  in_=ps[0:96, region:region + 1024], func=AF.Copy)
                    k.inc(sc.activation(out=vstage[0:32, :],
                                        in_=ps[0:32, (1 - 15 % 2) * 2048:(1 - 15 % 2) * 2048 + 1024],
                                        func=AF.Copy), s_e)
                k.s(_s2)
                sm = s_e.n

                def _gv(gq, sm=sm):
                    gq.wait_ge(s_e.h, sm)
                    k.dinc(gq.dma_start(out=vtm[96:128, 15 * 1024:16 * 1024],
                                        in_=vstage[0:32, :]), s_vb)
                k.g(_gv)


        vm_all = (s_vb, s_vb.n)
        for wp in range(NW // 2):
            em = s_e.n

            def _t3(te, wp=wp, em=em):
                te.wait_ge(s_e.h, em)
                te.wait_ge(vm_all[0].h, vm_all[1])
                last = None
                for wi in range(2):
                    w = 2 * wp + wi
                    qs = 32 + 64 * w
                    for h in range(8):
                        pslice = ps[0:64, wi * 2048 + h * 64:wi * 2048 + h * 64 + 64]
                        if w < 31:
                            last = te.matmul(pslice, qf[:, h * N + qs:h * N + qs + 64],
                                             kf[:, h * N + qs:h * N + qs + 64],
                                             start=True, stop=True)
                        else:
                            for (qa, qn, prow) in [(2016, 32, 0), (0, 32, 32)]:
                                for (ka, kn, pcol) in [(2016, 32, 0), (0, 32, 32)]:
                                    last = te.matmul(
                                        ps[prow:prow + qn,
                                           wi * 2048 + h * 64 + pcol:wi * 2048 + h * 64 + pcol + kn],
                                        qf[:, h * N + qa:h * N + qa + qn],
                                        kf[:, h * N + ka:h * N + ka + kn],
                                        start=True, stop=True)
                k.inc(last, s_t)
            k.t(_t3)
            tm3 = s_t.n

            def _v3(v, tm3=tm3):
                v.wait_ge(s_t.h, tm3)
                for wi in range(2):
                    ins = v.reduce_max(out=mx[0:64, wi * 8:wi * 8 + 8],
                                       in_=ps[0:64, wi * 2048:wi * 2048 + 512]
                                       .rearrange("p (h x) -> p h x", h=8),
                                       axis=AX.X, negate=True)
                k.inc(ins, s_e)
            k.v(_v3)
            vm3 = s_e.n

            def _s3(sc, vm3=vm3):
                sc.wait_ge(s_e.h, vm3)
                for wi in range(2):
                    for h in range(8):
                        sc.activation(out=attw[0:64, h * 128 + wi * 64:h * 128 + wi * 64 + 64],
                                      in_=ps[0:64, wi * 2048 + h * 64:wi * 2048 + h * 64 + 64],
                                      func=AF.Exp, bias=mx[0:64, wi * 8 + h:wi * 8 + h + 1],
                                      scale=1.0,
                                      accum_out=sums[0:64, wi * 8 + h:wi * 8 + h + 1])
                act_chain(k, sc, s_e, sc.activation(out=sums[0:64, :],
                                                    in_=sums[0:64, :], func=AF.Ln))
                k.inc(sc.activation(out=sums[0:64, :], in_=sums[0:64, :], func=AF.Exp,
                                    bias=0.0, scale=-1.0), s_e)
            k.s(_s3)
            sm3 = s_e.n

            def _v4(v, sm3=sm3):
                v.wait_ge(s_e.h, sm3)
                for wi in range(2):
                    for h in range(8):
                        ins = v.tensor_scalar(
                            out=attw[0:64, h * 128 + wi * 64:h * 128 + wi * 64 + 64],
                            in0=attw[0:64, h * 128 + wi * 64:h * 128 + wi * 64 + 64],
                            scalar1=sums[0:64, wi * 8 + h:wi * 8 + h + 1],
                            scalar2=None, op0=OP.mult)
                k.inc(ins, s_e)
            k.v(_v4)
            vm4 = s_e.n

            def _t4(te, vm4=vm4):
                te.wait_ge(s_e.h, vm4)
                for h in range(8):
                    last = te.transpose(psb[:, 7168 + h * 64:7168 + h * 64 + 64],
                                        attw[0:64, h * 128:h * 128 + 128],
                                        idb[0:64, 0:64])
                k.inc(last, s_t)
            k.t(_t4)
            tm4 = s_t.n

            def _s4(sc, tm4=tm4):
                sc.wait_ge(s_t.h, tm4)
                k.inc(sc.activation(out=attT[:], in_=psb[:, 7168:7168 + 512],
                                    func=AF.Copy), s_e)
            k.s(_s4)
            sm4 = s_e.n

            def _t5(te, wp=wp, sm4=sm4):
                te.wait_ge(s_e.h, sm4)
                last = None
                for wi in range(2):
                    for h in range(8):
                        last = te.matmul(
                            ps[:, 2560 + wi * 512 + h * 64:2560 + wi * 512 + h * 64 + 64],
                            vtm[wi * 64:wi * 64 + 64, wp * 1024 + h * 128:wp * 1024 + h * 128 + 128],
                            attT[wi * 64:wi * 64 + 64, h * 64:h * 64 + 64],
                            start=True, stop=True)
                k.inc(last, s_t)
            k.t(_t5)
            tm5 = s_t.n

            def _s5(sc, wp=wp, tm5=tm5):
                sc.wait_ge(s_t.h, tm5)
                for wi in range(2):
                    w = 2 * wp + wi
                    ins = sc.activation(
                        out=ofm.rearrange("p (h t) -> p h t", h=8)[:, :, w * 64:w * 64 + 64],
                        in_=ps[:, 2560 + wi * 512:2560 + wi * 512 + 512]
                        .rearrange("p (h x) -> p h x", h=8),
                        func=AF.Copy)
                k.inc(ins, s_e)
            k.s(_s5)

        for o in range(NDC):
            par = o % 2
            region = par * 2048
            slab = slabs[par]
            em = s_e.n
            om_prev = s_o.n

            sw4 = s_w[o % 2]

            def _gw4(gq, o=o, slab=slab, sw4=sw4):
                gq.wait_ge(s_t.h, s_t.n)
                k.dinc(gq.dma_start(
                    out=slab[:, 0:1024].rearrange("p (c o) -> p c o", o=128),
                    in_=wo[:, :, o * 128:o * 128 + 128]), sw4)
            k.g(_gw4)
            wm = (sw4, sw4.n)

            def _t6(te, o=o, slab=slab, wm=wm, region=region, em=em):
                te.wait_ge(wm[0].h, wm[1])
                te.wait_ge(s_e.h, em)
                for nb in range(4):
                    last = mm_acc(te, ps[:, region + nb * 512:region + nb * 512 + 512],
                                  [(slab[:, c * 128:c * 128 + 128],
                                    ofm[:, c * N + nb * 512:c * N + nb * 512 + 512])
                                   for c in range(NDC)])
                k.inc(last, s_t)
            k.t(_t6)
            tm6 = s_t.n
            buf = ostage[o % 2]

            def _s6(sc, o=o, region=region, tm6=tm6, buf=buf, om_prev=om_prev):
                sc.wait_ge(s_t.h, tm6)
                if om_prev:
                    sc.wait_ge(s_o.h, om_prev)
                k.inc(sc.activation(out=buf[:], in_=ps[:, region:region + 2048],
                                    func=AF.Identity, bias=scal[:, 8 + o:8 + o + 1],
                                    scale=2.0), s_e)
            k.s(_s6)
            sm6 = s_e.n

            def _g6(gq, o=o, buf=buf, sm6=sm6, R=R):
                gq.wait_ge(s_e.h, sm6)
                k.dinc(gq.dma_start(out=outv[:, o, R + 32:R + 2048], in_=buf[:, 0:2016]), s_o)
                k.dinc(gq.dma_start(out=outv[:, o, R:R + 32], in_=buf[:, 2016:2048]), s_o)
            k.g(_g6)
    return s_o


def l3_attn_host(k, inputs):
    wqkv = np.asarray(inputs["L3_attn_in_proj_w"])
    b = np.asarray(inputs["L3_attn_in_proj_b"])
    k.weight("L3a_wqT", wimg_lhsT(wqkv[0:1024].T))
    k.weight("L3a_wkT", wimg_lhsT(wqkv[1024:2048].T))
    k.weight("L3a_wvT", wimg_lhsT(wqkv[2048:3072].T))
    k.weight("L3a_woT", wimg_lhsT(np.asarray(inputs["L3_attn_out_w"]).T))
    k.weight("L3a_bq", bimg(b[0:1024]))
    k.weight("L3a_bk", bimg(b[1024:2048]))
    k.weight("L3a_bv_row", b[2048:3072][None, :])
    k.weight("L3a_bo", bimg(np.asarray(inputs["L3_attn_out_b"])))


def alloc_l3a_bufs(k):
    N = 2048
    k.sb["l3a_xn"] = bfv(k, "ARC", 0, NDC * N)
    k.sb["l3a_q"] = bfv(k, "ARA", 0, NDC * N)
    k.sb["l3a_k"] = bfv(k, "ARA", NDC * N, 2 * NDC * N)
    k.sb["l3a_v"] = bfv(k, "ARB", 0, 16 * 1024)
    k.sb["slab_wv"] = bfv(k, "ARB", 16 * 1024, 24 * 1024)
    k.sb["l3a_aw"] = bfv(k, "ARD", 0, 1024)
    k.sb["l3a_at"] = bfv(k, "ARD", 1024, 1536)
    k.sb["l3a_vs"] = k.sb["ARD"][0:32, 1536:2560]
    k.sb["l3a_bvb"] = k.sb["ARD"][0:1, 2560:3584]
    k.sb["l3a_os0"] = f32v(k, "ARD", 2048, 4096)
    k.sb["l3a_mx"] = f32v(k, "ARD", 1792, 1808)
    k.sb["l3a_sm"] = f32v(k, "ARD", 1808, 1824)
    k.sb["l3a_scal"] = f32v(k, "ARD", 1824, 1848)



def alloc_arenas(k):
    k.sbuf("ARA", [128, 32768], BF16)
    k.sbuf("ARB", [128, 32768], BF16)
    k.sbuf("ARC", [128, 16384], BF16)
    k.sbuf("ARD", [128, 8192], BF16)
    k.sbuf("ARE", [128, 4096], BF16)


def bfv(k, name, c0, c1):
    return k.sb[name][:, c0:c1]


def f32v(k, name, c0, c1):
    return k.sb[name].bitcast(F32)[:, c0:c1]



def emit_ln(k, x_hbm, xn_hbm, stats_hbm, tag, eps=1e-5):
    T = k.T
    TC = 512
    nch = T // TC
    ones_col = k.sb["ones_col_f32"]
    ones_row = k.sb["ones_row_f32"]
    xb = [f32v(k, "ARA", 0, 4096), f32v(k, "ARA", 4096, 8192)]
    sq = f32v(k, "ARA", 8192, 12288)
    stg = [bfv(k, "ARD", 0, 4096), bfv(k, "ARD", 4096, 8192)]
    er = f32v(k, "ARA", 12288, 14336)
    ps = k.ps_arena
    s_ld = [k.sem("ln_l0"), k.sem("ln_l1")]
    s_t = k.sem("ln_t")
    s_s = k.sem("ln_s")
    s_v = k.sem("ln_v")
    s_g = [k.sem("ln_g0"), k.sem("ln_g1")]

    x_v = x_hbm.rearrange("(c p) t -> p c t", p=128)
    xnv = xn_hbm.rearrange("p (c t) -> p c t", c=NDC)
    ld_mark = [None] * nch
    v_done = [0] * (nch + 2)
    g_done = [None] * (nch + 2)

    mrow = er[0:1, 0:TC]
    erow = er[0:1, 512:512 + TC]
    rrow = er[0:1, 1024:1024 + TC]
    brow = er[0:1, 1536:1536 + TC]
    qrow = f32v(k, "ARA", 15872, 16384)[0:1, 0:TC]

    def emit_load(i):
        buf = xb[i % 2]
        vm = v_done[i - 2] if i >= 2 else 0
        sl = s_ld[i % 2]

        def _y(yq, i=i, buf=buf, vm=vm, sl=sl):
            if vm:
                yq.wait_ge(s_v.h, vm)
            k.dinc(yq.dma_start(out=buf, in_=x_v[:, :, i * TC:(i + 1) * TC]), sl)
        k.y(_y)
        ld_mark[i] = (sl, sl.n)

    def emit_compute(i):
        buf = xb[i % 2]
        stgb = stg[i % 2]
        t0 = i * TC
        b0 = (i % 2) * 2048
        pA = ps[0:1, b0:b0 + TC]
        pB = ps[0:1, b0 + 512:b0 + 512 + TC]
        pC = ps[:, b0 + 1024:b0 + 1024 + TC]
        pD = ps[:, b0 + 1536:b0 + 1536 + TC]

        tprev = s_t.n

        def _s1(sc):
            sc.wait_ge(ld_mark[i][0].h, ld_mark[i][1])
            if tprev:
                sc.wait_ge(s_t.h, tprev)
            k.inc(sc.activation(out=sq, in_=buf, func=AF.Square), s_s)
        k.s(_s1)
        sm = s_s.n

        def _t1(te):
            te.wait_ge(ld_mark[i][0].h, ld_mark[i][1])
            if i >= 2:
                te.wait_ge(s_v.h, v_done[i - 2])
            mm_acc(te, pA, [(ones_col[:], buf[:, c * TC:(c + 1) * TC]) for c in range(NDC)])
            te.wait_ge(s_s.h, sm)
            last = mm_acc(te, pB, [(ones_col[:], sq[:, c * TC:(c + 1) * TC]) for c in range(NDC)])
            k.inc(last, s_t)
        k.t(_t1)
        tm = s_t.n

        sprev = s_s.n

        def _v1(v):
            v.wait_ge(s_t.h, tm)
            if sprev:
                v.wait_ge(s_s.h, sprev)
            v.tensor_scalar(out=mrow, in0=pA, scalar1=1.0 / D, scalar2=None, op0=OP.mult)
            k.inc(v.tensor_scalar(out=erow, in0=pB, scalar1=1.0 / D, scalar2=None, op0=OP.mult), s_v)
            v.wait_ge(s_v.h, s_v.n)
            k.inc(v.tensor_tensor(out=qrow, in0=mrow, in1=mrow, op=OP.mult), s_v)
            v.wait_ge(s_v.h, s_v.n)
            k.inc(v.tensor_tensor(out=erow, in0=erow, in1=qrow, op=OP.subtract), s_v)
        k.v(_v1)
        vm1 = s_v.n

        gp_sl = s_g[(i + 1) % 2]
        gprev = (gp_sl, gp_sl.n) if gp_sl.n else None

        def _s2(sc):
            sc.wait_ge(s_v.h, vm1)
            if gprev:
                sc.wait_ge(gprev[0].h, gprev[1])
            act_chain(k, sc, s_s, sc.activation(out=erow, in_=erow, func=AF.Ln,
                                                bias=float(eps), scale=1.0))
            k.inc(sc.activation(out=rrow, in_=erow, func=AF.Exp, bias=0.0, scale=-0.5), s_s)
        k.s(_s2)
        sm2 = s_s.n

        def _v2(v):
            v.wait_ge(s_s.h, sm2)
            if gprev:
                v.wait_ge(gprev[0].h, gprev[1])
            k.inc(v.scalar_tensor_tensor(out=brow, in0=mrow, scalar=-1.0, in1=rrow,
                                         op0=OP.mult, op1=OP.mult), s_v)
        k.v(_v2)
        vm2 = s_v.n

        def _t2(te):
            te.wait_ge(s_v.h, vm2)
            te.matmul(pC, ones_row[:], rrow, start=True, stop=True)
            k.inc(te.matmul(pD, ones_row[:], brow, start=True, stop=True), s_t)
        k.t(_t2)
        tm2 = s_t.n

        tmpx = [f32v(k, "ARA", 14848, 15360), f32v(k, "ARA", 15360, 15872)]

        def _v3(v):
            v.wait_ge(s_t.h, tm2)
            gm = g_done[i - 2] if i >= 2 else None
            if gm:
                v.wait_ge(gm[0].h, gm[1])
            for c in range(NDC):
                k.inc(v.tensor_tensor(out=tmpx[c % 2], in0=buf[:, c * TC:(c + 1) * TC],
                                      in1=pC, op=OP.mult), s_v)
                v.wait_ge(s_v.h, s_v.n)
                ins = v.tensor_tensor(out=stgb[:, c * TC:(c + 1) * TC],
                                      in0=tmpx[c % 2], in1=pD, op=OP.add)
            k.inc(ins, s_v)
        k.v(_v3)
        v_done[i] = s_v.n
        vm3 = s_v.n

        sg = s_g[i % 2]

        def _g1(g, sg=sg):
            g.wait_ge(s_v.h, vm3)
            k.dinc(g.dma_start(out=xnv[:, :, t0:t0 + TC],
                               in_=stgb.rearrange("p (c t) -> p c t", c=NDC)), sg)
            k.dinc(g.dma_start(out=stats_hbm[0:1, t0:t0 + TC], in_=rrow), sg)
            k.dinc(g.dma_start(out=stats_hbm[1:2, t0:t0 + TC], in_=brow), sg)
        k.g(_g1)
        g_done[i] = (sg, sg.n)

    emit_load(0)
    if nch > 1:
        emit_load(1)
    for i in range(nch):
        emit_compute(i)
        if i + 2 < nch:
            emit_load(i + 2)
    return s_g[0]



def emit_add3(k, x_hbm, a_hbm, b_hbm, out_hbm, tag):
    T = k.T
    TC = 512
    nch = T // TC
    bx = [f32v(k, "ARA", 0, 4096), f32v(k, "ARA", 4096, 8192)]
    ba = [f32v(k, "ARA", 8192, 12288), f32v(k, "ARA", 12288, 16384)]
    bb = [f32v(k, "ARB", 0, 4096), f32v(k, "ARB", 4096, 8192)]
    bo = [f32v(k, "ARB", 8192, 12288), f32v(k, "ARB", 12288, 16384)]
    s_l = [k.sem("a3_l0"), k.sem("a3_l1")]
    s_v = k.sem("a3_v")
    s_o = [k.sem("a3_o0"), k.sem("a3_o1")]
    vx = x_hbm.rearrange("(c p) t -> p c t", p=128)
    va = a_hbm.rearrange("(c p) t -> p c t", p=128)
    vb = b_hbm.rearrange("(c p) t -> p c t", p=128)
    vo = out_hbm.rearrange("(c p) t -> p c t", p=128)
    vdone = [0] * (nch + 2)
    odone = [None] * (nch + 2)

    def emit_load(i):
        vm = vdone[i - 2] if i >= 2 else 0
        slm = s_l[i % 2]

        def _y(yq, i=i, vm=vm, slm=slm):
            if vm:
                yq.wait_ge(s_v.h, vm)
            sl = slice(i * TC, (i + 1) * TC)
            k.dinc(yq.dma_start(out=bx[i % 2], in_=vx[:, :, sl]), slm)
            k.dinc(yq.dma_start(out=ba[i % 2], in_=va[:, :, sl]), slm)
            k.dinc(yq.dma_start(out=bb[i % 2], in_=vb[:, :, sl]), slm)
        k.y(_y)
        return (slm, slm.n)

    lmark = [None] * nch
    lmark[0] = emit_load(0)
    if nch > 1:
        lmark[1] = emit_load(1)
    for i in range(nch):
        om = odone[i - 2] if i >= 2 else None

        def _v(v, i=i, om=om):
            v.wait_ge(lmark[i][0].h, lmark[i][1])
            if om:
                v.wait_ge(om[0].h, om[1])
            k.inc(v.tensor_tensor(out=ba[i % 2], in0=bx[i % 2], in1=ba[i % 2], op=OP.add), s_v)
        k.v(_v)
        vm0 = s_v.n

        def _p(g, i=i, vm0=vm0, om=om):
            g.wait_ge(s_v.h, vm0)
            if om:
                g.wait_ge(om[0].h, om[1])
            k.inc(g.tensor_tensor(out=bo[i % 2], in0=ba[i % 2], in1=bb[i % 2], op=OP.add), s_v)
        k.g(_p)
        vdone[i] = s_v.n
        vm = s_v.n

        so = s_o[i % 2]

        def _g(g, i=i, vm=vm, so=so):
            g.wait_ge(s_v.h, vm)
            k.dinc(g.dma_start(out=vo[:, :, i * TC:(i + 1) * TC], in_=bo[i % 2]), so)
        k.g(_g)
        odone[i] = (so, so.n)
        if i + 2 < nch:
            lmark[i + 2] = emit_load(i + 2)
    return s_o[0]



def emit_layer3(k, x_hbm, xout_hbm):
    emit_ln(k, x_hbm, k.dram["xn_s"], k.dram["stats_s"], "L3")
    k.barrier()
    emit_l3_attn(k, k.dram["xn_s"], k.dram["att_s"])
    k.barrier()
    emit_l3_ffn(k, k.dram["xn_s"], k.dram["ffn_s"])
    k.barrier()
    s_o = emit_add3(k, x_hbm, k.dram["att_s"], k.dram["ffn_s"], xout_hbm, "L3")
    return s_o


def alloc_scratch_dram(k):
    T = k.T
    k.dram_tmp("mask_s", [1, T], F32)
    k.dram_tmp("srow_s", [1, T], F32)
    k.dram_tmp("mxrow_s", [1, 1024], F32)
    k.dram_tmp("x1_s", [D, T], F32)
    k.dram_tmp("f1_s", [1, T], F32)
    k.dram_tmp("comb_s", [8, T], F32)
    k.dram_tmp("xn_s", [128, NDC * T], BF16)
    k.dram_tmp("att_s", [D, T], F32)
    k.dram_tmp("ffn_s", [D, T], F32)
    k.dram_tmp("stats_s", [2, T], F32)


L3_SMALL_W = [("L3a_bq", F32), ("L3a_bk", F32), ("L3a_bo", F32),
              ("L3_s1", F32), ("L3_s2", F32), ("L3_sh1", F32),
              ("L3_sh2", F32), ("L3_s3", F32),
              ("ident_bf", BF16), ("ones_row_bf", BF16)]


def host_consts(k):
    k.weight("ident_bf", np.eye(128, dtype=np.float32))
    k.weight("ones_row_bf", np.ones((1, 128), np.float32))
    k.weight("ident_f32c", np.eye(128, dtype=np.float32))
    k.weight("ones_col_bfc", np.ones((128, 1), np.float32))
    k.weight("rev8", np.tile(8.0 - np.arange(8, dtype=np.float32), (128, 1)))


COMMON_SMALL_W = [("ident_bf", BF16), ("ones_row_bf", BF16),
                  ("ident_f32c", F32), ("ones_col_bfc", BF16), ("rev8", F32)]



def emit_l2_attn(k, xn_hbm, out_hbm, tag="l2a"):
    T = k.T
    TC = 512
    N = 2048
    nrows = T // N
    nch = N // TC
    INN = 2048
    NIT = INN // 128

    xnb = [bfv(k, "ARD", 0, 4096), bfv(k, "ARD", 4096, 8192)]
    exts = [bfv(k, "ARB", 0, 8320), bfv(k, "ARB", 8320, 16640)]
    ubuf = bfv(k, "ARB", 16640, 16640 + 8192)
    xgbuf = bfv(k, "ARB", 24832, 24832 + 7936)
    xgbuf = bfv(k, "ARC", 0, 8192)
    ybuf = bfv(k, "ARC", 8192, 16384)
    gtmp = bfv(k, "ARD", 8192 - 512, 8192)
    zbuf = f32v(k, "ARA", 0, 4096)
    sqb = f32v(k, "ARA", 4096, 4608)
    rows = f32v(k, "ARA", 4608, 5120)
    gtmp = f32v(k, "ARA", 5120, 5376).bitcast(BF16)
    gtmp2 = f32v(k, "ARA", 5888, 6144).bitcast(BF16)
    bus = f32v(k, "ARE", 0, 512)
    hs32 = f32v(k, "ARE", 512, 1024)
    dec = f32v(k, "ARE", 1024, 1536)
    hcar = f32v(k, "ARE", 1536, 1540)
    hsbf = f32v(k, "ARE", 1544, 1800).bitcast(BF16)
    ps = k.ps_arena

    s_x = k.sem(f"{tag}x")
    s_w = [k.sem(f"{tag}w0"), k.sem(f"{tag}w1")]
    s_t = k.sem("C_t")
    s_e = k.sem("C_e")
    s_p = k.sem("C_p")
    s_o = [k.sem(f"{tag}o0"), k.sem(f"{tag}o1")]

    win = k.dram["L2_winT"].rearrange("p (c o) -> p c o", c=NDC)
    wout = k.dram["L2_woutT"].rearrange("p (c o) -> p c o", c=NIT)
    wB = k.sb["sb_L2_wBT"]
    wC = k.sb["sb_L2_wCT"]
    cw = k.sb["sb_L2_convw"]
    cb = k.sb["sb_L2_convb"]
    dpar = k.sb["sb_L2_D"]
    lng = k.sb["sb_L2_lng"]
    lnb = k.sb["sb_L2_lnb"]
    ones_col = k.sb["ones_col_f32"]
    ones_row = k.sb["ones_row_f32"]
    slabs = [k.sb["slab0"], k.sb["slab1"]]

    xnv = xn_hbm.rearrange("p (c t) -> p c t", c=NDC)
    outv = out_hbm.rearrange("(c p) t -> p c t", p=128)

    def _vd(v):
        v.memset(dec, 0.95)
        k.inc(v.memset(hcar, 0.0), s_e)
    k.v(_vd)

    for r in range(nrows):
        em0 = s_e.n
        tprev0 = s_t.n

        def _vr(v, em0=em0, tprev0=tprev0):
            v.wait_ge(s_e.h, em0)
            if tprev0:
                v.wait_ge(s_t.h, tprev0)
            v.memset(hcar, 0.0)
            k.inc(v.memset(exts[0].rearrange("p (a b) -> p a b", b=520)[:, :, 0:3], 0.0), s_e)
        k.v(_vr)

        for i in range(nch):
            pbar(k, 0)
            t0 = r * N + i * TC
            xbuf = xnb[i % 2]
            ext = exts[i % 2]
            ext_n = exts[(i + 1) % 2]
            extv = ext.rearrange("p (a b) -> p a b", b=520)
            extnv = ext_n.rearrange("p (a b) -> p a b", b=520)

            om = [(so, so.n) for so in s_o if so.n]

            def _gx(gq, t0=t0, xbuf=xbuf, om=om):
                for so, n in om:
                    gq.wait_ge(so.h, n)
                k.dinc(gq.dma_start(out=xbuf.rearrange("p (c t) -> p c t", c=NDC),
                                    in_=xnv[:, :, t0:t0 + TC]), s_x)
            k.g(_gx)
            xm = s_x.n

            for ot in range(32):
                par = ot % 2
                region = par * 2048
                slab = slabs[par]
                em = s_e.n
                tsl = s_t.n

                swp = s_w[ot % 2]

                def _gw(gq, ot=ot, slab=slab, tsl=tsl, swp=swp):
                    if tsl:
                        gq.wait_ge(s_t.h, tsl)
                    k.dinc(gq.dma_start(
                        out=slab[:, 0:1024].rearrange("p (c o) -> p c o", o=128),
                        in_=win[:, :, ot * 128:ot * 128 + 128]), swp)
                k.g(_gw)
                wm = (swp, swp.n)

                def _t1(te, ot=ot, slab=slab, wm=wm, region=region, xm=xm, em=em):
                    te.wait_ge(wm[0].h, wm[1])
                    te.wait_ge(s_x.h, xm)
                    if em:
                        te.wait_ge(s_e.h, em)
                    last = mm_acc(te, ps[:, region:region + TC],
                                  [(slab[:, c * 128:c * 128 + 128],
                                    xbuf[:, c * TC:(c + 1) * TC]) for c in range(NDC)])
                    k.inc(last, s_t)
                k.t(_t1)
                tm = s_t.n

                if ot < 16:
                    def _s1(sc, ot=ot, region=region, tm=tm):
                        sc.wait_ge(s_t.h, tm)
                        k.inc(sc.activation(out=extv[:, ot, 3:515],
                                            in_=ps[:, region:region + TC],
                                            func=AF.Copy), s_e)
                    k.s(_s1)
                else:
                    og = ot - 16

                    def _s1(sc, og=og, region=region, tm=tm):
                        sc.wait_ge(s_t.h, tm)
                        k.inc(sc.activation(out=xgbuf[:, og * TC:(og + 1) * TC],
                                            in_=ps[:, region:region + TC],
                                            func=AF.Copy), s_e)
                    k.s(_s1)

            emc = s_e.n

            def _vc(v, emc=emc):
                v.wait_ge(s_e.h, emc)
                for ot in range(NIT):
                    v.tensor_copy(out=extnv[:, ot, 0:3], in_=extv[:, ot, 512:515])
                for ot in range(NIT):
                    v.tensor_scalar(out=ubuf[:, ot * TC:(ot + 1) * TC],
                                    in0=extv[:, ot, 0:TC],
                                    scalar1=cw[:, ot * 4:ot * 4 + 1], scalar2=None,
                                    op0=OP.mult)
                for kk in range(1, 4):
                    for ot in range(NIT):
                        ins = v.scalar_tensor_tensor(
                            out=ubuf[:, ot * TC:(ot + 1) * TC],
                            in0=extv[:, ot, kk:kk + TC],
                            scalar=cw[:, ot * 4 + kk:ot * 4 + kk + 1],
                            in1=ubuf[:, ot * TC:(ot + 1) * TC],
                            op0=OP.mult, op1=OP.add)
                k.inc(ins, s_e)
            k.v(_vc)
            vmc = s_e.n

            def _sc2(sc, vmc=vmc):
                sc.wait_ge(s_e.h, vmc)
                for ot in range(NIT):
                    ins = sc.activation(out=ubuf[:, ot * TC:(ot + 1) * TC],
                                        in_=ubuf[:, ot * TC:(ot + 1) * TC],
                                        func=AF.Silu, bias=cb[:, ot:ot + 1], scale=1.0)
                k.inc(ins, s_e)
            k.s(_sc2)

            pbar(k, 1)
            em = s_e.n

            def _t2(te, em=em):
                te.wait_ge(s_e.h, em)
                last = mm_acc(te, ps[0:64, 0:TC],
                              [(wB[:, c * 64:(c + 1) * 64],
                                ubuf[:, c * TC:(c + 1) * TC]) for c in range(NIT)])
                k.inc(last, s_t)
            k.t(_t2)
            tm2 = s_t.n

            def _v3(v, tm2=tm2):
                v.wait_ge(s_t.h, tm2)
                k.inc(v.tensor_scalar(out=bus[0:64, :], in0=ps[0:64, 0:TC], scalar1=0.05,
                                      scalar2=None, op0=OP.mult), s_e)
                v.wait_ge(s_e.h, s_e.n)
                k.inc(v.tensor_tensor_scan(out=hs32[0:64, :], data0=dec[0:64, :],
                                           data1=bus[0:64, :], initial=hcar[0:64, 0:1],
                                           op0=OP.mult, op1=OP.add), s_e)
                v.wait_ge(s_e.h, s_e.n)
                v.tensor_copy(out=hcar[0:64, 0:1], in_=hs32[0:64, TC - 1:TC])
                k.inc(v.tensor_copy(out=hsbf[0:64, 0:TC], in_=hs32[0:64, :]), s_e)
            k.v(_v3)
            vm3 = s_e.n

            pbar(k, 2)
            for ot in range(NIT):
                par = ot % 2
                region = 2048 + par * 1024
                em = s_e.n

                def _t3(te, ot=ot, region=region, vm3=vm3, em=em):
                    te.wait_ge(s_e.h, max(vm3, em))
                    k.inc(te.matmul(ps[:, region:region + TC],
                                    wC[0:64, ot * 128:(ot + 1) * 128],
                                    hsbf[0:64, 0:TC], start=True, stop=True), s_t)
                k.t(_t3)
                tm3 = s_t.n

                def _s3(sc, ot=ot, tm3=tm3):
                    sc.wait_ge(s_t.h, tm3)
                    k.inc(sc.activation(out=gtmp[:, 0:TC],
                                        in_=xgbuf[:, ot * TC:(ot + 1) * TC],
                                        func=AF.Silu), s_e)
                k.s(_s3)
                sm3 = s_e.n

                def _v4(v, ot=ot, region=region, tm3=tm3, sm3=sm3):
                    v.wait_ge(s_t.h, tm3)
                    v.wait_ge(s_e.h, sm3)
                    k.inc(v.scalar_tensor_tensor(out=gtmp2[:, 0:TC],
                                                 in0=ubuf[:, ot * TC:(ot + 1) * TC],
                                                 scalar=dpar[:, ot:ot + 1],
                                                 in1=ps[:, region:region + TC],
                                                 op0=OP.mult, op1=OP.add), s_e)
                k.v(_v4)
                vm4 = s_e.n

                def _p4(g, ot=ot, vm4=vm4):
                    g.wait_ge(s_e.h, vm4)
                    k.inc(g.tensor_tensor(out=ybuf[:, ot * TC:(ot + 1) * TC],
                                          in0=gtmp2[:, 0:TC],
                                          in1=gtmp[:, 0:TC], op=OP.mult), s_p)
                k.g(_p4)

            pbar(k, 3)
            v5m = {}
            tln_mark = {}
            oln_mark = {}
            for o in range(NDC):
                par = o % 2
                region = par * 2048
                em = s_e.n
                tsl = s_t.n

                sw2 = s_w[o % 2]

                def _gw2(gq, o=o, tsl=tsl, sw2=sw2):
                    if tsl:
                        gq.wait_ge(s_t.h, tsl)
                    k.dinc(gq.dma_start(
                        out=slabs[o % 2][:, 0:2048].rearrange("p (c o) -> p c o", o=128),
                        in_=wout[:, :, o * 128:o * 128 + 128]), sw2)
                k.g(_gw2)
                wm2 = sw2.n

                pm = s_p.n

                def _t4(te, o=o, region=region, wm2=wm2, em=em, pm=pm, sw2=sw2):
                    te.wait_ge(sw2.h, wm2)
                    if pm:
                        te.wait_ge(s_p.h, pm)
                    if em:
                        te.wait_ge(s_e.h, em)
                    last = mm_acc(te, ps[:, region:region + TC],
                                  [(slabs[o % 2][:, c * 128:c * 128 + 128],
                                    ybuf[:, c * TC:(c + 1) * TC]) for c in range(NIT)])
                    k.inc(last, s_t)
                k.t(_t4)
                tm4 = s_t.n

                def _v5(v, o=o, region=region, tm4=tm4):
                    v.wait_ge(s_t.h, tm4)
                    k.inc(v.tensor_tensor(out=zbuf[:, o * TC:(o + 1) * TC],
                                          in0=ps[:, region:region + TC],
                                          in1=xbuf[:, o * TC:(o + 1) * TC], op=OP.add), s_e)
                k.v(_v5)
                v5m[o] = s_e.n

            if getattr(k, 'dbg_l2a', False) and r == 0 and i == 0:
                emz = s_e.n

                def _gd(gq, emz=emz):
                    gq.wait_ge(s_e.h, emz)
                    k.dinc(gq.dma_start(out=k.dram["dbg_z"][:, :], in_=zbuf[:, 0:4096]), s_o)
                    k.dinc(gq.dma_start(out=k.dram["dbg_u"][:, :], in_=ubuf[:, 0:8192]), s_o)
                    k.dinc(gq.dma_start(out=k.dram["dbg_hs"][:, :], in_=hs32[0:64, :]), s_o)
                    k.dinc(gq.dma_start(out=k.dram["dbg_y"][:, :], in_=ybuf[:, 0:8192]), s_o)
                    k.dinc(gq.dma_start(out=k.dram["dbg_xg"][:, :], in_=xgbuf[:, 0:8192]), s_o)
                k.g(_gd)
                dmark = s_o.n

                def _vw(v, dmark=dmark):
                    v.wait_ge(s_o.h, dmark)
                k.v(_vw)
                k.s(_vw)
                k.t(_vw)

            k.barrier()
            sq2 = [sqb, f32v(k, "ARA", 5376, 5888)]
            for o in range(NDC):
                vm5 = v5m[o]
                sqo = sq2[o % 2]
                tln_prev = tln_mark[o - 2] if o >= 2 else 0

                om_prev2 = [(so, so.n) for so in s_o if so.n] if o < 2 else []

                def _s4(sc, o=o, vm5=vm5, sqo=sqo, tln_prev=tln_prev, om_prev2=om_prev2):
                    sc.wait_ge(s_e.h, vm5)
                    for so, n in om_prev2:
                        sc.wait_ge(so.h, n)
                    if tln_prev:
                        sc.wait_ge(s_t.h, tln_prev)
                    k.inc(sc.activation(out=sqo[:, 0:TC],
                                        in_=zbuf[:, o * TC:(o + 1) * TC],
                                        func=AF.Square), s_e)
                k.s(_s4)
                sm4 = s_e.n

                ep_prev = s_e.n

                def _t5(te, o=o, sm4=sm4, sqo=sqo, ep_prev=ep_prev):
                    te.wait_ge(s_e.h, sm4)
                    if o == 0:
                        te.wait_ge(s_e.h, ep_prev)
                    te.matmul(ps[0:1, 1024:1024 + TC], ones_col[:],
                              zbuf[:, o * TC:(o + 1) * TC],
                              start=(o == 0), stop=(o == NDC - 1))
                    k.inc(te.matmul(ps[0:1, 1536:1536 + TC], ones_col[:],
                                    sqo[:, 0:TC], start=(o == 0), stop=(o == NDC - 1)), s_t)
                k.t(_t5)
                tln_mark[o] = s_t.n
            tm5 = s_t.n

            pbar(k, 5)
            mrow = rows[0:1, 0:TC]

            def _v6(v, tm5=tm5):
                v.wait_ge(s_t.h, tm5)
                v.tensor_scalar(out=mrow, in0=ps[0:1, 1024:1024 + TC],
                                scalar1=1.0 / D, scalar2=None, op0=OP.mult)
                v.tensor_scalar(out=sqb[0:1, 0:TC], in0=ps[0:1, 1536:1536 + TC],
                                scalar1=1.0 / D, scalar2=None, op0=OP.mult)
                k.inc(v.tensor_tensor(out=sq2[1][0:1, 0:TC], in0=mrow, in1=mrow,
                                      op=OP.mult), s_e)
                v.wait_ge(s_e.h, s_e.n)
                k.inc(v.tensor_tensor(out=sqb[0:1, 0:TC], in0=sqb[0:1, 0:TC],
                                      in1=sq2[1][0:1, 0:TC], op=OP.subtract), s_e)
                v.wait_ge(s_e.h, s_e.n)
                k.inc(v.tensor_scalar(out=sqb[0:1, 0:TC], in0=sqb[0:1, 0:TC],
                                      scalar1=1e-5, scalar2=None, op0=OP.add), s_e)
            k.v(_v6)
            vm6 = s_e.n

            def _s5(sc, vm6=vm6):
                sc.wait_ge(s_e.h, vm6)
                act_chain(k, sc, s_e, sc.activation(out=sqb[0:1, 0:TC],
                                                    in_=sqb[0:1, 0:TC], func=AF.Ln))
                k.inc(sc.activation(out=sqb[0:1, 0:TC], in_=sqb[0:1, 0:TC],
                                    func=AF.Exp, bias=0.0, scale=-0.5), s_e)
            k.s(_s5)
            sm5 = s_e.n

            def _v7(v, sm5=sm5):
                v.wait_ge(s_e.h, sm5)
                k.inc(v.scalar_tensor_tensor(out=mrow, in0=mrow, scalar=-1.0,
                                             in1=sqb[0:1, 0:TC], op0=OP.mult, op1=OP.mult), s_e)
            k.v(_v7)
            vm7 = s_e.n

            def _t6(te, vm7=vm7):
                te.wait_ge(s_e.h, vm7)
                te.matmul(ps[:, 1024:1024 + TC], ones_row[:], sqb[0:1, 0:TC],
                          start=True, stop=True)
                k.inc(te.matmul(ps[:, 1536:1536 + TC], ones_row[:], mrow,
                                start=True, stop=True), s_t)
            k.t(_t6)
            tm6 = s_t.n

            pbar(k, 6)
            for o in range(NDC):
                sqo = sq2[o % 2]
                so = s_o[o % 2]
                omk = oln_mark[o - 2] if o >= 2 else 0

                def _v8(v, o=o, tm6=tm6, sqo=sqo, omk=omk, so=so):
                    v.wait_ge(s_t.h, tm6)
                    if omk:
                        v.wait_ge(so.h, omk)
                    k.inc(v.tensor_tensor(out=sqo[:, 0:TC], in0=zbuf[:, o * TC:(o + 1) * TC],
                                          in1=ps[:, 1024:1024 + TC], op=OP.mult), s_e)
                    v.wait_ge(s_e.h, s_e.n)
                    k.inc(v.tensor_tensor(out=sqo[:, 0:TC], in0=sqo[:, 0:TC],
                                          in1=ps[:, 1536:1536 + TC], op=OP.add), s_e)
                    v.wait_ge(s_e.h, s_e.n)
                    k.inc(v.tensor_scalar(out=sqo[:, 0:TC], in0=sqo[:, 0:TC],
                                          scalar1=lng[:, o:o + 1], scalar2=lnb[:, o:o + 1],
                                          op0=OP.mult, op1=OP.add), s_e)
                k.v(_v8)
                vm8 = s_e.n

                def _g2(gq, o=o, vm8=vm8, t0=t0, sqo=sqo, so=so):
                    gq.wait_ge(s_e.h, vm8)
                    k.dinc(gq.dma_start(out=outv[:, o, t0:t0 + TC], in_=sqo[:, 0:TC]), so)
                k.g(_g2)
                oln_mark[o] = so.n
    return s_o


def l2_attn_host(k, inputs):
    k.weight("L2_winT", wimg_lhsT(np.asarray(inputs["L2_attn_W_in"]).T))
    k.weight("L2_woutT", wimg_lhsT(np.asarray(inputs["L2_attn_W_out"]).T))
    k.weight("L2_wBT", wimg_lhsT(np.asarray(inputs["L2_attn_W_B"]).T))
    wc = np.asarray(inputs["L2_attn_W_C"])
    k.weight("L2_wCT", np.ascontiguousarray(wc.T))
    cw = np.asarray(inputs["L2_attn_conv_w"])[:, 0, :]
    k.weight("L2_convw", np.ascontiguousarray(
        cw.reshape(16, 128, 4).transpose(1, 0, 2).reshape(128, 64)))
    k.weight("L2_convb", bimg(np.asarray(inputs["L2_attn_conv_b"])))
    k.weight("L2_D", bimg(np.asarray(inputs["L2_attn_D_param"])))
    k.weight("L2_lng", bimg(np.asarray(inputs["L2_attn_ln_gamma"])))
    k.weight("L2_lnb", bimg(np.asarray(inputs["L2_attn_ln_beta"])))


L2A_SMALL_W = [("L2_convw", F32),
               ("L2_convb", F32), ("L2_D", F32), ("L2_lng", F32), ("L2_lnb", F32)]
L2A_ALIAS_W = ["L2_wBT", "L2_wCT"]


def alloc_l2a_alias(k):
    k.sb["sb_L2_wBT"] = k.sb["ARA"][0:128, 24576:25600]
    k.sb["sb_L2_wCT"] = k.sb["ARA"][0:64, 25600:27648]



L2F_FLOOR = -8.733654
L2F_EPS = 0.1
L2F_C = 0.2079442


def emit_l2_ffn(k, x_hbm, xn_hbm, stats_hbm, out_hbm, tag="l2f"):
    T = k.T
    TC = 512
    N = 2048
    nrows = T // N
    nch_r = N // TC

    ps = k.ps_arena
    ones_col = k.sb["ones_col_f32"]
    ones_row = k.sb["ones_row_f32"]
    idf = k.sb["sb_ident_f32c"]
    rt = k.sb["sb_L2f_rt"]
    g_l = k.sb["sb_L2f_gl"]
    g_h = k.sb["sb_L2f_gh"]
    b1l = k.sb["sb_L2f_b1l"]
    b2l = k.sb["sb_L2f_b2l"]
    b1h = k.sb["sb_L2f_b1h"]
    b2h = k.sb["sb_L2f_b2h"]
    slabs = [k.sb["slab0"], k.sb["slab1"]]

    s_x = k.sem(f"{tag}x")
    s_x2 = k.sem(f"{tag}x2")
    s_x3 = k.sem(f"{tag}x3g")
    s_t = k.sem("C_t")
    s_e = k.sem("C_e")
    s_o = k.sem(f"{tag}o")

    xv = x_hbm.rearrange("(c p) t -> p c t", p=128)
    xnv = xn_hbm.rearrange("p (c t) -> p c t", c=NDC)
    outv = out_hbm.rearrange("(c p) t -> p c t", p=128)
    mask_hbm = k.dram["mask_s"]

    srow = f32v(k, "ARB", 0, 2048)
    rrow = f32v(k, "ARB", 2048, 4096)
    brow = f32v(k, "ARB", 4096, 6144)
    xbuf = f32v(k, "ARA", 0, 4096)
    spf = f32v(k, "ARE", 0, 16)
    Bpf = f32v(k, "ARE", 16, 32)
    vpf = f32v(k, "ARE", 32, 48)
    zpf = f32v(k, "ARE", 48, 64)
    wrk = f32v(k, "ARE", 64, 80)
    wrk2 = f32v(k, "ARE", 80, 96)
    cols = f32v(k, "ARE", 96, 128)
    rowt = f32v(k, "ARE", 128, 512)
    onespf = f32v(k, "ARE", 512, 528)

    def _vinit(v):
        v.memset(onespf, 1.0)
        v.memset(cols[:, 7:8], 0.0)
        k.inc(v.memset(rowt[0:1, 9:10], 0.0), s_e)
    k.v(_vinit)

    def _t0(te):
        te.wait_ge(k.sems["wload"].h, k.sems["wload"].n)
        k.inc(te.matmul(ps[0:1, 0:8], ones_col[:], rt[:], start=True, stop=True), s_t)
    k.t(_t0)
    t0m = s_t.n

    def _v0(v):
        v.wait_ge(s_t.h, t0m)
        k.inc(v.reduce_sum(out=rowt[0:1, 8:9], in_=ps[0:1, 0:8], axis=AX.X), s_e)
    k.v(_v0)

    for r in range(nrows):
        R = r * N
        for i in range(nch_r):
            tprev = s_t.n

            def _gx(yq, i=i, R=R, tprev=tprev):
                if tprev:
                    yq.wait_ge(s_t.h, tprev)
                k.dinc(yq.dma_start(out=xbuf, in_=xv[:, :, R + i * TC:R + (i + 1) * TC]), s_x)
            k.y(_gx)
            xm = s_x.n

            eprev = s_e.n

            def _t1(te, i=i, xm=xm, eprev=eprev):
                te.wait_ge(s_x.h, xm)
                if eprev:
                    te.wait_ge(s_e.h, eprev)
                last = mm_acc(te, ps[0:1, 512:512 + TC],
                              [(rt[:, c:c + 1], xbuf[:, c * TC:(c + 1) * TC])
                               for c in range(NDC)])
                k.inc(last, s_t)
            k.t(_t1)
            tm = s_t.n

            def _v1(v, i=i, tm=tm):
                v.wait_ge(s_t.h, tm)
                k.inc(v.tensor_copy(out=srow[0:1, i * TC:(i + 1) * TC],
                                    in_=ps[0:1, 512:512 + TC]), s_e)
            k.v(_v1)

        em = s_e.n

        def _gs(yq, R=R, em=em):
            if em:
                yq.wait_ge(s_e.h, em)
            k.dinc(yq.dma_start(out=rrow[0:1, 0:N], in_=stats_hbm[0:1, R:R + N]), s_x2)
            k.dinc(yq.dma_start(out=brow[0:1, 0:N], in_=stats_hbm[1:2, R:R + N]), s_x2)
        k.y(_gs)
        xm2 = s_x2.n

        def _v2(v, xm2=xm2):
            v.wait_ge(s_x2.h, xm2)
            v.wait_ge(s_e.h, s_e.n)
            k.inc(v.tensor_tensor(out=srow[0:1, 0:N], in0=srow[0:1, 0:N],
                                  in1=rrow[0:1, 0:N], op=OP.mult), s_e)
            v.wait_ge(s_e.h, s_e.n)
            k.inc(v.tensor_scalar(out=brow[0:1, 0:N], in0=brow[0:1, 0:N],
                                  scalar1=rowt[0:1, 8:9], scalar2=None, op0=OP.mult), s_e)
            v.wait_ge(s_e.h, s_e.n)
            k.inc(v.tensor_tensor(out=srow[0:1, 0:N], in0=srow[0:1, 0:N],
                                  in1=brow[0:1, 0:N], op=OP.add), s_e)
        k.v(_v2)
        vm2 = s_e.n

        def _gt(gq, vm2=vm2, R=R):
            gq.wait_ge(s_e.h, vm2)
            k.dinc(gq.dma_start(out=k.dram["srow_s"][0:1, R:R + N], in_=srow[0:1, 0:N]), s_x3)
        k.g(_gt)
        xm3a = s_x3.n

        def _gt2(gq, xm3a=xm3a, R=R):
            gq.wait_ge(s_x3.h, xm3a)
            k.dinc(gq.dma_start(out=spf,
                                in_=k.dram["srow_s"][0:1, R:R + N]
                                .rearrange("o (p f) -> (o p) f", p=128)), s_x3)
        k.g(_gt2)
        xm3 = s_x3.n

        def _v3(v, xm3=xm3):
            v.wait_ge(s_x3.h, xm3)
            k.inc(v.tensor_scalar(out=Bpf, in0=spf, scalar1=0.0, scalar2=None,
                                  op0=OP.max), s_e)
        k.v(_v3)

        for it in range(50):
            em_it = s_e.n

            def _vi1(v, em_it=em_it):
                v.wait_ge(s_e.h, em_it)
                k.inc(v.tensor_tensor(out=vpf, in0=spf, in1=Bpf, op=OP.subtract), s_e)
                v.wait_ge(s_e.h, s_e.n)
                k.inc(v.reduce_max(out=cols[:, 5:6], in_=vpf, axis=AX.X), s_e)
            k.v(_vi1)
            vi1 = s_e.n

            def _ti1(te, vi1=vi1):
                te.wait_ge(s_e.h, vi1)
                k.inc(te.transpose(ps[0:1, 1024:1024 + 128], cols[:, 5:6], idf[:]), s_t)
            k.t(_ti1)
            ti1 = s_t.n

            def _vi2(v, ti1=ti1):
                v.wait_ge(s_t.h, ti1)
                k.inc(v.reduce_max(out=rowt[0:1, 0:1], in_=ps[0:1, 1024:1024 + 128],
                                   axis=AX.X), s_e)
            k.v(_vi2)
            vi2 = s_e.n

            def _ti2(te, vi2=vi2):
                te.wait_ge(s_e.h, vi2)
                k.inc(te.matmul(ps[:, 1536:1537], ones_row[:], rowt[0:1, 0:1],
                                start=True, stop=True), s_t)
            k.t(_ti2)
            ti2 = s_t.n

            def _vi3(v, ti2=ti2):
                v.wait_ge(s_t.h, ti2)
                k.inc(v.tensor_scalar(out=cols[:, 1:2], in0=ps[:, 1536:1537],
                                      scalar1=-1.0 / L2F_EPS, scalar2=None, op0=OP.mult), s_e)
            k.v(_vi3)
            vi3 = s_e.n

            def _si1(sc, vi3=vi3):
                sc.wait_ge(s_e.h, vi3)
                k.inc(sc.activation(out=wrk, in_=vpf, func=AF.Exp,
                                    bias=cols[:, 1:2], scale=1.0 / L2F_EPS,
                                    accum_out=cols[:, 6:7]), s_e)
            k.s(_si1)
            si1 = s_e.n

            def _ti3(te, si1=si1):
                te.wait_ge(s_e.h, si1)
                k.inc(te.matmul(ps[0:1, 1664:1665], ones_col[:], cols[:, 6:7],
                                start=True, stop=True), s_t)
            k.t(_ti3)
            ti3 = s_t.n

            def _si2(sc, ti3=ti3):
                sc.wait_ge(s_t.h, ti3)
                k.inc(sc.activation(out=rowt[0:1, 1:2], in_=ps[0:1, 1664:1665],
                                    func=AF.Ln), s_e)
            k.s(_si2)
            si2 = s_e.n

            def _vi4(v, si2=si2):
                v.wait_ge(s_e.h, si2)
                k.inc(v.tensor_scalar(out=rowt[0:1, 2:3], in0=rowt[0:1, 1:2],
                                      scalar1=-L2F_EPS, scalar2=float(L2F_C),
                                      op0=OP.mult, op1=OP.add), s_e)
                v.wait_ge(s_e.h, s_e.n)
                k.inc(v.tensor_tensor(out=rowt[0:1, 2:3], in0=rowt[0:1, 2:3],
                                      in1=rowt[0:1, 0:1], op=OP.subtract), s_e)
            k.v(_vi4)
            vi4 = s_e.n

            def _ti4(te, vi4=vi4):
                te.wait_ge(s_e.h, vi4)
                k.inc(te.matmul(ps[:, 1792:1793], ones_row[:], rowt[0:1, 2:3],
                                start=True, stop=True), s_t)
            k.t(_ti4)
            ti4 = s_t.n

            def _vi5(v, ti4=ti4):
                v.wait_ge(s_t.h, ti4)
                k.inc(v.tensor_copy(out=cols[:, 2:3], in_=ps[:, 1792:1793]), s_e)
                v.wait_ge(s_e.h, s_e.n)
                k.inc(v.tensor_scalar(out=Bpf, in0=spf, scalar1=cols[:, 2:3],
                                      scalar2=0.0, op0=OP.add, op1=OP.max), s_e)
            k.v(_vi5)

        em_s = s_e.n

        def _vs1(v, em_s=em_s):
            v.wait_ge(s_e.h, em_s)
            k.inc(v.tensor_scalar(out=zpf, in0=spf, scalar1=cols[:, 2:3],
                                  scalar2=0.0, op0=OP.add, op1=OP.min), s_e)
            v.wait_ge(s_e.h, s_e.n)
            k.inc(v.tensor_scalar(out=zpf, in0=zpf, scalar1=float(L2F_FLOOR),
                                  scalar2=None, op0=OP.max), s_e)
            v.wait_ge(s_e.h, s_e.n)
            k.inc(v.tensor_scalar(out=wrk, in0=zpf, scalar1=float(L2F_FLOOR),
                                  scalar2=None, op0=OP.is_gt), s_e)
            v.wait_ge(s_e.h, s_e.n)
            k.inc(v.scalar_tensor_tensor(out=wrk2, in0=wrk, scalar=-1.0, in1=onespf,
                                         op0=OP.mult, op1=OP.add), s_e)
            v.wait_ge(s_e.h, s_e.n)
            k.inc(v.reduce_sum(out=cols[:, 6:7], in_=wrk, axis=AX.X), s_e)
        k.v(_vs1)
        vs1 = s_e.n

        def _ts1(te, vs1=vs1):
            te.wait_ge(s_e.h, vs1)
            k.inc(te.matmul(ps[0:1, 1664:1665], ones_col[:], cols[:, 6:7],
                            start=True, stop=True), s_t)
        k.t(_ts1)
        ts1 = s_t.n

        def _vs2(v, ts1=ts1):
            v.wait_ge(s_t.h, ts1)
            k.inc(v.tensor_scalar(out=rowt[0:1, 3:4], in0=ps[0:1, 1664:1665],
                                  scalar1=-1.0, scalar2=1024.0, op0=OP.mult, op1=OP.add), s_e)
            v.wait_ge(s_e.h, s_e.n)
            k.inc(v.tensor_tensor_scan(out=wrk, data0=onespf, data1=wrk2,
                                       initial=cols[0:128, 7:8], op0=OP.mult, op1=OP.add)
                  if False else
                  v.tensor_tensor_scan(out=wrk, data0=onespf, data1=wrk2,
                                       initial=cols[:, 7:8], op0=OP.mult, op1=OP.add), s_e)
        k.v(_vs2)
        vs2 = s_e.n

        def _ts2(te, vs2=vs2):
            te.wait_ge(s_e.h, vs2)
            k.inc(te.transpose(ps[0:1, 1024:1024 + 128], wrk[:, 15:16], idf[:]), s_t)
        k.t(_ts2)
        ts2 = s_t.n

        def _vs3(v, ts2=ts2):
            v.wait_ge(s_t.h, ts2)
            k.inc(v.tensor_copy(out=rowt[0:1, 16:144], in_=ps[0:1, 1024:1024 + 128]), s_e)
            v.wait_ge(s_e.h, s_e.n)
            k.inc(v.tensor_tensor_scan(out=rowt[0:1, 160:288], data0=ones_row[0:1, 0:128],
                                       data1=rowt[0:1, 16:144], initial=rowt[0:1, 9:10],
                                       op0=OP.mult, op1=OP.add), s_e)
            v.wait_ge(s_e.h, s_e.n)
            k.inc(v.tensor_tensor(out=rowt[0:1, 160:288], in0=rowt[0:1, 160:288],
                                  in1=rowt[0:1, 16:144], op=OP.subtract), s_e)
        k.v(_vs3)
        vs3 = s_e.n

        def _ts3(te, vs3=vs3):
            te.wait_ge(s_e.h, vs3)
            k.inc(te.transpose(ps[:, 1024:1025], rowt[0:1, 160:288], idf[0:1, 0:1]), s_t)
        k.t(_ts3)
        ts3 = s_t.n

        def _vs4(v, ts3=ts3):
            v.wait_ge(s_t.h, ts3)
            k.inc(v.tensor_copy(out=cols[:, 3:4], in_=ps[:, 1024:1025]), s_e)
            v.wait_ge(s_e.h, s_e.n)
            k.inc(v.tensor_tensor(out=wrk, in0=wrk, in1=wrk2, op=OP.subtract), s_e)
            v.wait_ge(s_e.h, s_e.n)
            k.inc(v.tensor_scalar(out=wrk, in0=wrk, scalar1=cols[:, 3:4],
                                  scalar2=None, op0=OP.add), s_e)
        k.v(_vs4)
        vs4 = s_e.n

        def _ts4(te, vs4=vs4):
            te.wait_ge(s_e.h, vs4)
            k.inc(te.matmul(ps[:, 1792:1793], ones_row[:], rowt[0:1, 3:4],
                            start=True, stop=True), s_t)
        k.t(_ts4)
        ts4 = s_t.n

        def _vs5(v, ts4=ts4, r=r):
            v.wait_ge(s_t.h, ts4)
            k.inc(v.tensor_copy(out=cols[:, 4:5], in_=ps[:, 1792:1793]), s_e)
            v.wait_ge(s_e.h, s_e.n)
            k.inc(v.tensor_scalar(out=wrk, in0=wrk, scalar1=cols[:, 4:5],
                                  scalar2=None, op0=OP.is_lt), s_e)
            v.wait_ge(s_e.h, s_e.n)
            k.inc(v.tensor_tensor(out=wrk, in0=wrk, in1=wrk2, op=OP.mult), s_e)
            v.wait_ge(s_e.h, s_e.n)
            k.inc(v.tensor_scalar(out=wrk2, in0=zpf, scalar1=float(L2F_FLOOR),
                                  scalar2=None, op0=OP.is_gt), s_e)
            v.wait_ge(s_e.h, s_e.n)
            k.inc(v.tensor_tensor(out=wrk, in0=wrk, in1=wrk2, op=OP.add), s_e)
        k.v(_vs5)
        vs5 = s_e.n

        def _gm(gq, vs5=vs5, R=R):
            gq.wait_ge(s_e.h, vs5)
            k.dinc(gq.dma_start(out=mask_hbm[0:1, R:R + N]
                                .rearrange("o (p f) -> (o p) f", p=128),
                                in_=wrk), s_o)
        k.g(_gm)

    return s_o


def emit_l2_ffn_compute(k, xn_hbm, out_hbm, tag="l2fc"):
    T = k.T
    TC = 512
    nch = T // TC
    NHC = 32

    ps = k.ps_arena
    w1h = k.dram["L2f_w1hT"]
    w2h = k.dram["L2f_w2hT"].rearrange("p (c o) -> p c o", c=NHC)
    w1l = k.dram["L2f_w1lT"]
    w2l = k.dram["L2f_w2lT"]
    g_l = k.sb["sb_L2f_gl"]
    g_h = k.sb["sb_L2f_gh"]
    b1l = k.sb["sb_L2f_b1l"]
    b2l = k.sb["sb_L2f_b2l"]
    b1h = k.sb["sb_L2f_b1h"]
    b2h = k.sb["sb_L2f_b2h"]
    ones_row = k.sb["ones_row_f32"]
    ones_col_bf = k.sb["sb_ones_col_bfc"]
    slabs = [k.sb["slab0"], k.sb["slab1"]]
    mask_hbm = k.dram["mask_s"]
    xnv = xn_hbm.rearrange("p (c t) -> p c t", c=NDC)
    outv = out_hbm.rearrange("(c p) t -> p c t", p=128)

    w1h_sb = k.sb["ARA"]
    xnb = bfv(k, "ARD", 0, 4096)
    sqb = bfv(k, "ARC", 0, 4096)
    hbuf = bfv(k, "ARC", 0, 16384)
    nxl = bfv(k, "ARB", 0, 4096)
    nxh = bfv(k, "ARB", 4096, 8192)
    hl = bfv(k, "ARB", 8192, 10240)
    nxt = [bfv(k, "ARB", 10240, 10752), bfv(k, "ARB", 10752, 11264)]
    lout = f32v(k, "ARB", 6144, 10240)
    fout = f32v(k, "ARB", 10240, 14336)
    htmp = f32v(k, "ARB", 14336, 14848)
    mskb = f32v(k, "ARB", 14848, 15360)
    frow = f32v(k, "ARB", 15360, 15872)

    s_x = k.sem(f"{tag}x")
    s_w0 = k.sem(f"{tag}w0r")
    s_wl = k.sem(f"{tag}wl")
    s_w = [k.sem(f"{tag}wa"), k.sem(f"{tag}wb")]
    s_t = k.sem("C_t")
    s_e = k.sem("C_e")
    s_p = k.sem("C_p")
    s_o = k.sem(f"{tag}o")

    def _gw0(gq):
        k.dinc(gq.dma_start(out=w1h_sb[:], in_=w1h[:, :]), s_w0)
    k.g(_gw0)
    w0m = s_w0.n

    for i in range(nch):
        t0 = i * TC
        om = s_o.n

        def _gx(gq, t0=t0, om=om):
            if om:
                gq.wait_ge(s_o.h, om)
            k.dinc(gq.dma_start(out=xnb.rearrange("p (c t) -> p c t", c=NDC),
                                in_=xnv[:, :, t0:t0 + TC]), s_x)
            k.dinc(gq.dma_start(out=frow[0:1, 0:TC],
                                in_=mask_hbm[0:1, t0:t0 + TC]), s_x)
        k.g(_gx)
        xm = s_x.n

        em = s_e.n

        def _s0(sc, xm=xm, em=em):
            sc.wait_ge(s_x.h, xm)
            if em:
                sc.wait_ge(s_e.h, em)
            k.inc(sc.activation(out=sqb[:, 0:4096], in_=xnb[:, 0:4096], func=AF.Square), s_e)
        k.s(_s0)
        s0m = s_e.n

        def _t0(te, s0m=s0m):
            te.wait_ge(s_e.h, s0m)
            last = mm_acc(te, ps[0:1, 0:TC],
                          [(ones_col_bf[:], sqb[:, c * TC:(c + 1) * TC]) for c in range(NDC)])
            k.inc(te.matmul(ps[:, 512:512 + TC], ones_row[:], frow[0:1, 0:TC],
                            start=True, stop=True), s_t)
        k.t(_t0)
        t0m2 = s_t.n

        def _v0(v, t0m2=t0m2):
            v.wait_ge(s_t.h, t0m2)
            k.inc(v.tensor_copy(out=mskb[:, 0:TC], in_=ps[:, 512:512 + TC]), s_e)
            k.inc(v.tensor_scalar(out=frow[0:1, 0:TC],
                                  in0=ps[0:1, 0:TC], scalar1=1e-24, scalar2=None,
                                  op0=OP.max), s_e)
        k.v(_v0)
        v0m = s_e.n

        def _s1(sc, v0m=v0m):
            sc.wait_ge(s_e.h, v0m)
            act_chain(k, sc, s_e, sc.activation(out=frow[0:1, 0:TC], in_=frow[0:1, 0:TC],
                                                func=AF.Ln))
            k.inc(sc.activation(out=frow[0:1, 0:TC], in_=frow[0:1, 0:TC],
                                func=AF.Exp, bias=0.0, scale=-0.5), s_e)
        k.s(_s1)
        s1m = s_e.n

        def _t1(te, s1m=s1m):
            te.wait_ge(s_e.h, s1m)
            k.inc(te.matmul(ps[:, 1024:1024 + TC], ones_row[:], frow[0:1, 0:TC],
                            start=True, stop=True), s_t)
        k.t(_t1)
        t1m = s_t.n

        def _v1(v, t1m=t1m):
            v.wait_ge(s_t.h, t1m)
            for c in range(NDC):
                k.inc(v.tensor_tensor(out=nxt[c % 2][:, 0:512],
                                      in0=xnb[:, c * TC:(c + 1) * TC],
                                      in1=ps[:, 1024:1024 + TC], op=OP.mult), s_e)
                v.wait_ge(s_e.h, s_e.n)
                v.tensor_scalar(out=nxl[:, c * TC:(c + 1) * TC],
                                in0=nxt[c % 2][:, 0:512],
                                scalar1=g_l[:, c:c + 1], scalar2=32.0,
                                op0=OP.mult, op1=OP.mult)
                ins = v.tensor_scalar(out=nxh[:, c * TC:(c + 1) * TC],
                                      in0=nxt[c % 2][:, 0:512],
                                      scalar1=g_h[:, c:c + 1], scalar2=32.0,
                                      op0=OP.mult, op1=OP.mult)
            k.inc(ins, s_e)
        k.v(_v1)
        v1m = s_e.n

        def _gwl(gq, i=i):
            gq.wait_ge(s_t.h, s_t.n)
            k.dinc(gq.dma_start(out=slabs[0][:, 0:4096], in_=w1l[:, :]), s_wl)
            k.dinc(gq.dma_start(out=slabs[1][:, 0:4096], in_=w2l[:, :]), s_wl)
        k.g(_gwl)
        wlm = s_wl.n

        lep = {}
        for ol in range(4):
            lem = lep.get(ol - 2, 0)

            def _t2(te, ol=ol, wlm=wlm, v1m=v1m, lem=lem):
                te.wait_ge(s_wl.h, wlm)
                te.wait_ge(s_e.h, max(v1m, lem))
                last = mm_acc(te, ps[:, 2048 + (ol % 2) * 512:2048 + (ol % 2) * 512 + TC],
                              [(slabs[0][:, c * 512 + ol * 128:c * 512 + ol * 128 + 128],
                                nxl[:, c * TC:(c + 1) * TC]) for c in range(NDC)])
                k.inc(last, s_t)
            k.t(_t2)
            t2m = s_t.n

            def _s2(sc, ol=ol, t2m=t2m):
                sc.wait_ge(s_t.h, t2m)
                k.inc(sc.activation(out=hl[:, ol * TC:(ol + 1) * TC],
                                    in_=ps[:, 2048 + (ol % 2) * 512:2048 + (ol % 2) * 512 + TC],
                                    func=AF.Gelu, bias=b1l[:, ol:ol + 1], scale=1.0), s_e)
            k.s(_s2)
            lep[ol] = s_e.n

        em2 = s_e.n
        oep = {}
        for o2 in range(NDC):
            oem = oep.get(o2 - 2, 0)

            def _t3(te, o2=o2, em2=em2, oem=oem):
                te.wait_ge(s_e.h, max(em2, oem))
                last = mm_acc(te, ps[:, 3072 + (o2 % 2) * 512:3072 + (o2 % 2) * 512 + TC],
                              [(slabs[1][:, c * 1024 + o2 * 128:c * 1024 + o2 * 128 + 128],
                                hl[:, c * TC:(c + 1) * TC]) for c in range(4)])
                k.inc(last, s_t)
            k.t(_t3)
            t3m = s_t.n

            def _v3(v, o2=o2, t3m=t3m):
                v.wait_ge(s_t.h, t3m)
                k.inc(v.tensor_scalar(out=lout[:, o2 * TC:(o2 + 1) * TC],
                                      in0=ps[:, 3072 + (o2 % 2) * 512:3072 + (o2 % 2) * 512 + TC],
                                      scalar1=b2l[:, o2:o2 + 1], scalar2=None,
                                      op0=OP.add), s_e)
            k.v(_v3)
            oep[o2] = s_e.n

        for o in range(NHC):
            par = o % 2
            em3 = s_e.n

            def _t4(te, o=o, par=par, em3=em3, v1m=v1m):
                te.wait_ge(s_w0.h, w0m)
                te.wait_ge(s_e.h, v1m if o == 0 else em3)
                last = mm_acc(te, ps[:, par * 512:par * 512 + TC],
                              [(w1h_sb[:, c * 4096 + o * 128:c * 4096 + o * 128 + 128],
                                nxh[:, c * TC:(c + 1) * TC]) for c in range(NDC)])
                k.inc(last, s_t)
            k.t(_t4)
            t4m = s_t.n

            def _s4(sc, o=o, par=par, t4m=t4m):
                sc.wait_ge(s_t.h, t4m)
                k.inc(sc.activation(out=hbuf[:, o * TC:(o + 1) * TC],
                                    in_=ps[:, par * 512:par * 512 + TC],
                                    func=AF.Gelu, bias=b1h[:, o:o + 1], scale=1.0), s_e)
            k.s(_s4)

        em4 = s_e.n
        hep = {}
        for o2 in range(NDC):
            hem = hep.get(o2 - 2, 0)
            sw2 = s_w[o2 % 2]

            def _gw2b(gq, o2=o2, sw2=sw2):
                gq.wait_ge(s_t.h, s_t.n)
                k.dinc(gq.dma_start(
                    out=slabs[o2 % 2][:, 0:4096].rearrange("p (c o) -> p c o", o=128),
                    in_=w2h[:, :, o2 * 128:o2 * 128 + 128]), sw2)
            k.g(_gw2b)
            w2m = (sw2, sw2.n)

            def _t5(te, o2=o2, w2m=w2m, em4=em4, hem=hem):
                te.wait_ge(w2m[0].h, w2m[1])
                te.wait_ge(s_e.h, max(em4, hem))
                last = mm_acc(te, ps[:, 2048 + (o2 % 2) * 512:2048 + (o2 % 2) * 512 + TC],
                              [(slabs[o2 % 2][:, c * 128:c * 128 + 128],
                                hbuf[:, c * TC:(c + 1) * TC]) for c in range(NHC)])
                k.inc(last, s_t)
            k.t(_t5)
            t5m = s_t.n

            pm_prev = s_p.n

            def _v5(v, o2=o2, t5m=t5m, pm_prev=pm_prev):
                v.wait_ge(s_t.h, t5m)
                if pm_prev:
                    v.wait_ge(s_p.h, pm_prev)
                k.inc(v.scalar_tensor_tensor(out=htmp[:, 0:TC],
                                             in0=ps[:, 2048 + (o2 % 2) * 512:2048 + (o2 % 2) * 512 + TC],
                                             scalar=b2h[:, o2:o2 + 1], in1=mskb[:, 0:TC],
                                             op0=OP.add, op1=OP.mult), s_e)
            k.v(_v5)
            v5m = s_e.n

            def _p5(g, o2=o2, v5m=v5m):
                g.wait_ge(s_e.h, v5m)
                k.inc(g.tensor_tensor(out=fout[:, o2 * TC:(o2 + 1) * TC],
                                      in0=lout[:, o2 * TC:(o2 + 1) * TC],
                                      in1=htmp[:, 0:TC], op=OP.add), s_p)
            k.g(_p5)
            p5m = s_p.n
            hep[o2] = s_e.n

            def _g5(gq, o2=o2, p5m=p5m, t0=t0):
                gq.wait_ge(s_p.h, p5m)
                k.dinc(gq.dma_start(out=outv[:, o2, t0:t0 + TC],
                                    in_=fout[:, o2 * TC:(o2 + 1) * TC]), s_o)
            k.g(_g5)
    return s_o


def l2_ffn_host(k, inputs):
    k.weight("L2f_rt", bimg(np.asarray(inputs["L2_ffn_routing_token"])))
    k.weight("L2f_gl", bimg(np.asarray(inputs["L2_ffn_gamma_light"])))
    k.weight("L2f_gh", bimg(np.asarray(inputs["L2_ffn_gamma_heavy"])))
    k.weight("L2f_w1lT", wimg_lhsT(np.asarray(inputs["L2_ffn_w1_light"])))
    k.weight("L2f_b1l", bimg(np.asarray(inputs["L2_ffn_b1_light"])))
    k.weight("L2f_w2lT", wimg_lhsT(np.asarray(inputs["L2_ffn_w2_light"])))
    k.weight("L2f_b2l", bimg(np.asarray(inputs["L2_ffn_b2_light"])))
    k.weight("L2f_w1hT", wimg_lhsT(np.asarray(inputs["L2_ffn_w1_heavy"])))
    k.weight("L2f_b1h", bimg(np.asarray(inputs["L2_ffn_b1_heavy"])))
    k.weight("L2f_w2hT", wimg_lhsT(np.asarray(inputs["L2_ffn_w2_heavy"])))
    k.weight("L2f_b2h", bimg(np.asarray(inputs["L2_ffn_b2_heavy"])))


L2F_SMALL_W = [("L2f_rt", F32), ("L2f_gl", F32), ("L2f_gh", F32),
               ("L2f_b1l", F32), ("L2f_b2l", F32), ("L2f_b1h", F32),
               ("L2f_b2h", F32)]



def emit_l1_attn(k, xn_hbm, out_hbm, tag="l1a"):
    T = k.T
    N = 2048
    nrows = T // N
    TC = 512

    xnr = bfv(k, "ARC", 0, 16384)
    qf = bfv(k, "ARA", 0, 16384)
    ektm = bfv(k, "ARA", 16384, 32768)
    vtm = bfv(k, "ARB", 0, 16384)
    aof = bfv(k, "ARB", 16384, 32768)
    gatef = bfv(k, "ARA", 16384, 32768)
    aon = bfv(k, "ARA", 0, 16384)
    ghsb = bfv(k, "ARD", 0, 2048)
    kvsb = bfv(k, "ARD", 2048, 2560)
    sqp0 = bfv(k, "ARD", 3072, 3584)
    sqp1 = bfv(k, "ARD", 3584, 4096)
    mx8 = f32v(k, "ARD", 2048, 2056)
    sm8 = f32v(k, "ARD", 2056, 2064)
    rc8 = f32v(k, "ARD", 2064, 2072)
    mxrow = f32v(k, "ARD", 2072, 3096)[0:1, :]
    bvrow = k.sb["ARD"][0:1, 6192:7216]
    stgo = [f32v(k, "ARE", 0, 512), f32v(k, "ARE", 512, 1024)]
    lnr = f32v(k, "ARE", 1024, 2048)
    ps = k.ps_arena
    ones_row = k.sb["ones_row_f32"]
    ones_col = k.sb["ones_col_f32"]
    onesrow_bf = k.sb["sb_ones_row_bf"]
    slabs = [k.sb["slab0"], k.sb["slab1"]]

    s_x = k.sem(f"{tag}x")
    s_w = [k.sem(f"{tag}w0"), k.sem(f"{tag}w1")]
    s_b = k.sem(f"{tag}b")
    s_t = k.sem("C_t")
    s_e = k.sem("C_e")
    s_p = k.sem("C_p")
    s_o = [k.sem(f"{tag}o0"), k.sem(f"{tag}o1")]

    wq = k.dram["L1a_wqT"].rearrange("p (c o) -> p c o", c=NDC)
    wk = k.dram["L1a_wkT"].rearrange("p (c o) -> p c o", c=NDC)
    wo = k.dram["L1a_woT"].rearrange("p (c o) -> p c o", c=NDC)
    bq = k.sb["sb_L1a_bq"]
    bg1 = k.sb["sb_L1a_bg1"]
    bg2 = k.sb["sb_L1a_bg2"]
    bo = k.sb["sb_L1a_bo"]
    lng = k.sb["sb_L1a_lng"]
    lnb = k.sb["sb_L1a_lnb"]
    wg1 = k.sb["sb_L1a_wg1T"]
    wg2 = k.sb["sb_L1a_wg2T"]
    xnv = xn_hbm.rearrange("p (c t) -> p c t", c=NDC)
    outv = out_hbm.rearrange("(c p) t -> p c t", p=128)

    def _g0(gq):
        k.dinc(gq.dma_start(out=bvrow[:], in_=k.dram["L1a_bv_row"][:, :]), s_b)
    k.g(_g0)

    for r in range(nrows):
        kv_em = {}
        out_em = {}
        gate_em = {}
        ln_tm = {}
        nm_em = {}
        wo_ems = {}
        R = r * N
        tprev = s_t.n

        def _gx(gq, R=R, tprev=tprev):
            if tprev:
                gq.wait_ge(s_t.h, tprev)
            k.dinc(gq.dma_start(out=xnr.rearrange("p (c t) -> p c t", c=NDC)[:, :, 0:N],
                                in_=xnv[:, :, R:R + N]), s_x)
        k.g(_gx)
        xm = s_x.n

        for mi, wd in [(0, wq), (1, wk)]:
            for o in range(NDC):
                par = (mi * NDC + o) % 2
                region = par * 2048
                slab = slabs[par]
                sw = s_w[par]
                em = s_e.n

                def _gw(gq, wd=wd, o=o, slab=slab, sw=sw):
                    gq.wait_ge(s_t.h, s_t.n)
                    k.dinc(gq.dma_start(
                        out=slab[:, 0:1024].rearrange("p (c o) -> p c o", o=128),
                        in_=wd[:, :, o * 128:o * 128 + 128]), sw)
                k.g(_gw)
                wm = (sw, sw.n)

                def _t1(te, o=o, slab=slab, wm=wm, region=region, xm=xm, em=em):
                    te.wait_ge(wm[0].h, wm[1])
                    te.wait_ge(s_x.h, xm)
                    if em:
                        te.wait_ge(s_e.h, em)
                    for nb in range(4):
                        last = mm_acc(te, ps[:, region + nb * 512:region + nb * 512 + 512],
                                      [(slab[:, c * 128:c * 128 + 128],
                                        xnr[:, c * N + nb * 512:c * N + nb * 512 + 512])
                                       for c in range(NDC)])
                    k.inc(last, s_t)
                k.t(_t1)
                tm = s_t.n

                if mi == 0:
                    def _s1(sc, o=o, region=region, tm=tm):
                        sc.wait_ge(s_t.h, tm)
                        k.inc(sc.activation(out=qf[:, o * N:o * N + N],
                                            in_=ps[:, region:region + 2048], func=AF.Silu,
                                            bias=bq[:, o:o + 1], scale=1.0), s_e)
                    k.s(_s1)
                else:
                    def _v1(v, o=o, region=region, tm=tm):
                        v.wait_ge(s_t.h, tm)
                        k.inc(v.reduce_max(out=mx8[:, o:o + 1],
                                           in_=ps[:, region:region + 2048],
                                           axis=AX.X, negate=True), s_e)
                    k.v(_v1)
                    vm1 = s_e.n

                    def _s2(sc, o=o, region=region, vm1=vm1):
                        sc.wait_ge(s_e.h, vm1)
                        k.inc(sc.activation(out=vtm[:, 0:2048], in_=ps[:, region:region + 2048],
                                            func=AF.Exp, bias=mx8[:, o:o + 1], scale=1.0,
                                            accum_out=sm8[:, o:o + 1]), s_e)
                    k.s(_s2)

        sm_all = s_e.n

        def _s3(sc, sm_all=sm_all):
            sc.wait_ge(s_e.h, sm_all)
            act_chain(k, sc, s_e, sc.activation(out=rc8[:], in_=sm8[:], func=AF.Ln))
            k.inc(sc.activation(out=rc8[:], in_=rc8[:], func=AF.Exp,
                                bias=0.0, scale=-1.0), s_e)
        k.s(_s3)
        sm3 = s_e.n

        def _tb(te, sm3=sm3):
            te.wait_ge(s_e.h, sm3)
            k.inc(te.transpose(ps[0:8, 3840:3840 + 128], mx8[:], k.sb["sb_ident_f32c"][:]), s_t)
        k.t(_tb)
        tbm = s_t.n

        def _vb(v, tbm=tbm):
            v.wait_ge(s_t.h, tbm)
            k.inc(v.tensor_copy(out=stgo[0][0:8, 0:128], in_=ps[0:8, 3840:3840 + 128]), s_e)
        k.v(_vb)
        vbm = s_e.n

        def _gb(gq, vbm=vbm):
            gq.wait_ge(s_e.h, vbm)
            k.dinc(gq.dma_start(out=k.dram["mxrow_s"][0:1, 0:1024]
                                .rearrange("o (c p) -> c (o p)", p=128),
                                in_=stgo[0][0:8, 0:128]), s_b)
        k.g(_gb)
        bm1 = s_b.n

        def _gb2(gq, bm1=bm1):
            gq.wait_ge(s_b.h, bm1)
            k.dinc(gq.dma_start(out=mxrow, in_=k.dram["mxrow_s"][0:1, 0:1024]), s_b)
        k.g(_gb2)
        bm2 = s_b.n

        for mi in range(2):
            wname = "L1a_wvT" if mi == 0 else "L1a_wkT"
            dst = vtm if mi == 0 else ektm
            brow = bvrow if mi == 0 else mxrow
            sw = s_w[mi % 2]
            tsl = s_t.n

            def _gws(gq, wname=wname, sw=sw, tsl=tsl):
                if tsl:
                    gq.wait_ge(s_t.h, tsl)
                k.dinc(gq.dma_start(out=slabs[0][:], in_=k.dram[wname][:, 0:4096]), sw)
                k.dinc(gq.dma_start(out=slabs[1][:], in_=k.dram[wname][:, 4096:8192]), sw)
            k.g(_gws)
            wm = (sw, sw.n)

            for m in range(16):
                par = m % 2
                region = par * 2048
                em = s_e.n

                def _t2(te, m=m, region=region, em=em, xm=xm, wm=wm, bm2=bm2, mi=mi):
                    te.wait_ge(wm[0].h, wm[1])
                    te.wait_ge(s_x.h, xm)
                    if mi == 1:
                        te.wait_ge(s_b.h, bm2)
                    te.wait_ge(s_e.h, em)
                    for nb in range(2):
                        pairs = []
                        for c in range(NDC):
                            wsl = slabs[c // 4][:, (c % 4) * 1024 + nb * 512:
                                               (c % 4) * 1024 + nb * 512 + 512]
                            pairs.append((xnr[:, c * N + m * 128:c * N + m * 128 + 128], wsl))
                        if mi == 0:
                            pairs.append((onesrow_bf[0:1, 0:128],
                                          brow[0:1, nb * 512:nb * 512 + 512]))
                        else:
                            pairs.append((ones_row[0:1, 0:128],
                                          brow[0:1, nb * 512:nb * 512 + 512]))
                        last = mm_acc(te, ps[:, region + nb * 512:region + nb * 512 + 512], pairs)
                    k.inc(last, s_t)
                k.t(_t2)
                tm2 = s_t.n

                if mi == 0:
                    def _s4(sc, m=m, region=region, tm2=tm2):
                        sc.wait_ge(s_t.h, tm2)
                        k.inc(sc.activation(out=vtm[:, m * 1024:(m + 1) * 1024],
                                            in_=ps[:, region:region + 1024],
                                            func=AF.Copy), s_e)
                    k.s(_s4)
                else:
                    def _s4(sc, m=m, region=region, tm2=tm2):
                        sc.wait_ge(s_t.h, tm2)
                        k.inc(sc.activation(out=ektm[:, m * 1024:(m + 1) * 1024],
                                            in_=ps[:, region:region + 1024],
                                            func=AF.Exp), s_e)
                    k.s(_s4)

        emc = s_e.n
        for h in range(16):
            reg = 3584 + (h % 2) * 128
            pb = (h % 2) * 64

            hwm = emc if h < 2 else kv_em[h - 2]

            def _t3(te, h=h, reg=reg, hwm=hwm, pb=pb):
                te.wait_ge(s_e.h, hwm)
                last = mm_acc(te, ps[pb:pb + 64, reg:reg + 64],
                              [(ektm[:, m * 1024 + h * 64:m * 1024 + h * 64 + 64],
                                vtm[:, m * 1024 + h * 64:m * 1024 + h * 64 + 64])
                               for m in range(16)])
                k.inc(last, s_t)
            k.t(_t3)
            tm3 = s_t.n

            def _v3(v, h=h, reg=reg, tm3=tm3, pb=pb):
                v.wait_ge(s_t.h, tm3)
                k.inc(v.tensor_scalar(out=kvsb[pb:pb + 64,
                                               (h // 2) * 64:(h // 2) * 64 + 64],
                                      in0=ps[pb:pb + 64, reg:reg + 64],
                                      scalar1=rc8[(h % 2) * 64:(h % 2) * 64 + 64,
                                                  h // 2:h // 2 + 1],
                                      scalar2=None, op0=OP.mult), s_e)
            k.v(_v3)
            kv_em[h] = s_e.n

        emd = s_e.n
        for h in range(16):
            for nb in range(4):
                reg = 2560 + ((h * 4 + nb) % 2) * 512
                em = out_em.get((h * 4 + nb) - 2, 0)

                pb = (h % 2) * 64

                def _t4(te, h=h, nb=nb, reg=reg, emd=emd, em=em, pb=pb):
                    te.wait_ge(s_e.h, max(emd, em))
                    k.inc(te.matmul(ps[pb:pb + 64, reg:reg + 512],
                                    kvsb[pb:pb + 64,
                                         (h // 2) * 64:(h // 2) * 64 + 64],
                                    qf[pb:pb + 64,
                                       (h // 2) * N + nb * 512:(h // 2) * N + nb * 512 + 512],
                                    start=True, stop=True), s_t)
                k.t(_t4)
                tm4 = s_t.n

                def _s5(sc, h=h, nb=nb, reg=reg, tm4=tm4, pb=pb):
                    sc.wait_ge(s_t.h, tm4)
                    k.inc(sc.activation(out=aof[pb:pb + 64,
                                                (h // 2) * N + nb * 512:(h // 2) * N + nb * 512 + 512],
                                        in_=ps[pb:pb + 64, reg:reg + 512], func=AF.Copy), s_e)
                k.s(_s5)
                out_em[h * 4 + nb] = s_e.n

        eme = s_e.n

        def _t5(te, eme=eme, xm=xm):
            te.wait_ge(s_e.h, eme)
            te.wait_ge(s_x.h, xm)
            for nb in range(4):
                last = mm_acc(te, ps[0:64, nb * 512:nb * 512 + 512],
                              [(wg1[:, c * 64:(c + 1) * 64],
                                xnr[:, c * N + nb * 512:c * N + nb * 512 + 512])
                               for c in range(NDC)])
            k.inc(last, s_t)
        k.t(_t5)
        tm5 = s_t.n

        def _s6(sc, tm5=tm5):
            sc.wait_ge(s_t.h, tm5)
            k.inc(sc.activation(out=ghsb[0:64, 0:N], in_=ps[0:64, 0:2048],
                                func=AF.Identity, bias=bg1[0:64, 0:1], scale=1.0), s_e)
        k.s(_s6)
        sm6 = s_e.n

        for o in range(NDC):
            reg2 = (o % 2) * 2048
            egm = gate_em.get(o - 2, 0)

            def _t6(te, o=o, reg2=reg2, sm6=sm6, egm=egm):
                te.wait_ge(s_e.h, max(sm6, egm))
                for nb in range(4):
                    last = te.matmul(ps[:, reg2 + nb * 512:reg2 + nb * 512 + 512],
                                     wg2[0:64, o * 128:o * 128 + 128],
                                     ghsb[0:64, nb * 512:nb * 512 + 512],
                                     start=True, stop=True)
                k.inc(last, s_t)
            k.t(_t6)
            tm6 = s_t.n

            def _s7(sc, o=o, reg2=reg2, tm6=tm6):
                sc.wait_ge(s_t.h, tm6)
                k.inc(sc.activation(out=gatef[:, o * N:o * N + N],
                                    in_=ps[:, reg2:reg2 + 2048], func=AF.Sigmoid,
                                    bias=bg2[:, o:o + 1], scale=1.0), s_e)
            k.s(_s7)
            gate_em[o] = s_e.n

        for i in range(4):
            t0 = i * TC
            emf = s_e.n
            sqp = [sqp0, sqp1]
            for c in range(NDC):
                vm5 = emf if c == 0 else s_e.n
                sqo = sqp[c % 2]
                tlp = ln_tm.get(c - 2, 0)
                plp = nm_em.get(c - 2 if c >= 2 else c + 6, 0)

                def _s8(sc, c=c, t0=t0, vm5=vm5, sqo=sqo, tlp=tlp, plp=plp):
                    sc.wait_ge(s_e.h, vm5)
                    if tlp:
                        sc.wait_ge(s_t.h, tlp)
                    if plp:
                        sc.wait_ge(s_p.h, plp)
                    k.inc(sc.activation(out=sqo, in_=aof[:, c * N + t0:c * N + t0 + TC],
                                        func=AF.Square), s_e)
                k.s(_s8)
                sm8_ = s_e.n

                def _t7(te, c=c, t0=t0, sm8_=sm8_, sqo=sqo):
                    te.wait_ge(s_e.h, sm8_)
                    te.matmul(ps[0:1, 1024:1024 + TC], k.sb["sb_ones_col_bfc"][:],
                              aof[:, c * N + t0:c * N + t0 + TC],
                              start=(c == 0), stop=(c == NDC - 1))
                    k.inc(te.matmul(ps[0:1, 1536:1536 + TC], k.sb["sb_ones_col_bfc"][:],
                                    sqo, start=(c == 0), stop=(c == NDC - 1)), s_t)
                k.t(_t7)
                ln_tm[c] = s_t.n
            tm7 = s_t.n
            mu_r = lnr[0:1, 0:512]
            va_r = k.sb["ARE"].bitcast(F32)[0:1, 1536:2048]

            def _v6(v, tm7=tm7, mu_r=mu_r, va_r=va_r):
                v.wait_ge(s_t.h, tm7)
                v.tensor_scalar(out=mu_r, in0=ps[0:1, 1024:1024 + TC],
                                scalar1=1.0 / D, scalar2=None, op0=OP.mult)
                k.inc(v.tensor_scalar(out=va_r, in0=ps[0:1, 1536:1536 + TC],
                                      scalar1=1.0 / D, scalar2=None, op0=OP.mult), s_e)
                v.wait_ge(s_e.h, s_e.n)
                k.inc(v.tensor_tensor(out=stgo[0][0:1, 0:TC], in0=mu_r, in1=mu_r,
                                      op=OP.mult), s_e)
                v.wait_ge(s_e.h, s_e.n)
                k.inc(v.tensor_tensor(out=va_r, in0=va_r,
                                      in1=stgo[0][0:1, 0:TC], op=OP.subtract), s_e)
                v.wait_ge(s_e.h, s_e.n)
                k.inc(v.tensor_scalar(out=va_r, in0=va_r, scalar1=1e-5, scalar2=None,
                                      op0=OP.add), s_e)
            k.v(_v6)
            vm6 = s_e.n

            def _s9(sc, vm6=vm6, va_r=va_r):
                sc.wait_ge(s_e.h, vm6)
                act_chain(k, sc, s_e, sc.activation(out=va_r, in_=va_r, func=AF.Ln))
                k.inc(sc.activation(out=va_r, in_=va_r, func=AF.Exp, bias=0.0, scale=-0.5), s_e)
            k.s(_s9)
            sm9 = s_e.n

            def _v7(v, sm9=sm9, mu_r=mu_r, va_r=va_r):
                v.wait_ge(s_e.h, sm9)
                k.inc(v.scalar_tensor_tensor(out=mu_r, in0=mu_r, scalar=-1.0,
                                             in1=va_r, op0=OP.mult, op1=OP.mult), s_e)
            k.v(_v7)
            vm7 = s_e.n

            def _t8(te, vm7=vm7, mu_r=mu_r, va_r=va_r):
                te.wait_ge(s_e.h, vm7)
                te.matmul(ps[:, 1024:1024 + TC], ones_row[:], va_r, start=True, stop=True)
                k.inc(te.matmul(ps[:, 1536:1536 + TC], ones_row[:], mu_r,
                                start=True, stop=True), s_t)
            k.t(_t8)
            tm8 = s_t.n

            for c in range(NDC):
                sqo = sqp[c % 2]
                nem = nm_em.get(c - 2, 0)

                def _v8(v, c=c, t0=t0, tm8=tm8, sqo=sqo, nem=nem):
                    v.wait_ge(s_t.h, tm8)
                    if nem:
                        v.wait_ge(s_p.h, nem)
                    k.inc(v.tensor_tensor(out=sqo, in0=aof[:, c * N + t0:c * N + t0 + TC],
                                          in1=ps[:, 1024:1024 + TC], op=OP.mult), s_e)
                    v.wait_ge(s_e.h, s_e.n)
                    k.inc(v.tensor_tensor(out=sqo, in0=sqo, in1=ps[:, 1536:1536 + TC],
                                          op=OP.add), s_e)
                    v.wait_ge(s_e.h, s_e.n)
                    k.inc(v.tensor_scalar(out=sqo, in0=sqo, scalar1=lng[:, c:c + 1],
                                          scalar2=lnb[:, c:c + 1], op0=OP.mult,
                                          op1=OP.add), s_e)
                k.v(_v8)
                vm8 = s_e.n

                def _p8(g, c=c, t0=t0, vm8=vm8, sqo=sqo):
                    g.wait_ge(s_e.h, vm8)
                    k.inc(g.tensor_tensor(out=aon[:, c * N + t0:c * N + t0 + TC],
                                          in0=sqo, in1=gatef[:, c * N + t0:c * N + t0 + TC],
                                          op=OP.mult), s_p)
                k.g(_p8)
                nm_em[c] = s_p.n

        pmf = s_p.n
        for o in range(NDC):
            par = o % 2
            region = par * 2048
            slab = slabs[par]
            sw = s_w[par]

            def _gw4(gq, o=o, slab=slab, sw=sw):
                gq.wait_ge(s_t.h, s_t.n)
                k.dinc(gq.dma_start(
                    out=slab[:, 0:1024].rearrange("p (c o) -> p c o", o=128),
                    in_=wo[:, :, o * 128:o * 128 + 128]), sw)
            k.g(_gw4)
            wm4 = (sw, sw.n)
            wo_em = wo_ems.get(o - 2, 0)

            def _t9(te, o=o, slab=slab, wm4=wm4, region=region, pmf=pmf, wo_em=wo_em):
                te.wait_ge(wm4[0].h, wm4[1])
                te.wait_ge(s_p.h, pmf)
                if wo_em:
                    te.wait_ge(s_e.h, wo_em)
                for nb in range(4):
                    last = mm_acc(te, ps[:, region + nb * 512:region + nb * 512 + 512],
                                  [(slab[:, c * 128:c * 128 + 128],
                                    aon[:, c * N + nb * 512:c * N + nb * 512 + 512])
                                   for c in range(NDC)])
                k.inc(last, s_t)
            k.t(_t9)
            tm9 = s_t.n

            for nb in range(4):
                so = s_o[nb % 2]
                stb = stgo[nb % 2]
                omk = so.n

                def _s10(sc, o=o, nb=nb, region=region, tm9=tm9, stb=stb, omk=omk, so=so):
                    sc.wait_ge(s_t.h, tm9)
                    if omk:
                        sc.wait_ge(so.h, omk)
                    k.inc(sc.activation(out=stb[:, 0:512],
                                        in_=ps[:, region + nb * 512:region + nb * 512 + 512],
                                        func=AF.Identity, bias=bo[:, o:o + 1], scale=1.0), s_e)
                k.s(_s10)
                sme = s_e.n

                def _g5(gq, o=o, nb=nb, sme=sme, stb=stb, R=R, so=so):
                    gq.wait_ge(s_e.h, sme)
                    k.dinc(gq.dma_start(out=outv[:, o, R + nb * 512:R + nb * 512 + 512],
                                        in_=stb[:, 0:512]), so)
                k.g(_g5)
            wo_ems[o] = s_e.n
    return s_o


def l1_attn_host(k, inputs):
    w = np.asarray(inputs["L1_attn_w_qkv"])
    b = np.asarray(inputs["L1_attn_b_qkv"])
    k.weight("L1a_wqT", wimg_lhsT(w[0:1024].T))
    k.weight("L1a_wkT", wimg_lhsT(w[1024:2048].T))
    k.weight("L1a_wvT", wimg_lhsT(w[2048:3072].T))
    k.weight("L1a_woT", wimg_lhsT(np.asarray(inputs["L1_attn_w_out"]).T))
    k.weight("L1a_bq", bimg(b[0:1024]))
    k.weight("L1a_bv_row", b[2048:3072][None, :])
    k.weight("L1a_bo", bimg(np.asarray(inputs["L1_attn_b_out"])))
    k.weight("L1a_wg1T", wimg_lhsT(np.asarray(inputs["L1_attn_w_g1"]).T))
    k.weight("L1a_wg2T", np.ascontiguousarray(np.asarray(inputs["L1_attn_w_g2"]).T))
    k.weight("L1a_bg1", np.asarray(inputs["L1_attn_b_g1"])[:, None])
    k.weight("L1a_bg2", bimg(np.asarray(inputs["L1_attn_b_g2"])))
    k.weight("L1a_lng", bimg(np.asarray(inputs["L1_attn_ln_g"])))
    k.weight("L1a_lnb", bimg(np.asarray(inputs["L1_attn_ln_b"])))


L1A_SMALL_W = [("L1a_bq", F32), ("L1a_bo", F32), ("L1a_wg2T", BF16),
               ("L1a_bg1", F32), ("L1a_bg2", F32), ("L1a_lng", F32),
               ("L1a_lnb", F32)]



def emit_l1_ffn_attn(k, xn_hbm, x1_hbm, tag="l1f"):
    T = k.T
    N = 2048
    nrows = T // N
    TC = 512

    xnr = bfv(k, "ARC", 0, 16384)
    nx = bfv(k, "ARA", 0, 16384)
    Qf = bfv(k, "ARA", 16384, 32768)
    VcT = bfv(k, "ARB", 0, 4096)
    attw = bfv(k, "ARB", 4096, 12288)
    sqs = bfv(k, "ARB", 12288, 16384)
    o1 = bfv(k, "ARB", 16384, 32768)
    Kcf = bfv(k, "ARD", 0, 4096)
    rrow = f32v(k, "ARD", 2048, 2560)
    frow = f32v(k, "ARD", 3584, 4096)[0:1, :]
    rb = bfv(k, "ARD", 5120, 5632)
    nxt = [bfv(k, "ARD", 5632, 6144), bfv(k, "ARD", 6144, 6656)]
    stg1 = [f32v(k, "ARD", 2560, 3072), f32v(k, "ARD", 3072, 3584)]
    q2row = f32v(k, "ARE", 0, 2048)[0:1, :]
    ps = k.ps_arena
    ones_row = k.sb["ones_row_f32"]
    ones_col = k.sb["ones_col_f32"]
    onescol_bf = k.sb["sb_ones_col_bfc"]
    slabs = [k.sb["slab0"], k.sb["slab1"]]

    s_x = k.sem(f"{tag}x")
    s_w = [k.sem(f"{tag}w0"), k.sem(f"{tag}w1")]
    s_t = k.sem("C_t")
    s_e = k.sem("C_e")
    s_o = [k.sem(f"{tag}o0"), k.sem(f"{tag}o1")]

    wq = k.dram["L1f_wqT"].rearrange("p (c o) -> p c o", c=NDC)
    wkc = k.dram["L1f_wkcT"].rearrange("p (c o) -> p c o", c=NDC)
    wo = k.dram["L1f_woT"].rearrange("p (c o) -> p c o", c=NDC)
    g1 = k.sb["sb_L1f_g1"]
    xnv = xn_hbm.rearrange("p (c t) -> p c t", c=NDC)
    x1v = x1_hbm.rearrange("(c p) t -> p c t", p=128)

    for r in range(nrows):
        R = r * N
        tprev = s_t.n

        def _gx(gq, R=R, tprev=tprev):
            if tprev:
                gq.wait_ge(s_t.h, tprev)
            k.dinc(gq.dma_start(out=xnr.rearrange("p (c t) -> p c t", c=NDC)[:, :, 0:N],
                                in_=xnv[:, :, R:R + N]), s_x)
        k.g(_gx)
        xm = s_x.n

        for i in range(4):
            t0 = i * TC
            em = s_e.n

            def _s0(sc, t0=t0, xm=xm, em=em):
                sc.wait_ge(s_x.h, xm)
                if em:
                    sc.wait_ge(s_e.h, em)
                k.inc(sc.activation(
                    out=sqs.rearrange("p (c t) -> p c t", c=NDC),
                    in_=xnr.rearrange("p (c t) -> p c t", c=NDC)[:, :, t0:t0 + TC],
                    func=AF.Square), s_e)
            k.s(_s0)
            s0m = s_e.n

            def _t0(te, s0m=s0m):
                te.wait_ge(s_e.h, s0m)
                last = mm_acc(te, ps[0:1, 3584:3584 + TC],
                              [(onescol_bf[:], sqs[:, c * TC:(c + 1) * TC])
                               for c in range(NDC)])
                k.inc(last, s_t)
            k.t(_t0)
            t0m = s_t.n

            def _v0(v, t0m=t0m):
                v.wait_ge(s_t.h, t0m)
                k.inc(v.tensor_scalar(out=frow, in0=ps[0:1, 3584:3584 + TC],
                                      scalar1=1.0 / D, scalar2=1e-6,
                                      op0=OP.mult, op1=OP.add), s_e)
            k.v(_v0)
            v0m = s_e.n

            def _s1(sc, v0m=v0m):
                sc.wait_ge(s_e.h, v0m)
                act_chain(k, sc, s_e, sc.activation(out=frow, in_=frow, func=AF.Ln))
                k.inc(sc.activation(out=frow, in_=frow, func=AF.Exp, bias=0.0,
                                    scale=-0.5), s_e)
            k.s(_s1)
            s1m = s_e.n

            def _t1(te, s1m=s1m):
                te.wait_ge(s_e.h, s1m)
                k.inc(te.matmul(ps[:, 3584:3584 + TC], ones_row[:], frow,
                                start=True, stop=True), s_t)
            k.t(_t1)
            t1m = s_t.n

            def _v1(v, t0=t0, t1m=t1m):
                v.wait_ge(s_t.h, t1m)
                for c in range(NDC):
                    k.inc(v.tensor_tensor(out=nxt[c % 2],
                                          in0=xnr[:, c * N + t0:c * N + t0 + TC],
                                          in1=ps[:, 3584:3584 + TC], op=OP.mult), s_e)
                    v.wait_ge(s_e.h, s_e.n)
                    ins = v.tensor_scalar(out=nx[:, c * N + t0:c * N + t0 + TC],
                                          in0=nxt[c % 2], scalar1=g1[:, c:c + 1],
                                          scalar2=None, op0=OP.mult)
                k.inc(ins, s_e)
            k.v(_v1)

        nxm = s_e.n
        for o in range(NDC):
            par = o % 2
            region = par * 2048
            slab = slabs[par]
            sw = s_w[par]
            em = s_e.n

            def _gw(gq, o=o, slab=slab, sw=sw):
                gq.wait_ge(s_t.h, s_t.n)
                k.dinc(gq.dma_start(
                    out=slab[:, 0:1024].rearrange("p (c o) -> p c o", o=128),
                    in_=wq[:, :, o * 128:o * 128 + 128]), sw)
            k.g(_gw)
            wm = (sw, sw.n)

            def _tq(te, o=o, slab=slab, wm=wm, region=region, nxm=nxm, em=em):
                te.wait_ge(wm[0].h, wm[1])
                te.wait_ge(s_e.h, max(nxm, em))
                for nb in range(4):
                    last = mm_acc(te, ps[:, region + nb * 512:region + nb * 512 + 512],
                                  [(slab[:, c * 128:c * 128 + 128],
                                    nx[:, c * N + nb * 512:c * N + nb * 512 + 512])
                                   for c in range(NDC)])
                k.inc(last, s_t)
            k.t(_tq)
            tqm = s_t.n

            def _sq(sc, o=o, region=region, tqm=tqm):
                sc.wait_ge(s_t.h, tqm)
                k.inc(sc.activation(out=Qf[:, o * N:o * N + N],
                                    in_=ps[:, region:region + 2048], func=AF.Copy), s_e)
            k.s(_sq)

        for o in range(2):
            par = o % 2
            region = par * 2048
            slab = slabs[par]
            sw = s_w[par]
            em = s_e.n

            def _gw2(gq, o=o, slab=slab, sw=sw):
                gq.wait_ge(s_t.h, s_t.n)
                k.dinc(gq.dma_start(
                    out=slab[:, 0:1024].rearrange("p (c o) -> p c o", o=128),
                    in_=wkc[:, :, o * 128:o * 128 + 128]), sw)
            k.g(_gw2)
            wm = (sw, sw.n)

            def _tk(te, o=o, slab=slab, wm=wm, region=region, em=em):
                te.wait_ge(wm[0].h, wm[1])
                if em:
                    te.wait_ge(s_e.h, em)
                for nb in range(4):
                    last = mm_acc(te, ps[:, region + nb * 512:region + nb * 512 + 512],
                                  [(slab[:, c * 128:c * 128 + 128],
                                    nx[:, c * N + nb * 512:c * N + nb * 512 + 512])
                                   for c in range(NDC)])
                k.inc(last, s_t)
            k.t(_tk)
            tkm = s_t.n

            def _sk(sc, o=o, region=region, tkm=tkm):
                sc.wait_ge(s_t.h, tkm)
                k.inc(sc.activation(out=Kcf[:, o * N:o * N + N],
                                    in_=ps[:, region:region + 2048], func=AF.Copy), s_e)
            k.s(_sk)

        sw = s_w[0]
        tsl = s_t.n

        def _gwv(gq, sw=sw, tsl=tsl):
            gq.wait_ge(s_t.h, tsl)
            k.dinc(gq.dma_start(out=slabs[0][:, 0:2048], in_=k.dram["L1f_wvcT"][:, :]), sw)
        k.g(_gwv)
        wvm = (sw, sw.n)

        for m in range(16):
            par = m % 2
            region = par * 2048
            em = s_e.n

            def _tv(te, m=m, region=region, em=em, wvm=wvm, nxm=nxm):
                te.wait_ge(wvm[0].h, wvm[1])
                te.wait_ge(s_e.h, max(nxm, em))
                last = mm_acc(te, ps[:, region:region + 256],
                              [(nx[:, c * N + m * 128:c * N + m * 128 + 128],
                                slabs[0][:, c * 256:(c + 1) * 256]) for c in range(NDC)])
                k.inc(last, s_t)
            k.t(_tv)
            tvm = s_t.n

            def _sv(sc, m=m, region=region, tvm=tvm):
                sc.wait_ge(s_t.h, tvm)
                k.inc(sc.activation(out=VcT[:, m * 256:(m + 1) * 256],
                                    in_=ps[:, region:region + 256], func=AF.Copy), s_e)
            k.s(_sv)

        emk = s_e.n

        def _sk2(sc, emk=emk):
            sc.wait_ge(s_e.h, emk)
            k.inc(sc.activation(out=sqs[:, 0:4096], in_=Kcf[:, 0:4096], func=AF.Square), s_e)
        k.s(_sk2)
        sk2m = s_e.n

        def _tk2(te, sk2m=sk2m):
            te.wait_ge(s_e.h, sk2m)
            for nb in range(4):
                last = mm_acc(te, ps[0:1, 2048 + nb * 512:2048 + nb * 512 + 512],
                              [(onescol_bf[:], sqs[:, c * 2048 + nb * 512:c * 2048 + nb * 512 + 512])
                               for c in range(2)])
            k.inc(last, s_t)
        k.t(_tk2)
        tk2m = s_t.n

        def _vk2(v, tk2m=tk2m):
            v.wait_ge(s_t.h, tk2m)
            k.inc(v.reduce_max(out=frow[0:1, 0:1], in_=ps[0:1, 2048:2048 + 2048],
                               axis=AX.X), s_e)
        k.v(_vk2)
        vk2m = s_e.n

        def _sk3(sc, vk2m=vk2m):
            sc.wait_ge(s_e.h, vk2m)
            act_chain(k, sc, s_e, sc.activation(out=frow[0:1, 0:1], in_=frow[0:1, 0:1],
                                                func=AF.Ln))
            k.inc(sc.activation(out=frow[0:1, 0:1], in_=frow[0:1, 0:1], func=AF.Exp,
                                bias=0.0, scale=0.5), s_e)
        k.s(_sk3)
        sk3m = s_e.n

        for h in range(4):
            emh = s_e.n

            def _sq2(sc, h=h, emh=emh):
                sc.wait_ge(s_e.h, emh)
                k.inc(sc.activation(out=sqs[:, 0:4096],
                                    in_=Qf[:, 2 * h * N:2 * h * N + 4096],
                                    func=AF.Square), s_e)
            k.s(_sq2)
            sq2m = s_e.n

            def _tq2(te, sq2m=sq2m):
                te.wait_ge(s_e.h, sq2m)
                for nb in range(4):
                    last = mm_acc(te, ps[0:1, 2048 + nb * 512:2048 + nb * 512 + 512],
                                  [(onescol_bf[:], sqs[:, c * 2048 + nb * 512:c * 2048 + nb * 512 + 512])
                                   for c in range(2)])
                k.inc(last, s_t)
            k.t(_tq2)
            tq2m = s_t.n

            def _vq2(v, tq2m=tq2m):
                v.wait_ge(s_t.h, tq2m)
                k.inc(v.tensor_copy(out=q2row, in_=ps[0:1, 2048:2048 + 2048]), s_e)
            k.v(_vq2)
            vq2m = s_e.n

            def _sq3(sc, vq2m=vq2m):
                sc.wait_ge(s_e.h, vq2m)
                act_chain(k, sc, s_e, sc.activation(out=q2row, in_=q2row, func=AF.Ln,
                                                    bias=1e-12, scale=1.0))
                k.inc(sc.activation(out=q2row, in_=q2row, func=AF.Exp, bias=0.0,
                                    scale=0.5), s_e)
            k.s(_sq3)
            sq3m = s_e.n

            def _vq3(v, sq3m=sq3m):
                v.wait_ge(s_e.h, sq3m)
                k.inc(v.tensor_scalar(out=q2row, in0=q2row, scalar1=frow[0:1, 0:1],
                                      scalar2=-1.0, op0=OP.mult, op1=OP.mult), s_e)
            k.v(_vq3)
            mqm = s_e.n

            for sc_i in range(4):
                s0 = sc_i * TC
                ems = s_e.n
                att_em = {}
                for st in range(16):
                    reg = (st % 2) * 512
                    emt = att_em.get(st - 2, ems)

                    def _ts(te, h=h, s0=s0, st=st, reg=reg, emt=emt, mqm=mqm):
                        te.wait_ge(s_e.h, max(emt, mqm))
                        last = None
                        for dc in range(2):
                            last = te.matmul(ps[:, reg:reg + 512],
                                             Kcf[:, dc * N + st * 128:dc * N + st * 128 + 128],
                                             Qf[:, (2 * h + dc) * N + s0:(2 * h + dc) * N + s0 + 512],
                                             start=(dc == 0), stop=False)
                        last = te.matmul(ps[:, reg:reg + 512], ones_row[:],
                                         q2row[0:1, s0:s0 + 512], start=False, stop=True)
                        k.inc(last, s_t)
                    k.t(_ts)
                    tsm = s_t.n

                    def _se(sc, st=st, reg=reg, tsm=tsm):
                        sc.wait_ge(s_t.h, tsm)
                        k.inc(sc.activation(out=attw[:, st * 512:st * 512 + 512],
                                            in_=ps[:, reg:reg + 512], func=AF.Exp,
                                            bias=0.0, scale=0.0625), s_e)
                    k.s(_se)
                    att_em[st] = s_e.n
                ema = s_e.n

                def _tsum(te, ema=ema):
                    te.wait_ge(s_e.h, ema)
                    last = mm_acc(te, ps[0:1, 1024:1024 + 512],
                                  [(onescol_bf[:], attw[:, st * 512:st * 512 + 512])
                                   for st in range(16)])
                    k.inc(last, s_t)
                k.t(_tsum)
                tsumm = s_t.n

                def _vsum(v, tsumm=tsumm):
                    v.wait_ge(s_t.h, tsumm)
                    k.inc(v.tensor_copy(out=rrow[0:1, 0:512],
                                        in_=ps[0:1, 1024:1024 + 512]), s_e)
                k.v(_vsum)
                vsumm = s_e.n

                def _ssum(sc, vsumm=vsumm):
                    sc.wait_ge(s_e.h, vsumm)
                    act_chain(k, sc, s_e, sc.activation(out=rrow[0:1, 0:512],
                                                        in_=rrow[0:1, 0:512], func=AF.Ln,
                                                        bias=1e-38, scale=1.0))
                    k.inc(sc.activation(out=rrow[0:1, 0:512], in_=rrow[0:1, 0:512],
                                        func=AF.Exp, bias=0.0, scale=-1.0), s_e)
                k.s(_ssum)
                ssumm = s_e.n

                def _trb(te, ssumm=ssumm):
                    te.wait_ge(s_e.h, ssumm)
                    k.inc(te.matmul(ps[:, 1536:1536 + 512], ones_row[:], rrow[0:1, 0:512],
                                    start=True, stop=True), s_t)
                k.t(_trb)
                trbm = s_t.n

                def _vrb(v, trbm=trbm):
                    v.wait_ge(s_t.h, trbm)
                    k.inc(v.tensor_copy(out=rb[:, 0:512], in_=ps[:, 1536:1536 + 512]), s_e)
                k.v(_vrb)
                vrbm = s_e.n

                for dc in range(2):
                    reg = 2048 + dc * 512

                    def _tv2(te, h=h, s0=s0, dc=dc, reg=reg, ema=ema, vrbm=vrbm):
                        te.wait_ge(s_e.h, max(ema, vrbm))
                        last = mm_acc(te, ps[:, reg:reg + 512],
                                      [(VcT[:, m * 256 + dc * 128:m * 256 + dc * 128 + 128],
                                        attw[:, m * 512:m * 512 + 512]) for m in range(16)])
                        k.inc(last, s_t)
                    k.t(_tv2)
                    tv2m = s_t.n

                    def _vo(v, h=h, s0=s0, dc=dc, reg=reg, tv2m=tv2m):
                        v.wait_ge(s_t.h, tv2m)
                        k.inc(v.tensor_tensor(out=o1[:, (2 * h + dc) * N + s0:(2 * h + dc) * N + s0 + 512],
                                              in0=ps[:, reg:reg + 512], in1=rb[:, 0:512],
                                              op=OP.mult), s_e)
                    k.v(_vo)

        for o in range(NDC):
            par = o % 2
            region = par * 2048
            slab = slabs[par]
            sw = s_w[par]
            emo = s_e.n

            def _gw3(gq, o=o, slab=slab, sw=sw):
                gq.wait_ge(s_t.h, s_t.n)
                k.dinc(gq.dma_start(
                    out=slab[:, 0:1024].rearrange("p (c o) -> p c o", o=128),
                    in_=wo[:, :, o * 128:o * 128 + 128]), sw)
            k.g(_gw3)
            wm = (sw, sw.n)

            def _to(te, o=o, slab=slab, wm=wm, region=region, emo=emo):
                te.wait_ge(wm[0].h, wm[1])
                te.wait_ge(s_e.h, emo)
                for nb in range(4):
                    last = mm_acc(te, ps[:, region + nb * 512:region + nb * 512 + 512],
                                  [(slab[:, c * 128:c * 128 + 128],
                                    o1[:, c * N + nb * 512:c * N + nb * 512 + 512])
                                   for c in range(NDC)])
                k.inc(last, s_t)
            k.t(_to)
            tom = s_t.n

            for nb in range(4):
                so = s_o[nb % 2]
                stb = stg1[nb % 2]
                omk = so.n

                def _vx(v, o=o, nb=nb, region=region, tom=tom, stb=stb, omk=omk, so=so):
                    v.wait_ge(s_t.h, tom)
                    if omk:
                        v.wait_ge(so.h, omk)
                    k.inc(v.tensor_tensor(out=stb[:, 0:512],
                                          in0=ps[:, region + nb * 512:region + nb * 512 + 512],
                                          in1=xnr[:, o * N + nb * 512:o * N + nb * 512 + 512],
                                          op=OP.add), s_e)
                k.v(_vx)
                vxm = s_e.n

                def _go(gq, o=o, nb=nb, vxm=vxm, stb=stb, R=R, so=so):
                    gq.wait_ge(s_e.h, vxm)
                    k.dinc(gq.dma_start(out=x1v[:, o, R + nb * 512:R + nb * 512 + 512],
                                        in_=stb[:, 0:512]), so)
                k.g(_go)
    return s_o


def l1_ffn_host(k, inputs):
    k.weight("L1f_wqT", wimg_lhsT(np.asarray(inputs["L1_ffn_Wq"]).T))
    k.weight("L1f_wkcT", wimg_lhsT(np.asarray(inputs["L1_ffn_Wkc"]).T))
    k.weight("L1f_wvcT", wimg_lhsT(np.asarray(inputs["L1_ffn_Wvc"]).T))
    k.weight("L1f_woT", wimg_lhsT(np.asarray(inputs["L1_ffn_Wo"]).T))
    k.weight("L1f_g1", bimg(np.asarray(inputs["L1_ffn_g1"])))
    k.weight("L1f_g2", bimg(np.asarray(inputs["L1_ffn_g2"])))
    k.weight("L1f_wrT", wimg_lhsT(np.asarray(inputs["L1_ffn_Wr"]).T))
    k.weight("L1f_eb8", np.asarray(inputs["L1_ffn_expert_bias"])[:, None])
    k.weight("L1f_sW1T", wimg_lhsT(np.asarray(inputs["L1_ffn_sW1"]).T))
    k.weight("L1f_sW2T", wimg_lhsT(np.asarray(inputs["L1_ffn_sW2"]).T))
    k.weight("L1f_sW3T", wimg_lhsT(np.asarray(inputs["L1_ffn_sW3"]).T))
    for e in range(8):
        k.weight(f"L1f_eW1T_{e}", wimg_lhsT(np.asarray(inputs["L1_ffn_eW1"])[e].T))
        k.weight(f"L1f_eW2T_{e}", wimg_lhsT(np.asarray(inputs["L1_ffn_eW2"])[e].T))
        k.weight(f"L1f_eW3T_{e}", wimg_lhsT(np.asarray(inputs["L1_ffn_eW3"])[e].T))


L1F_SMALL_W = [("L1f_g1", F32), ("L1f_g2", F32), ("L1f_wrT", F32), ("L1f_eb8", F32)]



def emit_l1_ffn_moe(k, x1_hbm, out_hbm, tag="l1m"):
    T = k.T
    TC = 512
    NSC = 2048
    nsc = T // NSC

    ps = k.ps_arena
    ones_row = k.sb["ones_row_f32"]
    ones_col = k.sb["ones_col_f32"]
    idf = k.sb["sb_ident_f32c"]
    rev8 = k.sb["sb_rev8"]
    g2 = k.sb["sb_L1f_g2"]
    wr = k.sb["sb_L1f_wrT"]
    eb8 = k.sb["sb_L1f_eb8"]
    slabs = [k.sb["slab0"], k.sb["slab1"]]
    x1v = x1_hbm.rearrange("(c p) t -> p c t", p=128)
    outv = out_hbm.rearrange("(c p) t -> p c t", p=128)

    nx2 = bfv(k, "ARA", 0, 16384)
    hbe = bfv(k, "ARA", 16384, 32768)
    acc = k.sb["ARB"].bitcast(F32)
    x1c = k.sb["ARC"].bitcast(F32)[:, 0:4096]
    asb = bfv(k, "ARC", 8192, 12288)
    cbb2 = [k.sb["ARC"].bitcast(F32)[:, 6144 + i * 512:6144 + (i + 1) * 512]
            for i in range(4)]
    x1b = f32v(k, "ARD", 0, 1024)
    sqf = f32v(k, "ARD", 1024, 1152)
    raw8 = f32v(k, "ARD", 1152, 1280)[0:8, :]
    f8 = f32v(k, "ARD", 1280, 1408)[0:8, :]
    cm8 = f32v(k, "ARD", 1408, 1536)[0:8, :]
    f1row = f32v(k, "ARD", 1536, 1664)[0:1, :]
    stg2 = [f32v(k, "ARD", 0, 512), f32v(k, "ARD", 512, 1024)]
    wg = f32v(k, "ARE", 0, 48)
    f1row2 = f32v(k, "ARE", 48, 560)[0:1, :]
    nxq = [bfv(k, "ARE", 1120, 1632), bfv(k, "ARE", 1632, 2144)]
    t3c = bfv(k, "ARE", 2144, 2656)
    tsb = bfv(k, "ARE", 2656, 3168)
    wrg = f32v(k, "ARE", 560, 624)

    s_x = k.sem(f"{tag}x")
    s_w = [k.sem(f"{tag}w0"), k.sem(f"{tag}w1")]
    s_t = k.sem("C_t")
    s_e = k.sem("C_e")
    s_p = k.sem("C_p")
    s_o = [k.sem(f"{tag}o0"), k.sem(f"{tag}o1")]
    s_c = k.sem(f"{tag}c")
    s_cf = k.sem(f"{tag}cf")

    def _vw(v):
        v.wait_ge(k.sems["wload"].h, k.sems["wload"].n)
        for c in range(NDC):
            ins = v.tensor_scalar(out=wrg[:, c * 8:(c + 1) * 8], in0=wr[:, c * 8:(c + 1) * 8],
                                  scalar1=g2[:, c:c + 1], scalar2=None, op0=OP.mult)
        k.inc(ins, s_e)
    k.v(_vw)

    for i in range(T // 128):
        t0 = i * 128
        emr = s_e.n

        def _gx(gq, t0=t0, emr=emr):
            gq.wait_ge(s_e.h, emr)
            k.dinc(gq.dma_start(out=x1b.rearrange("p (c t) -> p c t", c=NDC)[:, :, 0:128],
                                in_=x1v[:, :, t0:t0 + 128]), s_x)
        k.g(_gx)
        xm = s_x.n

        for c in range(NDC):
            s1p = s_e.n
            t1p = s_t.n

            def _s1(sc, c=c, xm=xm, s1p=s1p, t1p=t1p):
                sc.wait_ge(s_x.h, xm)
                sc.wait_ge(s_e.h, s1p)
                if t1p:
                    sc.wait_ge(s_t.h, t1p)
                k.inc(sc.activation(out=sqf[:, 0:128], in_=x1b[:, c * 128:c * 128 + 128],
                                    func=AF.Square), s_e)
            k.s(_s1)
            s1m = s_e.n

            def _t1(te, c=c, s1m=s1m, xm=xm):
                te.wait_ge(s_e.h, s1m)
                te.wait_ge(s_x.h, xm)
                te.matmul(ps[0:1, 3072:3072 + 128], ones_col[:], sqf[:, 0:128],
                          start=(c == 0), stop=(c == NDC - 1))
                k.inc(te.matmul(ps[0:8, 3712:3712 + 128], wrg[:, c * 8:(c + 1) * 8],
                                x1b[:, c * 128:c * 128 + 128],
                                start=(c == 0), stop=(c == NDC - 1)), s_t)
            k.t(_t1)
        t1m = s_t.n

        fwm = s_cf.n

        def _v1(v, t1m=t1m, fwm=fwm):
            v.wait_ge(s_t.h, t1m)
            if fwm:
                v.wait_ge(s_cf.h, fwm)
            k.inc(v.tensor_scalar(out=f1row, in0=ps[0:1, 3072:3072 + 128],
                                  scalar1=1.0 / D, scalar2=1e-6, op0=OP.mult,
                                  op1=OP.add), s_e)
            v.wait_ge(s_e.h, s_e.n)
            k.inc(v.tensor_copy(out=raw8, in_=ps[0:8, 3712:3712 + 128]), s_e)
        k.v(_v1)
        v1m = s_e.n

        def _s2(sc, v1m=v1m):
            sc.wait_ge(s_e.h, v1m)
            act_chain(k, sc, s_e, sc.activation(out=f1row, in_=f1row, func=AF.Ln))
            k.inc(sc.activation(out=f1row, in_=f1row, func=AF.Exp, bias=0.0,
                                scale=-0.5), s_e)
        k.s(_s2)
        s2m = s_e.n

        def _gf(gq, t0=t0, s2m=s2m):
            gq.wait_ge(s_e.h, s2m)
            k.dinc(gq.dma_start(out=k.dram["f1_s"][0:1, t0:t0 + 128], in_=f1row), s_cf)
        k.g(_gf)

        def _t2(te, s2m=s2m):
            te.wait_ge(s_e.h, s2m)
            k.inc(te.matmul(ps[:, 3840:3840 + 128], ones_row[:], f1row,
                            start=True, stop=True), s_t)
        k.t(_t2)
        t2m = s_t.n

        def _v2(v, t2m=t2m):
            v.wait_ge(s_t.h, t2m)
            k.inc(v.tensor_copy(out=f8, in_=ps[0:8, 3840:3840 + 128]), s_e)
            v.wait_ge(s_e.h, s_e.n)
            k.inc(v.tensor_tensor(out=raw8, in0=raw8, in1=f8, op=OP.mult), s_e)
            v.wait_ge(s_e.h, s_e.n)
            k.inc(v.tensor_scalar(out=raw8, in0=raw8, scalar1=eb8[0:8, 0:1],
                                  scalar2=None, op0=OP.add), s_e)
        k.v(_v2)
        v2m = s_e.n

        def _t3(te, v2m=v2m):
            te.wait_ge(s_e.h, v2m)
            k.inc(te.transpose(ps[0:128, 3584:3584 + 8], raw8, idf[0:8, 0:8]), s_t)
        k.t(_t3)
        t3m = s_t.n

        def _v3(v, t3m=t3m):
            E = s_e

            def step(ins):
                k.inc(ins, E)
                v.wait_ge(E.h, E.n)
            v.wait_ge(s_t.h, t3m)
            step(v.tensor_copy(out=wg[:, 0:8], in_=ps[0:128, 3584:3584 + 8]))
            step(v.reduce_max(out=wg[:, 32:33], in_=wg[:, 0:8], axis=AX.X))
            step(v.tensor_scalar(out=wg[:, 24:32], in0=wg[:, 0:8], scalar1=wg[:, 32:33],
                                 scalar2=None, op0=OP.is_equal))
            step(v.tensor_tensor(out=wg[:, 24:32], in0=wg[:, 24:32], in1=rev8[:, 0:8],
                                 op=OP.mult))
            step(v.reduce_max(out=wg[:, 33:34], in_=wg[:, 24:32], axis=AX.X))
            step(v.tensor_scalar(out=wg[:, 8:16], in0=wg[:, 24:32], scalar1=wg[:, 33:34],
                                 scalar2=None, op0=OP.is_equal))
            step(v.scalar_tensor_tensor(out=wg[:, 24:32], in0=wg[:, 8:16], scalar=-1e30,
                                        in1=wg[:, 0:8], op0=OP.mult, op1=OP.add))
            step(v.reduce_max(out=wg[:, 34:35], in_=wg[:, 24:32], axis=AX.X))
            step(v.tensor_scalar(out=wg[:, 24:32], in0=wg[:, 24:32], scalar1=wg[:, 34:35],
                                 scalar2=None, op0=OP.is_equal))
            step(v.tensor_tensor(out=wg[:, 24:32], in0=wg[:, 24:32], in1=rev8[:, 0:8],
                                 op=OP.mult))
            step(v.reduce_max(out=wg[:, 35:36], in_=wg[:, 24:32], axis=AX.X))
            step(v.tensor_scalar(out=wg[:, 16:24], in0=wg[:, 24:32], scalar1=wg[:, 35:36],
                                 scalar2=None, op0=OP.is_equal))
            k.inc(v.tensor_scalar(out=wg[:, 36:37], in0=wg[:, 32:33], scalar1=-1.0,
                                  scalar2=None, op0=OP.mult), E)
        k.v(_v3)
        v3m = s_e.n

        def _s4(sc, v3m=v3m):
            sc.wait_ge(s_e.h, v3m)
            k.inc(sc.activation(out=wg[:, 24:32], in_=wg[:, 0:8], func=AF.Exp,
                                bias=wg[:, 36:37], scale=1.0, accum_out=wg[:, 37:38]), s_e)
        k.s(_s4)
        s4m = s_e.n

        def _v5(v, s4m=s4m):
            E = s_e

            def step(ins):
                k.inc(ins, E)
                v.wait_ge(E.h, E.n)
            v.wait_ge(E.h, s4m)
            step(v.tensor_tensor(out=wg[:, 40:48], in0=wg[:, 24:32], in1=wg[:, 8:16],
                                 op=OP.mult))
            step(v.reduce_sum(out=wg[:, 38:39], in_=wg[:, 40:48], axis=AX.X))
            step(v.tensor_tensor(out=wg[:, 40:48], in0=wg[:, 24:32], in1=wg[:, 16:24],
                                 op=OP.mult))
            step(v.reduce_sum(out=wg[:, 39:40], in_=wg[:, 40:48], axis=AX.X))
            k.inc(v.tensor_tensor(out=wg[:, 39:40], in0=wg[:, 39:40], in1=wg[:, 38:39],
                                  op=OP.subtract), E)
        k.v(_v5)
        v5m = s_e.n

        def _s5(sc, v5m=v5m):
            sc.wait_ge(s_e.h, v5m)
            act_chain(k, sc, s_e, sc.activation(out=wg[:, 37:38], in_=wg[:, 37:38],
                                                func=AF.Ln))
            k.inc(sc.activation(out=wg[:, 37:38], in_=wg[:, 37:38], func=AF.Exp,
                                bias=0.0, scale=-1.0), s_e)
        k.s(_s5)
        s5m = s_e.n

        def _v6(v, s5m=s5m):
            v.wait_ge(s_e.h, s5m)
            k.inc(v.tensor_tensor(out=wg[:, 39:40], in0=wg[:, 39:40], in1=wg[:, 37:38],
                                  op=OP.mult), s_e)
        k.v(_v6)
        v6m = s_e.n

        def _s6(sc, v6m=v6m):
            sc.wait_ge(s_e.h, v6m)
            k.inc(sc.activation(out=wg[:, 39:40], in_=wg[:, 39:40], func=AF.Sigmoid), s_e)
        k.s(_s6)
        s6m = s_e.n

        def _v7(v, s6m=s6m):
            E = s_e

            def step(ins):
                k.inc(ins, E)
                v.wait_ge(E.h, E.n)
            v.wait_ge(E.h, s6m)
            step(v.tensor_scalar(out=wg[:, 38:39], in0=wg[:, 39:40], scalar1=-1.0,
                                 scalar2=1.0, op0=OP.mult, op1=OP.add))
            step(v.tensor_scalar(out=wg[:, 40:48], in0=wg[:, 8:16], scalar1=wg[:, 38:39],
                                 scalar2=None, op0=OP.mult))
            k.inc(v.scalar_tensor_tensor(out=wg[:, 40:48], in0=wg[:, 16:24],
                                         scalar=wg[:, 39:40], in1=wg[:, 40:48],
                                         op0=OP.mult, op1=OP.add), E)
        k.v(_v7)
        v7m = s_e.n

        def _t4(te, v7m=v7m):
            te.wait_ge(s_e.h, v7m)
            k.inc(te.transpose(ps[0:8, 3584:3584 + 128], wg[:, 40:48], idf[:]), s_t)
        k.t(_t4)
        t4m = s_t.n

        cwm = s_c.n

        def _v8(v, t4m=t4m, cwm=cwm):
            v.wait_ge(s_t.h, t4m)
            if cwm:
                v.wait_ge(s_c.h, cwm)
            k.inc(v.tensor_copy(out=cm8, in_=ps[0:8, 3584:3584 + 128]), s_e)
        k.v(_v8)
        v8m = s_e.n

        def _gc(gq, t0=t0, v8m=v8m):
            gq.wait_ge(s_e.h, v8m)
            k.dinc(gq.dma_start(out=k.dram["comb_s"][0:8, t0:t0 + 128], in_=cm8), s_c)
        k.g(_gc)

    cbm = s_c.n
    cfm = s_cf.n
    for sc_i in range(nsc):
        S0 = sc_i * NSC
        acc_em = {}
        for j in range(4):
            t0 = S0 + j * TC
            emn = s_e.n

            def _gx2(gq, t0=t0, emn=emn, cfm=cfm):
                gq.wait_ge(s_e.h, emn)
                gq.wait_ge(s_cf.h, cfm)
                k.dinc(gq.dma_start(out=x1c.rearrange("p (c t) -> p c t", c=NDC),
                                    in_=x1v[:, :, t0:t0 + TC]), s_x)
                k.dinc(gq.dma_start(out=f1row2, in_=k.dram["f1_s"][0:1, t0:t0 + TC]), s_x)
            k.g(_gx2)
            xm3 = s_x.n

            def _t5(te, xm3=xm3):
                te.wait_ge(s_x.h, xm3)
                k.inc(te.matmul(ps[:, 3584:3584 + TC], ones_row[:], f1row2,
                                start=True, stop=True), s_t)
            k.t(_t5)
            t5m = s_t.n

            def _v9(v, j=j, t5m=t5m):
                v.wait_ge(s_t.h, t5m)
                for c in range(NDC):
                    k.inc(v.tensor_tensor(out=nxq[c % 2], in0=x1c[:, c * TC:(c + 1) * TC],
                                          in1=ps[:, 3584:3584 + TC], op=OP.mult), s_e)
                    v.wait_ge(s_e.h, s_e.n)
                    ins = v.tensor_scalar(out=nx2[:, c * NSC + j * TC:c * NSC + j * TC + TC],
                                          in0=nxq[c % 2], scalar1=g2[:, c:c + 1],
                                          scalar2=None, op0=OP.mult)
                k.inc(ins, s_e)
            k.v(_v9)
        nx2m = s_e.n

        for ei in range(9):
            shared = (ei == 0)
            w1n = "L1f_sW1T" if shared else f"L1f_eW1T_{ei - 1}"
            w3n = "L1f_sW3T" if shared else f"L1f_eW3T_{ei - 1}"
            w2n = "L1f_sW2T" if shared else f"L1f_eW2T_{ei - 1}"

            if not shared:
                ccm = s_e.n

                def _gcr(gq, ei=ei, S0=S0, ccm=ccm, cbm=cbm):
                    gq.wait_ge(s_c.h, cbm)
                    gq.wait_ge(s_e.h, ccm)
                    k.dinc(gq.dma_start(out=f1row2,
                                        in_=k.dram["comb_s"][ei - 1:ei, S0:S0 + TC]), s_x)
                k.g(_gcr)
                for nb in range(4):
                    if nb > 0:
                        xw = s_x.n

                        def _gcr2(gq, ei=ei, S0=S0, nb=nb, xw=xw):
                            gq.wait_ge(s_t.h, s_t.n)
                            k.dinc(gq.dma_start(
                                out=f1row2,
                                in_=k.dram["comb_s"][ei - 1:ei, S0 + nb * TC:S0 + (nb + 1) * TC]),
                                s_x)
                        k.g(_gcr2)
                    xcm = s_x.n

                    t6g = s_e.n

                    def _t6(te, nb=nb, xcm=xcm, t6g=t6g):
                        te.wait_ge(s_x.h, xcm)
                        te.wait_ge(s_e.h, t6g)
                        k.inc(te.matmul(ps[:, 3584:3584 + TC], ones_row[:], f1row2,
                                        start=True, stop=True), s_t)
                    k.t(_t6)
                    t6m = s_t.n

                    def _v10(v, nb=nb, t6m=t6m):
                        v.wait_ge(s_t.h, t6m)
                        k.inc(v.tensor_copy(out=cbb2[nb], in_=ps[:, 3584:3584 + TC]), s_e)
                    k.v(_v10)
            cbm2 = s_e.n

            for og in range(4):
                sw = s_w[og % 2]
                tslw = s_t.n

                def _gws(gq, w1n=w1n, w3n=w3n, og=og, sw=sw, tslw=tslw):
                    if tslw:
                        gq.wait_ge(s_t.h, tslw)
                    k.dinc(gq.dma_start(
                        out=slabs[0][:, 0:2048].rearrange("p (c o) -> p c o", o=256),
                        in_=k.dram[w1n].rearrange("p (c o) -> p c o", c=NDC)
                        [:, :, og * 256:(og + 1) * 256]), sw)
                    k.dinc(gq.dma_start(
                        out=slabs[1][:, 0:2048].rearrange("p (c o) -> p c o", o=256),
                        in_=k.dram[w3n].rearrange("p (c o) -> p c o", c=NDC)
                        [:, :, og * 256:(og + 1) * 256]), sw)
                k.g(_gws)
                wsm = (sw, sw.n)

                for oo in range(2):
                    o = og * 2 + oo
                    emw = s_e.n

                    def _t7(te, o=o, oo=oo, wsm=wsm, nx2m=nx2m, emw=emw, cbm2=cbm2):
                        te.wait_ge(wsm[0].h, wsm[1])
                        te.wait_ge(s_e.h, max(nx2m, emw, cbm2))
                        for nb in range(4):
                            last = mm_acc(
                                te, ps[:, nb * 512:nb * 512 + 512],
                                [(slabs[0][:, c * 256 + oo * 128:c * 256 + oo * 128 + 128],
                                  nx2[:, c * NSC + nb * 512:c * NSC + nb * 512 + 512])
                                 for c in range(NDC)])
                        for nb in range(4):
                            last = mm_acc(
                                te, ps[:, 2048 + nb * 512:2048 + nb * 512 + 512],
                                [(slabs[1][:, c * 256 + oo * 128:c * 256 + oo * 128 + 128],
                                  nx2[:, c * NSC + nb * 512:c * NSC + nb * 512 + 512])
                                 for c in range(NDC)])
                        k.inc(last, s_t)
                    k.t(_t7)
                    t7m = s_t.n

                    def _s7(sc, oo=oo, t7m=t7m):
                        sc.wait_ge(s_t.h, t7m)
                        k.inc(sc.activation(out=asb[:, oo * 2048:(oo + 1) * 2048],
                                            in_=ps[:, 0:2048], func=AF.Silu), s_e)
                    k.s(_s7)
                    s7m = s_e.n

                    for nb in range(4):
                        s8g = s_e.n

                        def _s8(sc, nb=nb, s7m=s7m, s8g=s8g):
                            sc.wait_ge(s_e.h, max(s7m, s8g))
                            k.inc(sc.activation(out=tsb[:, 0:512],
                                                in_=ps[:, 2048 + nb * 512:2048 + nb * 512 + 512],
                                                func=AF.Copy), s_e)
                        k.s(_s8)
                        s8m = s_e.n

                        v11g = s_p.n

                        def _v11(v, nb=nb, s8m=s8m, shared=shared, v11g=v11g):
                            v.wait_ge(s_e.h, s8m)
                            if v11g:
                                v.wait_ge(s_p.h, v11g)
                            if shared:
                                k.inc(v.tensor_copy(out=t3c[:, 0:512], in_=tsb[:, 0:512]), s_e)
                            else:
                                k.inc(v.tensor_tensor(out=t3c[:, 0:512], in0=tsb[:, 0:512],
                                                      in1=cbb2[nb], op=OP.mult), s_e)
                        k.v(_v11)
                        v11m = s_e.n

                        def _p7(g, o=o, oo=oo, nb=nb, v11m=v11m):
                            g.wait_ge(s_e.h, v11m)
                            k.inc(g.tensor_tensor(
                                out=hbe[:, o * NSC + nb * 512:o * NSC + nb * 512 + 512],
                                in0=asb[:, oo * 2048 + nb * 512:oo * 2048 + nb * 512 + 512],
                                in1=t3c[:, 0:512], op=OP.mult), s_p)
                        k.g(_p7)

            pm7 = s_p.n
            sw = s_w[0]
            tslw2 = s_t.n

            def _gw2s(gq, w2n=w2n, sw=sw, tslw2=tslw2):
                if tslw2:
                    gq.wait_ge(s_t.h, tslw2)
                k.dinc(gq.dma_start(out=slabs[0][:], in_=k.dram[w2n][:, 0:4096]), sw)
                k.dinc(gq.dma_start(out=slabs[1][:], in_=k.dram[w2n][:, 4096:8192]), sw)
            k.g(_gw2s)
            w2sm = (sw, sw.n)

            for o in range(NDC):
                par = o % 2
                region = par * 2048
                emw2 = acc_em.get(o - 2, 0)

                def _t8(te, o=o, region=region, w2sm=w2sm, pm7=pm7, emw2=emw2):
                    te.wait_ge(w2sm[0].h, w2sm[1])
                    te.wait_ge(s_p.h, pm7)
                    if emw2:
                        te.wait_ge(s_e.h, emw2)
                    for nb in range(4):
                        last = mm_acc(
                            te, ps[:, region + nb * 512:region + nb * 512 + 512],
                            [(slabs[c // 4][:, (c % 4) * 1024 + o * 128:
                                            (c % 4) * 1024 + o * 128 + 128],
                              hbe[:, c * NSC + nb * 512:c * NSC + nb * 512 + 512])
                             for c in range(NDC)])
                    k.inc(last, s_t)
                k.t(_t8)
                t8m = s_t.n

                def _v12(v, o=o, region=region, t8m=t8m, ei=ei):
                    v.wait_ge(s_t.h, t8m)
                    if ei == 0:
                        ins = v.tensor_copy(out=acc[:, o * NSC:(o + 1) * NSC],
                                            in_=ps[:, region:region + 2048])
                    else:
                        ins = v.tensor_tensor(out=acc[:, o * NSC:(o + 1) * NSC],
                                              in0=acc[:, o * NSC:(o + 1) * NSC],
                                              in1=ps[:, region:region + 2048], op=OP.add)
                    k.inc(ins, s_e)
                k.v(_v12)
                acc_em[o] = s_e.n

        accm = s_e.n
        for j in range(4):
            t0 = S0 + j * TC
            emo = s_e.n

            def _gx3(gq, t0=t0, emo=emo, accm=accm):
                gq.wait_ge(s_e.h, max(emo, accm))
                k.dinc(gq.dma_start(out=x1c.rearrange("p (c t) -> p c t", c=NDC),
                                    in_=x1v[:, :, t0:t0 + TC]), s_x)
            k.g(_gx3)
            xm4 = s_x.n

            for o in range(NDC):
                so = s_o[o % 2]
                stb = stg2[o % 2]
                omk = so.n

                def _v13(v, o=o, j=j, xm4=xm4, stb=stb, omk=omk, so=so):
                    v.wait_ge(s_x.h, xm4)
                    if omk:
                        v.wait_ge(so.h, omk)
                    k.inc(v.tensor_tensor(out=stb[:, 0:TC],
                                          in0=acc[:, o * NSC + j * TC:o * NSC + j * TC + TC],
                                          in1=x1c[:, o * TC:(o + 1) * TC], op=OP.add), s_e)
                k.v(_v13)
                v13m = s_e.n

                def _g6(gq, o=o, t0=t0, v13m=v13m, stb=stb, so=so):
                    gq.wait_ge(s_e.h, v13m)
                    k.dinc(gq.dma_start(out=outv[:, o, t0:t0 + TC], in_=stb[:, 0:TC]), so)
                k.g(_g6)
    return s_o



def emit_l0_ffn(k, x_hbm, xn_hbm, stats_hbm, out_hbm, tag="l0f"):
    T = k.T
    ones_col = k.sb["ones_col_f32"]
    ones_row = k.sb["ones_row_f32"]
    onesb = k.sb["sb_ones_row_bf"]
    idf = k.sb["sb_ident_f32c"]
    rev8 = k.sb["sb_rev8"]
    sel8r = k.sb["sb_L0f_sel8r"]
    wr = k.sb["sb_L0f_wr"]
    binm = k.sb["sb_L0f_bin"]
    bout8 = k.sb["sb_L0f_bout8"]
    ps = k.ps_arena

    mjp = k.sb["ARA"].bitcast(F32)[:, 0:T]
    ysb = k.sb["ARA"].bitcast(F32)[:, 4096:4096 + 8192]
    mjb = k.sb["ARC"][:, 0:T]
    xq = k.sb["ARC"][:, 4096:12288]
    hm = k.sb["ARB"][:, 0:8192]
    mbc = k.sb["ARB"][:, 8192:9216]
    x1b = f32v(k, "ARD", 0, 1024)
    stp = f32v(k, "ARD", 1024, 1280)
    rb8 = f32v(k, "ARD", 1280, 1408)[0:8, :]
    lgt = f32v(k, "ARD", 1408, 1536)[0:8, :]
    wgp = f32v(k, "ARE", 0, 64)
    wc = f32v(k, "ARE", 64, 128)
    csr = f32v(k, "ARE", 128, 144)[0:1, :]

    s_x = k.sem(f"{tag}x")
    s_w = [k.sem(f"{tag}w0"), k.sem(f"{tag}w1")]
    s_t = k.sem("C_t")
    s_e = k.sem("C_e")
    s_p = k.sem("C_p")
    s_o = [k.sem(f"{tag}o0"), k.sem(f"{tag}o1")]

    xv = x_hbm.rearrange("(c p) t -> p c t", p=128)
    xnv = xn_hbm.rearrange("p (c t) -> p c t", c=NDC)
    ov = out_hbm.rearrange("(c p) t -> p c t", p=128)

    def _t0(te):
        te.wait_ge(k.sems["wload"].h, k.sems["wload"].n)
        k.inc(mm_acc(te, ps[0:8, 1536:1537],
                     [(wr[:, c * 8:(c + 1) * 8], ones_col[:]) for c in range(NDC)]), s_t)
    k.t(_t0)
    t0m = s_t.n

    def _v0a(v, t0m=t0m):
        v.wait_ge(s_t.h, t0m)
        k.inc(v.tensor_copy(out=wc[0:8, 8:9], in_=ps[0:8, 1536:1537]), s_e)
    k.v(_v0a)
    v0am = s_e.n

    def _t0b(te, v0am=v0am):
        te.wait_ge(s_e.h, v0am)
        k.inc(te.transpose(ps[0:1, 2048:2056], wc[0:8, 8:9], idf[0:8, 0:8]), s_t)
    k.t(_t0b)
    t0bm = s_t.n

    def _v0(v, t0bm=t0bm):
        v.wait_ge(s_t.h, t0bm)
        k.inc(v.tensor_copy(out=csr[0:1, 0:8], in_=ps[0:1, 2048:2056]), s_e)
    k.v(_v0)
    v0m = s_e.n

    NP = T // 128
    em = {}
    for i in range(NP):
        t0 = i * 128
        prev = em.get(i - 1, {})

        def _gx(gq, t0=t0, prev=prev):
            if prev.get("x"):
                gq.wait_ge(s_e.h, prev["x"])
            k.dinc(gq.dma_start(out=x1b.rearrange("p (c t) -> p c t", c=NDC),
                                in_=xv[:, :, t0:t0 + 128]), s_x)
            k.dinc(gq.dma_start(out=stp[0:1, 0:128], in_=stats_hbm[0:1, t0:t0 + 128]), s_x)
            k.dinc(gq.dma_start(out=stp[0:1, 128:256], in_=stats_hbm[1:2, t0:t0 + 128]), s_x)
        k.g(_gx)
        xm = s_x.n

        def _t1(te, xm=xm, prev=prev, v0m=v0m):
            te.wait_ge(s_x.h, xm)
            te.wait_ge(s_e.h, max(v0m, prev.get("ps", 0)))
            mm_acc(te, ps[0:8, 0:128],
                   [(wr[:, c * 8:(c + 1) * 8], x1b[:, c * 128:(c + 1) * 128])
                    for c in range(NDC)])
            te.matmul(ps[0:8, 512:640], csr[0:1, 0:8], stp[0:1, 128:256],
                      start=True, stop=True)
            k.inc(te.matmul(ps[0:8, 1024:1152], ones_row[0:1, 0:8], stp[0:1, 0:128],
                            start=True, stop=True), s_t)
        k.t(_t1)
        t1m = s_t.n

        def _v1(v, t1m=t1m, prev=prev):
            E = s_e

            def step(ins):
                k.inc(ins, E)
                v.wait_ge(E.h, E.n)
            v.wait_ge(s_t.h, t1m)
            step(v.tensor_copy(out=rb8, in_=ps[0:8, 1024:1152]))
            step(v.tensor_tensor(out=lgt, in0=ps[0:8, 0:128], in1=rb8, op=OP.mult))
            k.inc(v.tensor_tensor(out=lgt, in0=lgt, in1=ps[0:8, 512:640], op=OP.add), E)
        k.v(_v1)
        v1m = s_e.n
        em.setdefault(i, {})["ps"] = v1m
        if getattr(k, "dbgl0f", False):
            s_dl = k.sem(f"{tag}dl")

            def _gdl(gq, t0=t0, v1m=v1m):
                gq.wait_ge(s_e.h, v1m)
                k.dinc(gq.dma_start(out=k.dram["dbg_lgt"][0:8, t0:t0 + 128],
                                    in_=lgt), s_dl)
            k.g(_gdl)

            def _vdl(v):
                v.wait_ge(s_dl.h, s_dl.n)
                k.inc(v.tensor_copy(out=wc[0:1, 15:16], in_=wc[0:1, 15:16]), s_e)
            k.v(_vdl)

        def _t2(te, v1m=v1m, prev=prev):
            te.wait_ge(s_e.h, max(v1m, prev.get("wg", 0)))
            k.inc(te.transpose(ps[0:128, 1544:1552], lgt, idf[0:8, 0:8]), s_t)
        k.t(_t2)
        t2m = s_t.n

        def _v2(v, t2m=t2m):
            E = s_e

            def step(ins):
                k.inc(ins, E)
                v.wait_ge(E.h, E.n)
            v.wait_ge(s_t.h, t2m)
            step(v.tensor_copy(out=wgp[:, 0:8], in_=ps[0:128, 1544:1552]))
            step(v.reduce_max(out=wc[:, 0:1], in_=wgp[:, 0:8], axis=AX.X, negate=True))
        k.v(_v2)
        v2m = s_e.n

        def _s1(sc, v2m=v2m):
            sc.wait_ge(s_e.h, v2m)
            act_chain(k, sc, s_e, sc.activation(out=wgp[:, 8:16], in_=wgp[:, 0:8],
                                                func=AF.Exp, bias=wc[:, 0:1], scale=1.0,
                                                accum_out=wc[:, 1:2]))
            act_chain(k, sc, s_e, sc.activation(out=wc[:, 2:3], in_=wc[:, 1:2], func=AF.Ln))
            k.inc(sc.activation(out=wc[:, 2:3], in_=wc[:, 2:3], func=AF.Exp,
                                bias=0.0, scale=-1.0), s_e)
        k.s(_s1)
        s1m = s_e.n

        def _v3(v, s1m=s1m):
            E = s_e

            def step(ins):
                k.inc(ins, E)
                v.wait_ge(E.h, E.n)
            v.wait_ge(s_e.h, s1m)
            step(v.tensor_scalar(out=wgp[:, 0:8], in0=wgp[:, 8:16], scalar1=wc[:, 2:3],
                                 scalar2=None, op0=OP.mult))
            step(v.reduce_max(out=wc[:, 3:4], in_=wgp[:, 0:8], axis=AX.X))
            step(v.tensor_scalar(out=wgp[:, 8:16], in0=wgp[:, 0:8], scalar1=wc[:, 3:4],
                                 scalar2=None, op0=OP.is_equal))
            step(v.tensor_tensor(out=wgp[:, 16:24], in0=wgp[:, 8:16], in1=rev8[:, 0:8],
                                 op=OP.mult))
            step(v.reduce_max(out=wc[:, 4:5], in_=wgp[:, 16:24], axis=AX.X))
            step(v.tensor_scalar(out=wgp[:, 8:16], in0=wgp[:, 16:24], scalar1=wc[:, 4:5],
                                 scalar2=None, op0=OP.is_equal))
            step(v.scalar_tensor_tensor(out=wgp[:, 16:24], in0=wgp[:, 8:16], scalar=-2.0,
                                        in1=wgp[:, 0:8], op0=OP.mult, op1=OP.add))
            step(v.reduce_max(out=wc[:, 5:6], in_=wgp[:, 16:24], axis=AX.X))
            step(v.tensor_scalar(out=wgp[:, 24:32], in0=wgp[:, 16:24], scalar1=wc[:, 5:6],
                                 scalar2=None, op0=OP.is_equal))
            step(v.tensor_tensor(out=wgp[:, 24:32], in0=wgp[:, 24:32], in1=rev8[:, 0:8],
                                 op=OP.mult))
            step(v.reduce_max(out=wc[:, 6:7], in_=wgp[:, 24:32], axis=AX.X))
            step(v.tensor_scalar(out=wgp[:, 24:32], in0=wgp[:, 24:32], scalar1=wc[:, 6:7],
                                 scalar2=None, op0=OP.is_equal))
            step(v.tensor_scalar(out=wgp[:, 8:16], in0=wgp[:, 8:16], scalar1=wc[:, 3:4],
                                 scalar2=None, op0=OP.mult))
            step(v.tensor_scalar(out=wgp[:, 24:32], in0=wgp[:, 24:32], scalar1=wc[:, 5:6],
                                 scalar2=None, op0=OP.mult))
            k.inc(v.tensor_tensor(out=wgp[:, 8:16], in0=wgp[:, 8:16], in1=wgp[:, 24:32],
                                  op=OP.add), E)
        k.v(_v3)
        v3m = s_e.n
        em[i]["wg"] = v3m

        def _t3(te, v3m=v3m, prev=prev):
            te.wait_ge(s_e.h, max(v3m, prev.get("mj", 0)))
            k.inc(te.transpose(ps[0:8, 2560:2688], wgp[:, 8:16], idf[:]), s_t)
        k.t(_t3)
        t3m = s_t.n

        def _v4(v, t3m=t3m, t0=t0):
            v.wait_ge(s_t.h, t3m)
            k.inc(v.tensor_copy(out=mjp[0:8, t0:t0 + 128], in_=ps[0:8, 2560:2688]), s_e)
        k.v(_v4)

        def _s2(sc, t3m=t3m, t0=t0):
            sc.wait_ge(s_t.h, t3m)
            k.inc(sc.activation(out=mjb[0:8, t0:t0 + 128], in_=ps[0:8, 2560:2688],
                                func=AF.Identity), s_e)
        k.s(_s2)
        em[i]["mj"] = s_e.n
        em[i]["x"] = s_e.n

    k.barrier()
    if getattr(k, "dbgl0f", False):
        def _gdbg(gq):
            k.dinc(gq.dma_start(out=k.dram["dbg_mjp"][0:8, :], in_=mjp[0:8, :]), s_x)
        k.g(_gdbg)
        k.barrier()

    TCF = 1024
    nch = T // TCF
    wiv = [k.dram[f"L0f_winT_{e}"].rearrange("p (c o) -> p c o", c=NDC) for e in range(8)]
    wov = [k.dram[f"L0f_woutT_{e}"].rearrange("p (c o) -> p c o", c=NDC) for e in range(8)]
    slabq = [(k.sb["slab0"][:, 0:1024], k.sem(f"{tag}sa")),
             (k.sb["slab0"][:, 1024:2048], k.sem(f"{tag}sb")),
             (k.sb["slab1"][:, 0:1024], k.sem(f"{tag}sc")),
             (k.sb["slab1"][:, 1024:2048], k.sem(f"{tag}sd"))]
    tmk = {}
    LQ = [0]

    def stage_load(src3, cidx):
        L = LQ[0]
        LQ[0] += 1
        slab, sem = slabq[L % 4]
        pm = tmk.get(L - 4)

        def _g(gq, src3=src3, cidx=cidx, slab=slab, sem=sem, pm=pm):
            if pm:
                gq.wait_ge(s_t.h, pm)
            k.dinc(gq.dma_start(
                out=slab.rearrange("p (c o) -> p c o", c=NDC),
                in_=src3[:, :, cidx * 128:(cidx + 1) * 128]), sem)
        k.g(_g)
        return (L, slab, sem, sem.n)

    for i in range(nch):
        t0 = i * TCF
        xm0 = s_t.n

        def _gq(gq, t0=t0, xm0=xm0):
            if xm0:
                gq.wait_ge(s_t.h, xm0)
            k.dinc(gq.dma_start(out=xq.rearrange("p (c t) -> p c t", c=NDC),
                                in_=xnv[:, :, t0:t0 + TCF]), s_x)
        k.g(_gq)
        xqm = s_x.n

        pend = []
        for e in range(8):
            mprev = s_e.n

            def _tb(te, e=e, t0=t0, mprev=mprev):
                te.wait_ge(s_e.h, mprev)
                for nb in range(2):
                    k.inc(te.matmul(ps[:, nb * 512:(nb + 1) * 512],
                                    sel8r[:, e * 128:(e + 1) * 128],
                                    mjb[0:8, t0 + nb * 512:t0 + nb * 512 + 512],
                                    start=True, stop=True), s_t)
            k.t(_tb)
            tbm = s_t.n

            def _sb(sc, tbm=tbm):
                sc.wait_ge(s_t.h, tbm)
                k.inc(sc.activation(out=mbc, in_=ps[:, 0:1024], func=AF.Identity), s_e)
            k.s(_sb)
            mbm = s_e.n

            hmarks = []
            for hc in range(8):
                ld = stage_load(wiv[e], hc)
                b0 = 1024 * (hc % 2)
                gprev = s_e.n

                spw = s_p.n

                def _th(te, e=e, hc=hc, ld=ld, b0=b0, xqm=xqm, spw=spw, mbm=mbm):
                    te.wait_ge(ld[2].h, ld[3])
                    te.wait_ge(s_x.h, xqm)
                    te.wait_ge(s_e.h, mbm)
                    if spw:
                        te.wait_ge(s_p.h, spw)
                    for nb in range(2):
                        last = mm_acc(te, ps[:, b0 + nb * 512:b0 + nb * 512 + 512],
                                      [(ld[1][:, c * 128:(c + 1) * 128],
                                        xq[:, c * TCF + nb * 512:c * TCF + nb * 512 + 512])
                                       for c in range(NDC)])
                    k.inc(last, s_t)
                k.t(_th)
                tmk[ld[0]] = s_t.n
                thm = s_t.n

                spw2 = s_p.n

                def _sh(sc, e=e, hc=hc, b0=b0, thm=thm, mbm=mbm, spw2=spw2):
                    sc.wait_ge(s_t.h, thm)
                    sc.wait_ge(s_e.h, mbm)
                    if spw2:
                        sc.wait_ge(s_p.h, spw2)
                    k.inc(sc.activation(out=hm[:, hc * 1024:(hc + 1) * 1024],
                                        in_=ps[:, b0:b0 + 1024],
                                        func=AF.Gelu_apprx_tanh,
                                        bias=binm[:, e * 8 + hc:e * 8 + hc + 1],
                                        scale=1.0), s_e)
                k.s(_sh)
                shm = s_e.n

                stw = s_t.n

                def _ph(g, hc=hc, shm=shm, stw=stw):
                    g.wait_ge(s_e.h, shm)
                    g.wait_ge(s_t.h, stw)
                    k.inc(g.tensor_tensor(out=hm[:, hc * 1024:(hc + 1) * 1024],
                                          in0=hm[:, hc * 1024:(hc + 1) * 1024],
                                          in1=mbc, op=OP.mult), s_p)
                k.g(_ph)
                hmarks.append(s_p.n)

            for oc in range(8):
                ld = stage_load(wov[e], oc)
                b0 = 2048 + 1024 * (oc % 2)
                epm = s_e.n

                def _ty(te, e=e, oc=oc, i=i, ld=ld, b0=b0, epm=epm, hmarks=hmarks, t0=t0):
                    te.wait_ge(ld[2].h, ld[3])
                    te.wait_ge(s_p.h, hmarks[-1])
                    if epm:
                        te.wait_ge(s_e.h, epm)
                    for nb in range(2):
                        last = None
                        for c in range(NDC):
                            last = te.matmul(
                                ps[:, b0 + nb * 512:b0 + nb * 512 + 512],
                                ld[1][:, c * 128:(c + 1) * 128],
                                hm[:, c * TCF + nb * 512:c * TCF + nb * 512 + 512],
                                start=(c == 0), stop=(c == NDC - 1 and e != 7))
                        if e == 7:
                            last = te.matmul(
                                ps[:, b0 + nb * 512:b0 + nb * 512 + 512],
                                bout8[:, oc * 128:(oc + 1) * 128],
                                mjp[0:8, t0 + nb * 512:t0 + nb * 512 + 512],
                                start=False, stop=True)
                    k.inc(last, s_t)
                k.t(_ty)
                tmk[ld[0]] = s_t.n
                tym = s_t.n

                sop = s_o[(i - 1) % 2]
                sopm = sop.n if i >= 1 else 0

                def _ve(v, e=e, oc=oc, i=i, b0=b0, tym=tym, sop=sop, sopm=sopm):
                    v.wait_ge(s_t.h, tym)
                    if e == 0 and sopm:
                        v.wait_ge(sop.h, sopm)
                    if e == 0:
                        k.inc(v.tensor_copy(out=ysb[:, oc * 1024:(oc + 1) * 1024],
                                            in_=ps[:, b0:b0 + 1024]), s_e)
                    else:
                        k.inc(v.tensor_tensor(out=ysb[:, oc * 1024:(oc + 1) * 1024],
                                              in0=ysb[:, oc * 1024:(oc + 1) * 1024],
                                              in1=ps[:, b0:b0 + 1024], op=OP.add), s_e)
                k.v(_ve)
                em[("y", i, oc, e)] = s_e.n

        fm = s_e.n
        so = s_o[i % 2]

        def _go(gq, i=i, t0=t0, fm=fm, so=so):
            gq.wait_ge(s_e.h, fm)
            k.dinc(gq.dma_start(out=ov[:, :, t0:t0 + TCF],
                                in_=ysb.rearrange("p (c t) -> p c t", c=NDC)), so)
        k.g(_go)
        em.setdefault(i, {})
    return s_o


def l0_ffn_host(k, inputs):
    rw = np.asarray(inputs["L0_ffn_router_w"], np.float32)
    win = np.asarray(inputs["L0_ffn_w_in"], np.float32)
    bi = np.asarray(inputs["L0_ffn_b_in"], np.float32)
    wo = np.asarray(inputs["L0_ffn_w_out"], np.float32)
    bo = np.asarray(inputs["L0_ffn_b_out"], np.float32)
    k.weight("L0f_wr", wimg_lhsT(rw))
    for e in range(8):
        k.weight(f"L0f_winT_{e}", wimg_lhsT(win[e]))
        k.weight(f"L0f_woutT_{e}", wimg_lhsT(wo[e]))
    k.weight("L0f_bin", np.concatenate([bimg(bi[e]) for e in range(8)], axis=1))
    k.weight("L0f_bout8", np.ascontiguousarray(bo))
    sel = np.zeros((8, 1024), np.float32)
    for e in range(8):
        sel[e, e * 128:(e + 1) * 128] = 1.0
    k.weight("L0f_sel8r", sel)


L0F_SMALL_W = [("L0f_wr", F32), ("L0f_bin", F32)]


def alloc_l0f_alias(k):
    k.sb["sb_L0f_sel8r"] = k.sb["ARE"][0:8, 1024:2048]
    k.sb["sb_L0f_bout8"] = k.sb["ARC"].bitcast(F32)[0:8, 6144:7168]

L0_SCALING = 0.08838834764831845
HPI = 1.5707963267948966
PIF = 3.141592653589793
TWOPI = 6.283185307179586


def cap_ap(apobj, dims, offset=None):
    import bass_rust
    a = apobj.copy()
    a.ap = bass_rust.VecI64Pair(dims)
    if offset is not None:
        a.offset = offset
    return a


def alloc_l0a_alias(k):
    k.sb["sb_L0a_bqkv"] = f32v(k, "ARE", 1120, 1144)
    k.sb["sb_L0a_bo"] = f32v(k, "ARE", 1144, 1152)
    k.sb["sb_L0a_maskblk"] = k.sb["ARE"][:, 2048:2176]
    k.sb["sb_L0a_sel8c"] = k.sb["ARE"][:, 2176:2240]
    k.sb["sb_L0a_th"] = f32v(k, "ARE", 1200, 1221)[0:1, :]
    k.sb["sb_L0a_ph"] = f32v(k, "ARE", 1221, 1242)[0:1, :]
    k.sb["sb_L0a_phi"] = f32v(k, "ARE", 1242, 1243)
    k.sb["sb_L0a_qwsl"] = f32v(k, "ARE", 1248, 1256)[0:1, :]


def emit_l0_attn(k, xn_hbm, out_hbm, tag="l0a"):
    T = k.T
    TC = 512
    nch = T // TC
    ones_row = k.sb["ones_row_f32"]
    onesb = k.sb["sb_ones_row_bf"]
    onescb = k.sb["sb_ones_col_bfc"]
    idb = k.sb["sb_ident_bf"]
    idf = k.sb["sb_ident_f32c"]
    mskb = k.sb["sb_L0a_maskblk"]
    sel8c = k.sb["sb_L0a_sel8c"]
    throw = k.sb["sb_L0a_th"]
    phrow = k.sb["sb_L0a_ph"]
    phi = k.sb["sb_L0a_phi"]
    qwsl = k.sb["sb_L0a_qwsl"]
    bqkv = k.sb["sb_L0a_bqkv"]
    bo = k.sb["sb_L0a_bo"]
    ps = k.ps_arena
    psb = ps.bitcast(BF16)

    ARAf = k.sb["ARA"].bitcast(F32)
    QR = ARAf[:, 0:4096]
    QI = ARAf[:, 4096:8192]
    KR = ARAf[:, 8192:12288]
    KI = ARAf[:, 12288:16384]
    ARB = k.sb["ARB"]
    ARBf = ARB.bitcast(F32)
    vre = ARB[:, 0:4096]
    vim = ARB[:, 4096:8192]
    qnb = ARB[:, 8192:12288]
    prods = ARB[:, 12288:16384]
    meas = ARB[:, 16384:20480]
    Aac = ARBf[:, 10240:11264]
    Bac = ARBf[:, 11264:12288]
    ATb = ARBf[:, 12288:13312]
    BTb = ARBf[:, 13312:14336]
    zr = ARBf[:, 14336:14848]
    zi = ARBf[:, 14848:15360]
    zt = ARBf[:, 15360:15872]
    ARC = k.sb["ARC"]
    ARCf = ARC.bitcast(F32)
    vqhp = ARC[:, 0:2048]
    blkb = [ARC[:, 2048:2176], ARC[:, 2176:2304]]
    ajm = ARC[:, 2560:2816]
    Sbuf = ARCf[:, 1920:2944][0:8, :]
    Ec = ARC[:, 5888:6912][0:8, :]
    rcrow = ARCf[:, 3456:4480][0:1, :]
    rcb16 = ARC[:, 8960:9984][0:1, :]
    sqf = ARCf[:, 4992:6016]
    r2b = ARCf[:, 6016:6528]
    eab = ARCf[:, 6528:7040]
    ebb = ARCf[:, 7040:7552]
    ftm = ARCf[:, 7552:8064]
    sqb = ARC[:, 9984:10496]
    ARD = k.sb["ARD"]
    pbufs = [ARD[:, 0:4096], ARD[:, 4096:8192]]
    stg = [ARD[:, 4096:4608], ARD[:, 4608:5120]]
    ampsl4 = ARD[:, 4096:8192][0:8, :]
    AREf = k.sb["ARE"].bitcast(F32)
    fac = AREf[0:1, 0:512]
    facb = k.sb["ARE"][0:1, 2560:3072]
    cst = AREf[:, 528:608]
    phsc = AREf[:, 608:610]
    grow = AREf[0:1, 664:792]
    prow = AREf[0:1, 792:872]

    s_x = k.sem(f"{tag}x")
    s_t = k.sem("C_t")
    s_e = k.sem("C_e")
    s_p = k.sem("C_p")
    s_a = k.sem(f"{tag}a")
    s_j = k.sem(f"{tag}j")
    s_w = [k.sem(f"{tag}w0"), k.sem(f"{tag}w1")]
    s_o = [k.sem(f"{tag}o0"), k.sem(f"{tag}o1")]

    xnv = xn_hbm.rearrange("p (c t) -> p c t", c=NDC)
    qkvv = k.dram["qkv_s"].rearrange("p (c t) -> p c t", c=24)
    amps_s = k.dram["amps_s"]
    ov = out_hbm.rearrange("(c p) t -> p c t", p=128)
    wqkv = k.dram["L0a_wqkvT"].rearrange("p (c o) -> p c o", c=NDC)
    wod = k.dram["L0a_woT"]

    def vstep(v, ins):
        k.inc(ins, s_e)
        v.wait_ge(s_e.h, s_e.n)

    def _v1(v):
        v.wait_ge(k.sems["wload"].h, k.sems["wload"].n)
        vstep(v, v.tensor_copy(out=phsc[:, 0:1], in_=phi))
        vstep(v, v.tensor_scalar(out=phsc[:, 1:2], in0=phi, scalar1=HPI,
                                 scalar2=None, op0=OP.add))
        for it in range(4):
            vstep(v, v.tensor_scalar(out=cst[:, 14:16], in0=phsc[:, 0:2],
                                     scalar1=PIF, scalar2=TWOPI,
                                     op0=OP.is_gt, op1=OP.mult))
            vstep(v, v.tensor_tensor(out=phsc[:, 0:2], in0=phsc[:, 0:2],
                                     in1=cst[:, 14:16], op=OP.subtract))
            vstep(v, v.tensor_scalar(out=cst[:, 14:16], in0=phsc[:, 0:2],
                                     scalar1=-PIF, scalar2=TWOPI,
                                     op0=OP.is_lt, op1=OP.mult))
            vstep(v, v.tensor_tensor(out=phsc[:, 0:2], in0=phsc[:, 0:2],
                                     in1=cst[:, 14:16], op=OP.add))
        k.inc(v.reduce_max(out=grow[0:1, 119:120], in_=qwsl[0:1, 4:8],
                           axis=AX.X, negate=True), s_e)
    k.v(_v1)
    v1m = s_e.n

    def _s1(sc, v1m=v1m):
        def step(ins):
            k.inc(ins, s_e)
            sc.wait_ge(s_e.h, s_e.n)
        sc.wait_ge(s_e.h, v1m)
        step(sc.activation(out=phsc[:, 0:2], in_=phsc[:, 0:2], func=AF.Sin))
        step(sc.activation(out=grow[0:1, 0:21], in_=throw, func=AF.Sin,
                           bias=HPI, scale=0.5))
        step(sc.activation(out=grow[0:1, 21:42], in_=throw, func=AF.Sin,
                           bias=0.0, scale=0.5))
        step(sc.activation(out=grow[0:1, 42:63], in_=phrow, func=AF.Sin,
                           bias=HPI, scale=1.0))
        step(sc.activation(out=grow[0:1, 63:84], in_=phrow, func=AF.Sin,
                           bias=0.0, scale=1.0))
        step(sc.activation(out=grow[0:1, 84:88], in_=qwsl[0:1, 0:4], func=AF.Sin))
        step(sc.activation(out=grow[0:1, 122:126], in_=qwsl[0:1, 4:8], func=AF.Exp,
                           bias=grow[0:1, 119:120], scale=1.0,
                           accum_out=grow[0:1, 120:121]))
        step(sc.activation(out=grow[0:1, 121:122], in_=grow[0:1, 120:121],
                           func=AF.Ln))
        step(sc.activation(out=grow[0:1, 121:122], in_=grow[0:1, 121:122],
                           func=AF.Exp, bias=0.0, scale=-1.0))
    k.s(_s1)
    s1m = s_e.n

    def _v2(v, s1m=s1m):
        v.wait_ge(s_e.h, s1m)
        vstep(v, v.tensor_tensor(out=grow[0:1, 96:117], in0=grow[0:1, 21:42],
                                 in1=grow[0:1, 42:63], op=OP.mult))
        vstep(v, v.tensor_tensor(out=prow[0:1, 21:42], in0=grow[0:1, 21:42],
                                 in1=grow[0:1, 63:84], op=OP.mult))
        vstep(v, v.tensor_scalar(out=prow[0:1, 0:21], in0=grow[0:1, 0:21],
                                 scalar1=-1.0, scalar2=None, op0=OP.add))
        vstep(v, v.tensor_tensor(out=prow[0:1, 42:63], in0=prow[0:1, 0:21],
                                 in1=grow[0:1, 96:117], op=OP.subtract))
        vstep(v, v.tensor_tensor(out=prow[0:1, 0:21], in0=prow[0:1, 0:21],
                                 in1=grow[0:1, 96:117], op=OP.add))
        vstep(v, v.tensor_scalar(out=grow[0:1, 88:92], in0=grow[0:1, 84:88],
                                 scalar1=0.3, scalar2=0.5, op0=OP.mult, op1=OP.add))
        vstep(v, v.tensor_scalar(out=grow[0:1, 92:96], in0=grow[0:1, 88:92],
                                 scalar1=-1.0, scalar2=1.0, op0=OP.mult, op1=OP.add))
        vstep(v, v.tensor_tensor(out=prow[0:1, 63:67], in0=grow[0:1, 88:92],
                                 in1=grow[0:1, 92:96], op=OP.mult))
        vstep(v, v.tensor_scalar(out=prow[0:1, 63:67], in0=prow[0:1, 63:67],
                                 scalar1=L0_SCALING, scalar2=None, op0=OP.mult))
        vstep(v, v.tensor_scalar(out=prow[0:1, 67:71], in0=grow[0:1, 122:126],
                                 scalar1=grow[0:1, 121:122], scalar2=None,
                                 op0=OP.mult))
    k.v(_v2)
    v2m = s_e.n

    def _s2(sc, v2m=v2m):
        sc.wait_ge(s_e.h, v2m)
        k.inc(sc.activation(out=prow[0:1, 71:75], in_=prow[0:1, 67:71],
                            func=AF.Ln), s_e)
    k.s(_s2)
    s2m = s_e.n

    def _v3(v, s2m=s2m):
        v.wait_ge(s_e.h, s2m)
        k.inc(v.tensor_scalar(out=prow[0:1, 75:79], in0=prow[0:1, 71:75],
                              scalar1=-13.815510557964274, scalar2=None,
                              op0=OP.add), s_e)
    k.v(_v3)
    v3m = s_e.n

    def _t1(te, v3m=v3m):
        te.wait_ge(s_e.h, v3m)
        k.inc(te.matmul(ps[:, 3584:3663], ones_row[0:1, :], prow[0:1, 0:79],
                        start=True, stop=True), s_t)
    k.t(_t1)
    t1m = s_t.n

    def _v4(v, t1m=t1m):
        v.wait_ge(s_t.h, t1m)
        k.inc(v.tensor_copy(out=cst[:, 0:79], in_=ps[:, 3584:3663]), s_e)
    k.v(_v4)
    preludem = s_e.n
    APa = lambda dj: cst[:, dj:dj + 1]
    APb = lambda dj: cst[:, 21 + dj:22 + dj]
    APg = lambda dj: cst[:, 42 + dj:43 + dj]
    APk = lambda c: cst[0:8, 63 + c:64 + c]
    APsw = lambda c: cst[:, 67 + c:68 + c]
    APb2 = lambda c: cst[:, 71 + c:72 + c]
    APb1 = lambda c: cst[:, 75 + c:76 + c]
    APsin = phsc[:, 0:1]
    APcos = phsc[:, 1:2]

    slabq = [(k.sb["slab0"][:, 0:1024], k.sem(f"{tag}sa")),
             (k.sb["slab0"][:, 1024:2048], k.sem(f"{tag}sb")),
             (k.sb["slab1"][:, 0:1024], k.sem(f"{tag}sc")),
             (k.sb["slab1"][:, 1024:2048], k.sem(f"{tag}sd"))]
    tmk = {}
    LQ = [0]

    def stage_load(src3, cidx):
        L = LQ[0]
        LQ[0] += 1
        slab, sem = slabq[L % 4]
        pm = tmk.get(L - 4)

        def _g(gq, src3=src3, cidx=cidx, slab=slab, sem=sem, pm=pm):
            if pm:
                gq.wait_ge(s_t.h, pm)
            k.dinc(gq.dma_start(
                out=slab.rearrange("p (c o) -> p c o", c=NDC),
                in_=src3[:, :, cidx * 128:(cidx + 1) * 128]), sem)
        k.g(_g)
        return (L, slab, sem, sem.n)

    for i in range(nch):
        t0 = i * TC
        xwm = s_t.n

        def _gx(gq, t0=t0, xwm=xwm):
            if xwm:
                gq.wait_ge(s_t.h, xwm)
            k.dinc(gq.dma_start(out=pbufs[0].rearrange("p (c t) -> p c t", c=NDC),
                                in_=xnv[:, :, t0:t0 + TC]), s_x)
        k.g(_gx)
        xm = s_x.n
        for oc in range(24):
            ld = stage_load(wqkv, oc)
            b0 = 512 * (oc % 2)
            evp = s_e.n

            def _tq(te, ld=ld, b0=b0, xm=xm, evp=evp, oc=oc):
                te.wait_ge(ld[2].h, ld[3])
                te.wait_ge(s_x.h, xm)
                if evp:
                    te.wait_ge(s_e.h, evp)
                k.inc(mm_acc(te, ps[:, b0:b0 + 512],
                             [(ld[1][:, c * 128:(c + 1) * 128],
                               pbufs[0][:, c * TC:(c + 1) * TC])
                              for c in range(NDC)]), s_t)
            k.t(_tq)
            tmk[ld[0]] = s_t.n
            tqm = s_t.n
            am = s_a.n

            def _sq(sc, oc=oc, b0=b0, tqm=tqm, am=am, preludem=preludem):
                sc.wait_ge(s_t.h, tqm)
                sc.wait_ge(s_e.h, preludem)
                if am:
                    sc.wait_ge(s_a.h, am)
                k.inc(sc.activation(out=stg[oc % 2], in_=ps[:, b0:b0 + 512],
                                    func=AF.Identity, bias=bqkv[:, oc:oc + 1],
                                    scale=1.0), s_e)
            k.s(_sq)
            sqm = s_e.n

            def _gs(gq, oc=oc, t0=t0, sqm=sqm):
                gq.wait_ge(s_e.h, sqm)
                k.dinc(gq.dma_start(out=qkvv[:, oc, t0:t0 + TC], in_=stg[oc % 2]),
                       s_a)
            k.g(_gs)
    k.barrier()

    def _gw(gq):
        k.dinc(gq.dma_start(out=k.sb["slab0"][:], in_=wod[:, 0:4096]), s_w[0])
        k.dinc(gq.dma_start(out=k.sb["slab1"][:], in_=wod[:, 4096:8192]), s_w[0])
    k.g(_gw)
    k.barrier()
    woslab = [k.sb["slab0"], k.sb["slab1"]]

    for ic in range(nch):
        t0 = ic * TC

        def _gl(gq, t0=t0):
            k.dinc(gq.dma_start(out=pbufs[0].rearrange("p (c t) -> p c t", c=NDC),
                                in_=qkvv[:, 0:8, t0:t0 + TC]), s_x)
            k.dinc(gq.dma_start(out=pbufs[1].rearrange("p (c t) -> p c t", c=NDC),
                                in_=qkvv[:, 8:16, t0:t0 + TC]), s_x)
        k.g(_gl)
        qkm = s_x.n

        def enc_part(part, pb, ldm):
            for h in range(8):
                sprev = s_e.n

                def _s(sc, h=h, pb=pb, ldm=ldm, sprev=sprev):
                    sc.wait_ge(s_x.h, ldm)
                    if sprev:
                        sc.wait_ge(s_e.h, sprev)
                    k.inc(sc.activation(out=sqb, in_=pb[:, h * TC:(h + 1) * TC],
                                        func=AF.Square), s_e)
                k.s(_s)
                sm = s_e.n
                tprev = s_t.n

                def _t(te, h=h, sm=sm, tprev=tprev, preludem=preludem):
                    te.wait_ge(s_e.h, max(sm, preludem))
                    if tprev:
                        te.wait_ge(s_t.h, tprev)
                    k.inc(te.matmul(ps[0:1, 3584:3584 + 512], onescb[:, 0:1], sqb,
                                    start=True, stop=True), s_t)
                k.t(_t)
                tm = s_t.n

                def _v(v, tm=tm):
                    v.wait_ge(s_t.h, tm)
                    k.inc(v.tensor_scalar(out=fac, in0=ps[0:1, 3584:3584 + 512],
                                          scalar1=1e-24, scalar2=None, op0=OP.max),
                          s_e)
                k.v(_v)
                vm = s_e.n

                def _s2(sc, vm=vm):
                    sc.wait_ge(s_e.h, vm)
                    act_chain(k, sc, s_e,
                              sc.activation(out=fac, in_=fac, func=AF.Ln))
                    act_chain(k, sc, s_e,
                              sc.activation(out=fac, in_=fac, func=AF.Exp,
                                            bias=0.0, scale=-0.5))
                    k.inc(sc.activation(out=facb, in_=fac, func=AF.Identity), s_e)
                k.s(_s2)
                s2m = s_e.n

                def _t2(te, s2m=s2m):
                    te.wait_ge(s_e.h, s2m)
                    k.inc(te.matmul(ps[:, 3072:3072 + 512], onesb[0:1, :], facb,
                                    start=True, stop=True), s_t)
                k.t(_t2)
                t2m = s_t.n

                if part == 'q':
                    outs = [(QR[:, h * TC:(h + 1) * TC], APcos),
                            (QI[:, h * TC:(h + 1) * TC], APsin)]
                elif part == 'k':
                    outs = [(KR[:, h * TC:(h + 1) * TC], APcos),
                            (KI[:, h * TC:(h + 1) * TC], APsin)]
                else:
                    outs = [(cap_ap(vre, [[32768, 128], [8, TC]], offset=h), APcos),
                            (cap_ap(vim, [[32768, 128], [8, TC]], offset=4096 + h),
                             APsin)]

                def _v2b(v, h=h, pb=pb, outs=outs, t2m=t2m, part=part):
                    v.wait_ge(s_t.h, t2m)
                    for o_ap, ph_ap in outs:
                        vstep(v, v.scalar_tensor_tensor(
                            out=o_ap, in0=pb[:, h * TC:(h + 1) * TC],
                            scalar=ph_ap, in1=ps[:, 3072:3072 + 512],
                            op0=OP.mult, op1=OP.mult))
                k.v(_v2b)

        enc_part('q', pbufs[0], qkm)
        enc_part('k', pbufs[1], qkm)
        qdone = s_e.n

        def _gv(gq, t0=t0, qdone=qdone):
            gq.wait_ge(s_e.h, qdone)
            k.dinc(gq.dma_start(out=pbufs[0].rearrange("p (c t) -> p c t", c=NDC),
                                in_=qkvv[:, 16:24, t0:t0 + TC]), s_x)
        k.g(_gv)
        vm2 = s_x.n
        enc_part('v', pbufs[0], vm2)
        encdone = s_e.n

        for (RE, IM) in ((QR, QI), (KR, KI)):
            for d in range(3):
                for j in range(7):
                    dj = d * 7 + j
                    a = RE[:, (j + 1) * TC:(j + 2) * TC]
                    b = IM[:, (j + 1) * TC:(j + 2) * TC]
                    rj = RE[:, j * TC:(j + 1) * TC]
                    ij = IM[:, j * TC:(j + 1) * TC]
                    em0 = s_e.n
                    pm0 = s_p.n

                    def _sc(sc, rj=rj, em0=em0, pm0=pm0):
                        sc.wait_ge(s_e.h, em0)
                        if pm0:
                            sc.wait_ge(s_p.h, pm0)
                        k.inc(sc.activation(out=r2b, in_=rj, func=AF.Square), s_e)
                    k.s(_sc)
                    scm = s_e.n

                    def _vc(v, ij=ij, a=a, b=b, dj=dj, em0=em0, pm0=pm0):
                        if pm0:
                            v.wait_ge(s_p.h, pm0)
                        v.wait_ge(s_e.h, em0)
                        vstep(v, v.tensor_tensor(out=eab, in0=ij, in1=ij,
                                                 op=OP.mult))
                        vstep(v, v.tensor_scalar(out=ftm, in0=a, scalar1=APa(dj),
                                                 scalar2=None, op0=OP.mult))
                        k.inc(v.scalar_tensor_tensor(out=ftm, in0=b, scalar=APb(dj),
                                                     in1=ftm, op0=OP.mult,
                                                     op1=OP.add), s_e)
                    k.v(_vc)
                    vcm = s_e.n

                    def _pc(g, scm=scm, vcm=vcm):
                        g.wait_ge(s_e.h, max(scm, vcm))
                        k.inc(g.tensor_tensor(out=ebb, in0=r2b, in1=eab,
                                              op=OP.add), s_p)
                        g.wait_ge(s_p.h, s_p.n)
                        k.inc(g.tensor_tensor(out=ftm, in0=ebb, in1=ftm,
                                              op=OP.mult), s_p)
                    k.g(_pc)
                    pcm = s_p.n

                    def _vd(v, rj=rj, a=a, b=b, dj=dj, pcm=pcm):
                        v.wait_ge(s_p.h, pcm)
                        vstep(v, v.tensor_tensor(out=rj, in0=a, in1=ftm, op=OP.add))
                        vstep(v, v.tensor_scalar(out=ftm, in0=a, scalar1=APb(dj),
                                                 scalar2=None, op0=OP.mult))
                        k.inc(v.scalar_tensor_tensor(out=ftm, in0=b, scalar=APg(dj),
                                                     in1=ftm, op0=OP.mult,
                                                     op1=OP.add), s_e)
                    k.v(_vd)
                    vdm = s_e.n

                    def _pd(g, vdm=vdm):
                        g.wait_ge(s_e.h, vdm)
                        k.inc(g.tensor_tensor(out=ftm, in0=ebb, in1=ftm,
                                              op=OP.mult), s_p)
                    k.g(_pd)
                    pdm = s_p.n

                    def _ve(v, ij=ij, b=b, pdm=pdm):
                        v.wait_ge(s_p.h, pdm)
                        k.inc(v.tensor_tensor(out=ij, in0=b, in1=ftm, op=OP.add),
                              s_e)
                    k.v(_ve)
        entneg = s_e.n

        def _vneg(v, entneg=entneg):
            v.wait_ge(s_e.h, entneg)
            for h in range(8):
                vstep(v, v.tensor_scalar(out=qnb[:, h * TC:(h + 1) * TC],
                                         in0=QI[:, h * TC:(h + 1) * TC],
                                         scalar1=-1.0, scalar2=None, op0=OP.mult))
        k.v(_vneg)
        entdone = s_e.n

        def _vz(v, entdone=entdone):
            v.wait_ge(s_e.h, entdone)
            v.wait_ge(s_p.h, s_p.n)
            for (Z, V, off) in ((zr, vre, 0), (zi, vim, 4096)):
                vstep(v, v.tensor_copy(
                    out=Z, in_=cap_ap(V, [[32768, 128], [8, TC]], offset=off)))
                for j in range(1, 8):
                    vstep(v, v.tensor_tensor(
                        out=Z, in0=Z,
                        in1=cap_ap(V, [[32768, 128], [8, TC]], offset=off + j),
                        op=OP.add))
            vstep(v, v.tensor_tensor(out=zt, in0=zr, in1=zr, op=OP.mult))
            vstep(v, v.tensor_tensor(out=ftm, in0=zi, in1=zi, op=OP.mult))
            vstep(v, v.tensor_tensor(out=zt, in0=zt, in1=ftm, op=OP.add))
        k.v(_vz)
        vzm = s_e.n

        def _sz(sc, vzm=vzm):
            sc.wait_ge(s_e.h, vzm)
            act_chain(k, sc, s_e, sc.activation(out=zt, in_=zt, func=AF.Ln,
                                                bias=1e-38, scale=1.0))
            k.inc(sc.activation(out=zt, in_=zt, func=AF.Exp, bias=0.0, scale=-0.5),
                  s_e)
        k.s(_sz)
        szm = s_e.n

        def _vz4(v, szm=szm):
            v.wait_ge(s_e.h, szm)
            vstep(v, v.tensor_tensor(out=zr, in0=zr, in1=zt, op=OP.mult))
            vstep(v, v.tensor_tensor(out=zi, in0=zi, in1=zt, op=OP.mult))
        k.v(_vz4)
        cohdone = s_e.n

        for pc in range(4):
            pt0 = pc * 128
            pevac = s_e.n
            for j in range(8):
                qr3 = cap_ap(QR, [[16384, 128], [1, 128], [TC, 8]], offset=pt0)
                qi3 = cap_ap(QI, [[16384, 128], [1, 128], [TC, 8]],
                             offset=4096 + pt0)
                qn3 = cap_ap(qnb, [[32768, 128], [1, 128], [TC, 8]],
                             offset=8192 + pt0)
                kr0 = cap_ap(KR, [[16384, 128], [1, 128], [0, 8]],
                             offset=8192 + j * TC + pt0)
                ki0 = cap_ap(KI, [[16384, 128], [1, 128], [0, 8]],
                             offset=12288 + j * TC + pt0)
                po = [cap_ap(ARB[:, :], [[32768, 128], [8, 128], [1, 8]],
                             offset=12288 + n * 1024) for n in range(4)]
                tpm = s_t.n

                def _vpj(v, qr3=qr3, kr0=kr0, ki0=ki0, po=po, tpm=tpm,
                         entdone=entdone):
                    v.wait_ge(s_e.h, entdone)
                    if tpm:
                        v.wait_ge(s_t.h, tpm)
                    vstep(v, v.tensor_tensor(out=po[0], in0=qr3, in1=kr0,
                                             op=OP.mult))
                    k.inc(v.tensor_tensor(out=po[2], in0=qr3, in1=ki0,
                                          op=OP.mult), s_e)
                k.v(_vpj)
                vjm = s_e.n

                def _ppj(g, qi3=qi3, qn3=qn3, kr0=kr0, ki0=ki0, po=po, tpm=tpm,
                         entdone=entdone):
                    g.wait_ge(s_e.h, entdone)
                    if tpm:
                        g.wait_ge(s_t.h, tpm)
                    k.inc(g.tensor_tensor(out=po[1], in0=qi3, in1=ki0,
                                          op=OP.mult), s_p)
                    g.wait_ge(s_p.h, s_p.n)
                    k.inc(g.tensor_tensor(out=po[3], in0=qn3, in1=kr0,
                                          op=OP.mult), s_p)
                k.g(_ppj)
                pjm = s_p.n

                def _tpj(te, j=j, vjm=vjm, pjm=pjm, pevac=pevac):
                    te.wait_ge(s_e.h, vjm)
                    te.wait_ge(s_p.h, pjm)
                    if j == 0 and pevac:
                        te.wait_ge(s_e.h, pevac)
                    last = None
                    for half in range(2):
                        sl = slice(half * 512, half * 512 + 512)
                        te.matmul(ps[0:8, half * 512:half * 512 + 512],
                                  sel8c[:, j * 8:(j + 1) * 8],
                                  prods[:, 0:1024][:, sl],
                                  start=(j == 0), stop=False)
                        te.matmul(ps[0:8, half * 512:half * 512 + 512],
                                  sel8c[:, j * 8:(j + 1) * 8],
                                  prods[:, 1024:2048][:, sl],
                                  start=False, stop=(j == 7))
                        te.matmul(ps[0:8, 1024 + half * 512:1536 + half * 512],
                                  sel8c[:, j * 8:(j + 1) * 8],
                                  prods[:, 2048:3072][:, sl],
                                  start=(j == 0), stop=False)
                        last = te.matmul(
                            ps[0:8, 1024 + half * 512:1536 + half * 512],
                            sel8c[:, j * 8:(j + 1) * 8],
                            prods[:, 3072:4096][:, sl],
                            start=False, stop=(j == 7))
                    k.inc(last, s_t)
                k.t(_tpj)
            pallm = s_t.n

            def _sS(sc, pallm=pallm):
                sc.wait_ge(s_t.h, pallm)
                act_chain(k, sc, s_e, sc.activation(out=Sbuf, in_=ps[0:8, 0:1024],
                                                    func=AF.Square))
                k.inc(sc.activation(out=sqf[0:8, :], in_=ps[0:8, 1024:2048],
                                    func=AF.Square), s_e)
            k.s(_sS)
            sSm = s_e.n

            def _vS(v, sSm=sSm):
                v.wait_ge(s_e.h, sSm)
                k.inc(v.tensor_tensor(out=Sbuf, in0=Sbuf, in1=sqf[0:8, :],
                                      op=OP.add), s_e)
            k.v(_vS)
            Sm = s_e.n

            for c in range(4):
                prevsA = s_e.n

                def _sE(sc, c=c, Sm=Sm, prevsA=prevsA):
                    sc.wait_ge(s_e.h, max(Sm, prevsA))
                    k.inc(sc.activation(out=Ec, in_=Sbuf, func=AF.Exp,
                                        bias=0.0, scale=APk(c)), s_e)
                k.s(_sE)
                sEm = s_e.n
                tc0 = s_t.n

                def _tE(te, sEm=sEm, tc0=tc0):
                    te.wait_ge(s_e.h, sEm)
                    if tc0:
                        te.wait_ge(s_t.h, tc0)
                    for half in range(2):
                        k.inc(te.matmul(ps[0:1, half * 512:half * 512 + 512],
                                        onescb[0:8, 0:1],
                                        Ec[:, half * 512:half * 512 + 512],
                                        start=True, stop=True), s_t)
                k.t(_tE)
                tEm = s_t.n

                def _sR(sc, tEm=tEm):
                    sc.wait_ge(s_t.h, tEm)
                    act_chain(k, sc, s_e,
                              sc.activation(out=rcrow, in_=ps[0:1, 0:1024],
                                            func=AF.Ln))
                    act_chain(k, sc, s_e,
                              sc.activation(out=rcrow, in_=rcrow, func=AF.Exp,
                                            bias=0.0, scale=-1.0))
                    k.inc(sc.activation(out=rcb16, in_=rcrow, func=AF.Identity),
                          s_e)
                k.s(_sR)
                sRm = s_e.n

                def _tR(te, sRm=sRm):
                    te.wait_ge(s_e.h, sRm)
                    for half in range(2):
                        k.inc(te.matmul(ps[0:8, 1024 + half * 512:1536 + half * 512],
                                        onesb[0:1, 0:8],
                                        rcb16[0:1, half * 512:half * 512 + 512],
                                        start=True, stop=True), s_t)
                k.t(_tR)
                tRm = s_t.n

                def _vA(v, tRm=tRm):
                    v.wait_ge(s_t.h, tRm)
                    k.inc(v.tensor_tensor(out=Ec, in0=Ec, in1=ps[0:8, 1024:2048],
                                          op=OP.mult), s_e)
                k.v(_vA)
                vAm = s_e.n
                am0 = s_a.n

                def _sA(sc, c=c, vAm=vAm, am0=am0):
                    sc.wait_ge(s_e.h, vAm)
                    if am0:
                        sc.wait_ge(s_a.h, am0)
                    k.inc(sc.activation(
                        out=cap_ap(ampsl4, [[8192, 8], [32, 128], [1, 8]],
                                   offset=4096 + c * 8),
                        in_=Ec.rearrange("p (a b) -> p a b", a=128),
                        func=AF.Sqrt, bias=1e-12, scale=1.0), s_e)
                k.s(_sA)
                sAm = s_e.n
            aw4 = s_e.n

            def _gA(gq, pc=pc, aw4=aw4, ic=ic):
                gq.wait_ge(s_e.h, aw4)
                RP = (T // 128) * 256
                for j in range(8):
                    k.dinc(gq.dma_start(
                        out=cap_ap(amps_s[0:1, :],
                                   [[32, 8], [8 * RP, 16], [1, 32]],
                                   offset=j * RP + (ic * 4 + pc) * 256),
                        in_=cap_ap(ampsl4, [[8192, 1], [512, 8], [32, 16],
                                            [1, 32]],
                                   offset=4096 + j * 8192)), s_a)
            k.g(_gA)
            amall = s_a.n

            jw0 = s_e.n

            def _yj(yq, pc=pc, amall=amall, jw0=jw0, ic=ic):
                yq.wait_ge(s_a.h, amall)
                yq.wait_ge(s_e.h, jw0)
                base = (ic * 4 + pc) * 256
                k.dinc(yq.dma_start(out=ajm,
                                    in_=amps_s[:, base:base + 256]), s_j)
            k.y(_yj)
            jm = s_j.n

            for g in range(8):
                base = pc * 1024 + g * 128
                slot = 6144 + (g % 2) * 512
                ep0 = s_e.n

                def _tv(te, base=base, slot=slot, ep0=ep0, encdone=encdone):
                    te.wait_ge(s_e.h, max(encdone, ep0))
                    te.transpose(psb[:, slot:slot + 128],
                                 vre[:, base:base + 128], idb[:])
                    k.inc(te.transpose(psb[:, slot + 128:slot + 256],
                                       vim[:, base:base + 128], idb[:]), s_t)
                k.t(_tv)
                tvm = s_t.n

                def _sv(sc, g=g, slot=slot, tvm=tvm):
                    sc.wait_ge(s_t.h, tvm)
                    k.inc(sc.activation(out=vqhp[:, g * 256:(g + 1) * 256],
                                        in_=psb[:, slot:slot + 256],
                                        func=AF.Identity), s_e)
                k.s(_sv)
            vqall = s_e.n

            for c in range(4):
                for q in range(2):
                    for gg in range(4):
                        g = q * 4 + gg
                        bpar = blkb[g % 2]
                        bw0 = s_t.n

                        def _vb(v, g=g, c=c, bpar=bpar, jm=jm, bw0=bw0):
                            v.wait_ge(s_j.h, jm)
                            if bw0:
                                v.wait_ge(s_t.h, bw0)
                            k.inc(v.tensor_tensor(
                                out=bpar.rearrange("p (a b) -> p a b", a=8),
                                in0=mskb.rearrange("p (a b) -> p a b", a=8),
                                in1=cap_ap(ARC[:, :], [[16384, 128], [1, 8], [0, 16]],
                                           offset=2560 + g * 32 + c * 8),
                                op=OP.mult), s_e)
                        k.v(_vb)
                        vbm = s_e.n
                        fprev = s_e.n

                        def _tw(te, gg=gg, g=g, bpar=bpar, vbm=vbm,
                                vqall=vqall, fprev=fprev):
                            te.wait_ge(s_e.h, max(vbm, vqall, fprev))
                            k.inc(te.matmul(
                                ps[:, 2048 + gg * 256:2048 + gg * 256 + 256],
                                bpar, vqhp[:, g * 256:(g + 1) * 256],
                                start=True, stop=True), s_t)
                        k.t(_tw)
                    wqm = s_t.n
                    A4 = Aac[:, q * 512:(q + 1) * 512]
                    B4 = Bac[:, q * 512:(q + 1) * 512]

                    def _sf(sc, wqm=wqm):
                        sc.wait_ge(s_t.h, wqm)
                        k.inc(sc.activation(out=sqf, in_=ps[:, 2048:3072],
                                            func=AF.Square), s_e)
                    k.s(_sf)
                    sfm = s_e.n

                    def _vf(v, sfm=sfm):
                        v.wait_ge(s_e.h, sfm)
                        k.inc(v.tensor_tensor(
                            out=r2b.rearrange("p (a b) -> p a b", a=4),
                            in0=cap_ap(sqf, [[8192, 128], [256, 4], [1, 128]],
                                       offset=4992),
                            in1=cap_ap(sqf, [[8192, 128], [256, 4], [1, 128]],
                                       offset=4992 + 128),
                            op=OP.add), s_e)
                    k.v(_vf)
                    vfm = s_e.n

                    def _sg(sc, c=c, vfm=vfm):
                        sc.wait_ge(s_e.h, vfm)
                        act_chain(k, sc, s_e,
                                  sc.activation(out=r2b, in_=r2b, func=AF.Ln,
                                                bias=1e-38, scale=1.0))
                        act_chain(k, sc, s_e,
                                  sc.activation(out=eab, in_=r2b, func=AF.Exp,
                                                bias=APb1(c), scale=-0.5))
                        k.inc(sc.activation(out=ebb, in_=r2b, func=AF.Exp,
                                            bias=APb2(c), scale=-0.5), s_e)
                    k.s(_sg)
                    sgm = s_e.n

                    pw0 = s_p.n

                    def _vg(v, c=c, sgm=sgm, pw0=pw0):
                        v.wait_ge(s_e.h, sgm)
                        if pw0:
                            v.wait_ge(s_p.h, pw0)
                        vstep(v, v.tensor_scalar(out=ebb, in0=ebb,
                                                 scalar1=APsw(c), scalar2=None,
                                                 op0=OP.min))
                        vstep(v, v.tensor_tensor(out=ebb, in0=ebb, in1=eab,
                                                 op=OP.max))
                        vstep(v, v.tensor_tensor(
                            out=ftm.rearrange("p (a b) -> p a b", a=4),
                            in0=ebb.rearrange("p (a b) -> p a b", a=4),
                            in1=cap_ap(ps[:, :], [[4096, 128], [256, 4], [1, 128]],
                                       offset=2048),
                            op=OP.mult))
                    k.v(_vg)
                    vgm = s_e.n

                    def _pA(g_, c=c, A4=A4, vgm=vgm):
                        g_.wait_ge(s_e.h, vgm)
                        if c == 0:
                            k.inc(g_.tensor_copy(out=A4, in_=ftm), s_p)
                        else:
                            k.inc(g_.tensor_tensor(out=A4, in0=A4, in1=ftm,
                                                   op=OP.add), s_p)
                    k.g(_pA)
                    pAm = s_p.n

                    def _vh(v, pAm=pAm):
                        v.wait_ge(s_p.h, pAm)
                        k.inc(v.tensor_tensor(
                            out=ftm.rearrange("p (a b) -> p a b", a=4),
                            in0=ebb.rearrange("p (a b) -> p a b", a=4),
                            in1=cap_ap(ps[:, :], [[4096, 128], [256, 4], [1, 128]],
                                       offset=2048 + 128),
                            op=OP.mult), s_e)
                    k.v(_vh)
                    vhm = s_e.n

                    def _pB(g_, c=c, B4=B4, vhm=vhm):
                        g_.wait_ge(s_e.h, vhm)
                        if c == 0:
                            k.inc(g_.tensor_copy(out=B4, in_=ftm), s_p)
                        else:
                            k.inc(g_.tensor_tensor(out=B4, in0=B4, in1=ftm,
                                                   op=OP.add), s_p)
                    k.g(_pB)
            fall = s_p.n

            for g in range(8):
                slot = 2048 + (g % 4) * 128
                tp0 = s_e.n

                def _ta(te, g=g, slot=slot, fall=fall, tp0=tp0):
                    te.wait_ge(s_p.h, fall)
                    te.wait_ge(s_e.h, tp0)
                    te.transpose(ps[:, slot:slot + 128],
                                 Aac[:, g * 128:(g + 1) * 128], idf[:])
                    k.inc(te.transpose(ps[:, slot + 1024:slot + 1152],
                                       Bac[:, g * 128:(g + 1) * 128], idf[:]), s_t)
                k.t(_ta)
                tam = s_t.n

                def _va(v, g=g, slot=slot, tam=tam):
                    v.wait_ge(s_t.h, tam)
                    vstep(v, v.tensor_copy(out=ATb[:, g * 128:(g + 1) * 128],
                                           in_=ps[:, slot:slot + 128]))
                    k.inc(v.tensor_copy(out=BTb[:, g * 128:(g + 1) * 128],
                                        in_=ps[:, slot + 1024:slot + 1152]), s_e)
                k.v(_va)
            atall = s_e.n

            for i8 in range(8):
                zr3 = cap_ap(zr, [[16384, 128], [16, 8], [1, 16]],
                             offset=14336 + pt0)
                zi3 = cap_ap(zi, [[16384, 128], [16, 8], [1, 16]],
                             offset=14848 + pt0)
                at3 = cap_ap(ATb, [[16384, 128], [128, 8], [1, 16]],
                             offset=12288 + i8 * 16)
                bt3 = cap_ap(BTb, [[16384, 128], [128, 8], [1, 16]],
                             offset=13312 + i8 * 16)
                mm0 = s_p.n

                def _vmm(v, zr3=zr3, zi3=zi3, at3=at3, bt3=bt3,
                         atall=atall, cohdone=cohdone, mm0=mm0):
                    v.wait_ge(s_e.h, max(atall, cohdone))
                    if mm0:
                        v.wait_ge(s_p.h, mm0)
                    vstep(v, v.tensor_tensor(
                        out=r2b[:, 0:128].rearrange("p (a b) -> p a b", a=8),
                        in0=zr3, in1=at3, op=OP.mult))
                    k.inc(v.tensor_tensor(
                        out=r2b[:, 128:256].rearrange("p (a b) -> p a b", a=8),
                        in0=zi3, in1=bt3, op=OP.mult), s_e)
                k.v(_vmm)
                vmmm = s_e.n

                def _pm(g_, i8=i8, pt0=pt0, vmmm=vmmm):
                    g_.wait_ge(s_e.h, vmmm)
                    k.inc(g_.tensor_tensor(
                        out=meas[:, i8 * 512 + pt0:i8 * 512 + pt0 + 128],
                        in0=r2b[:, 0:128], in1=r2b[:, 128:256],
                        op=OP.subtract), s_p)
                k.g(_pm)
        measall_e = s_e.n
        measall_p = s_p.n

        for oc in range(8):
            b0 = (oc % 2) * 512
            ew0 = s_e.n

            def _tw2(te, oc=oc, b0=b0, measall_e=measall_e,
                     measall_p=measall_p, ew0=ew0):
                te.wait_ge(s_p.h, measall_p)
                te.wait_ge(s_e.h, max(measall_e, ew0))
                te.wait_ge(s_w[0].h, s_w[0].n)
                last = None
                for kc in range(8):
                    lh = woslab[kc // 4][:, (kc % 4) * 1024 + oc * 128:
                                        (kc % 4) * 1024 + oc * 128 + 128]
                    last = te.matmul(ps[:, b0:b0 + 512], lh,
                                     meas[:, kc * 512:(kc + 1) * 512],
                                     start=(kc == 0), stop=(kc == 7))
                k.inc(last, s_t)
            k.t(_tw2)
            tw2m = s_t.n
            om0 = s_o[ic % 2].n

            def _so2(sc, oc=oc, b0=b0, tw2m=tw2m, om0=om0, ic=ic):
                sc.wait_ge(s_t.h, tw2m)
                if om0:
                    sc.wait_ge(s_o[ic % 2].h, om0)
                dst = zr if oc % 2 == 0 else zi
                k.inc(sc.activation(out=dst, in_=ps[:, b0:b0 + 512],
                                    func=AF.Identity, bias=bo[:, oc:oc + 1],
                                    scale=1.0), s_e)
            k.s(_so2)
            so2m = s_e.n

            def _go2(gq, oc=oc, t0=t0, so2m=so2m, ic=ic):
                gq.wait_ge(s_e.h, so2m)
                src = zr if oc % 2 == 0 else zi
                k.dinc(gq.dma_start(out=ov[:, oc, t0:t0 + TC], in_=src),
                       s_o[ic % 2])
            k.g(_go2)
        k.barrier()
    return s_o


def l0_attn_host(k, inputs):
    wqkv = np.asarray(inputs["L0_attn_W_qkv"], np.float32)
    bqkv = np.asarray(inputs["L0_attn_b_qkv"], np.float32)
    wout = np.asarray(inputs["L0_attn_W_out"], np.float32)
    bout = np.asarray(inputs["L0_attn_b_out"], np.float32)
    qw = np.asarray(inputs["L0_attn_quantum_weights"], np.float32)
    phph = np.asarray(inputs["L0_attn_photonic_phases"], np.float32)
    gates = np.asarray(inputs["L0_attn_entanglement_gates"], np.float32)
    sl = np.asarray(inputs["L0_attn_superposition_logits"], np.float32)
    k.weight("L0a_wqkvT", wimg_lhsT(np.ascontiguousarray(wqkv.T)))
    k.weight("L0a_woT", wimg_lhsT(np.ascontiguousarray(wout.T)))
    k.weight("L0a_bqkv", bimg(bqkv))
    k.weight("L0a_bo", bimg(bout))
    ch = np.arange(128)
    k.weight("L0a_phi", np.ascontiguousarray(phph[ch % 16, ch]).reshape(128, 1))
    k.weight("L0a_th", np.ascontiguousarray(gates[:, :7, 0]).reshape(1, 21))
    k.weight("L0a_ph", np.ascontiguousarray(gates[:, :7, 1]).reshape(1, 21))
    k.weight("L0a_qwsl", np.concatenate([qw[:, 0, 0], sl]).reshape(1, 8))
    msk = np.zeros((128, 128), np.float32)
    for t in range(16):
        for j in range(8):
            for i in range(8):
                msk[t * 8 + j, i * 16 + t] = 1.0
    k.weight("L0a_maskblk", msk)
    sel = np.zeros((128, 64), np.float32)
    for j in range(8):
        sel[:, j * 8 + j] = 1.0
    k.weight("L0a_sel8c", sel)


L0A_SMALL_W = []
L0A_ALIAS_W = ["L0a_bqkv", "L0a_bo", "L0a_maskblk", "L0a_sel8c", "L0a_th",
               "L0a_ph", "L0a_phi", "L0a_qwsl"]



def emit_layer0(k, x_hbm, xout_hbm):
    emit_ln(k, x_hbm, k.dram["xn_s"], k.dram["stats_s"], "L0")
    k.barrier()
    emit_l0_attn(k, k.dram["xn_s"], k.dram["att_s"])
    k.barrier()
    emit_load_w(k, ["L0f_sel8r", "L0f_bout8"])
    k.barrier()
    emit_l0_ffn(k, x_hbm, k.dram["xn_s"], k.dram["stats_s"], k.dram["ffn_s"])
    k.barrier()
    return emit_add3(k, x_hbm, k.dram["att_s"], k.dram["ffn_s"], xout_hbm, "L0")


def alloc_l1a_alias(k):
    k.sb["sb_L1a_wg1T"] = k.sb["ARD"][:, 2560:3072]


def emit_layer1(k, x_hbm, xout_hbm):
    emit_ln(k, x_hbm, k.dram["xn_s"], k.dram["stats_s"], "L1")
    k.barrier()
    emit_load_w(k, ["L1a_wg1T"])
    k.barrier()
    emit_l1_attn(k, k.dram["xn_s"], k.dram["att_s"])
    k.barrier()
    emit_l1_ffn_attn(k, k.dram["xn_s"], k.dram["x1_s"])
    k.barrier()
    emit_l1_ffn_moe(k, k.dram["x1_s"], k.dram["ffn_s"])
    k.barrier()
    return emit_add3(k, x_hbm, k.dram["att_s"], k.dram["ffn_s"], xout_hbm, "L1")


def emit_layer2(k, x_hbm, xout_hbm):
    emit_ln(k, x_hbm, k.dram["xn_s"], k.dram["stats_s"], "L2")
    k.barrier()
    emit_load_w(k, ["L2_wBT", "L2_wCT"])
    k.barrier()
    emit_l2_attn(k, k.dram["xn_s"], k.dram["att_s"])
    k.barrier()
    emit_l2_ffn(k, x_hbm, k.dram["xn_s"], k.dram["stats_s"], k.dram["ffn_s"])
    k.barrier()
    emit_l2_ffn_compute(k, k.dram["xn_s"], k.dram["ffn_s"])
    k.barrier()
    return emit_add3(k, x_hbm, k.dram["att_s"], k.dram["ffn_s"], xout_hbm, "L2")


def _dedupe(ws):
    seen = {}
    for n, dt in ws:
        seen.setdefault(n, dt)
    return list(seen.items())


def build_layer(L, inputs, T=4096):
    k = K(T)
    alloc_common(k)
    host_consts(k)
    if L == 0:
        l0_attn_host(k, inputs)
        l0_ffn_host(k, inputs)
        sw = _dedupe(L0A_SMALL_W + L0F_SMALL_W + COMMON_SMALL_W)
        alias = L0A_ALIAS_W + ["L0f_sel8r"]
    elif L == 1:
        l1_attn_host(k, inputs)
        l1_ffn_host(k, inputs)
        sw = _dedupe(L1A_SMALL_W + L1F_SMALL_W + COMMON_SMALL_W)
        alias = ["L1a_wg1T"]
    elif L == 2:
        l2_attn_host(k, inputs)
        l2_ffn_host(k, inputs)
        sw = _dedupe(L2A_SMALL_W + L2F_SMALL_W + COMMON_SMALL_W)
        alias = list(L2A_ALIAS_W)
    else:
        l3_attn_host(k, inputs)
        l3_ffn_host(k, inputs)
        sw = _dedupe(L3_SMALL_W + COMMON_SMALL_W)
        alias = []
    alloc_w_sb(k, sw)
    if L == 0:
        alloc_l0a_alias(k)
        alloc_l0f_alias(k)
        k.dram_tmp("qkv_s", [128, 24 * T], BF16)
        k.dram_tmp("amps_s", [128, (T // 128) * 256], BF16)
    elif L == 1:
        alloc_l1a_alias(k)
    elif L == 2:
        alloc_l2a_alias(k)
    elif L == 3:
        alloc_l3a_bufs(k)
        alloc_l3f_bufs(k)
    alloc_scratch_dram(k)
    x_ap = k.dram_in("x_in", [1024, T], F32)
    xo_ap = k.dram_out("x_out", [1024, T], F32)
    k.start()
    emit_init_consts(k)
    emit_load_w(k, [n for n, _ in sw] + alias)
    k.barrier()
    if L == 0:
        s_o = emit_layer0(k, x_ap, xo_ap)
    elif L == 1:
        s_o = emit_layer1(k, x_ap, xo_ap)
    elif L == 2:
        s_o = emit_layer2(k, x_ap, xo_ap)
    else:
        s_o = emit_layer3(k, x_ap, xo_ap)
    finish_outputs(k, [s_o])
    k.stk.close()
    return k


def kernel(**inputs):
    T = 4096
    x = np.ascontiguousarray(np.asarray(inputs["L0_attn_query"], np.float32))
    B, S, E = x.shape
    xc = x.reshape(NCORE, 2 * S, E)
    shards = [np.ascontiguousarray(xc[c].T) for c in range(NCORE)]
    k = build_fused(inputs, T)
    maps = []
    for c in range(NCORE):
        m = dict(k.host)
        m["x_in"] = shards[c]
        maps.append(m)
    res = run_bass_kernel_spmd(k.nc, maps, core_ids=list(range(NCORE)))
    shards = [np.ascontiguousarray(
        res.results[c]["x_out"].reshape(E, 2 * S)) for c in range(NCORE)]
    out = np.stack([s.T for s in shards]).reshape(B, S, E)
    return np.ascontiguousarray(out, np.float32)


def build_fused(inputs, T=4096):
    k = K(T)
    alloc_common(k)
    host_consts(k)
    l0_attn_host(k, inputs)
    l0_ffn_host(k, inputs)
    l1_attn_host(k, inputs)
    l1_ffn_host(k, inputs)
    l2_attn_host(k, inputs)
    l2_ffn_host(k, inputs)
    l3_attn_host(k, inputs)
    l3_ffn_host(k, inputs)
    sw = _dedupe(L0A_SMALL_W + L0F_SMALL_W + L1A_SMALL_W + L1F_SMALL_W
                 + L2A_SMALL_W + L2F_SMALL_W + L3_SMALL_W + COMMON_SMALL_W)
    alloc_w_sb(k, sw)
    alloc_l0a_alias(k)
    alloc_l0f_alias(k)
    alloc_l1a_alias(k)
    alloc_l2a_alias(k)
    alloc_l3a_bufs(k)
    alloc_l3f_bufs(k)
    k.dram_tmp("qkv_s", [128, 24 * T], BF16)
    k.dram_tmp("amps_s", [128, (T // 128) * 256], BF16)
    alloc_scratch_dram(k)
    k.dram_tmp("x_a", [1024, T], F32)
    k.dram_tmp("x_b", [1024, T], F32)
    x_ap = k.dram_in("x_in", [1024, T], F32)
    xo_ap = k.dram_out("x_out", [1024, T], F32)
    k.start()
    emit_init_consts(k)
    emit_load_w(k, [n for n, _ in sw] + L0A_ALIAS_W + ["L0f_sel8r"])
    k.barrier()
    emit_layer0(k, x_ap, k.dram["x_a"])
    k.barrier()
    emit_layer1(k, k.dram["x_a"], k.dram["x_b"])
    k.barrier()
    emit_load_w(k, ["L2_wBT", "L2_wCT"])
    k.barrier()
    s3 = emit_layer2(k, k.dram["x_b"], k.dram["x_a"])
    k.barrier()
    s_o = emit_layer3(k, k.dram["x_a"], xo_ap)
    finish_outputs(k, [s_o])
    k.stk.close()
    return k
